# version 12
# baseline (speedup 1.0000x reference)
import numpy as np
import ml_dtypes

import concourse.bass as bass
import concourse.mybir as mybir
from concourse.bass_utils import run_bass_kernel_spmd

F32 = mybir.dt.float32
BF16 = mybir.dt.bfloat16
I32 = mybir.dt.int32
AF = mybir.ActivationFunctionType
ALU = mybir.AluOpType
AX = mybir.AxisListType

AND_M = int(np.uint32(0xFFC00000).view(np.int32))
OR_M = 0x00200000
EXP_M = 0x7F800000
NEG_BIG = -1.0e9

D = 1024
DC = 8
NPAIR = 8
QCW = 512
EPS = 1e-5


class Cfg:
    def __init__(self, S=2048, I=2816):
        self.S = S
        self.NSEQ = 2
        self.I = I
        self.IC = I // 128
        self.NQC = S // QCW
        self.NT = S // 128
        self.NBL = []
        off = 0
        while off < I:
            w = min(512, I - off)
            self.NBL.append((off, w))
            off += w


class Sem:
    def __init__(self, handle):
        self.h = handle
        self.v = 0


class Eng:
    def __init__(self, prog, name):
        self.prog = prog
        self.name = name

    @property
    def real(self):
        return self.prog.active_eng if self.prog.active == self.name else None

    def _fin(self, ins, inc):
        if inc is not None:
            sem, n = (inc, 1) if isinstance(inc, Sem) else inc
            sem.v += n
            if ins is not None:
                ins.then_inc(sem.h, n)
        return ins

    def wait(self, sem, val=None):
        if val is None:
            val = sem.v
        if val <= 0:
            return
        e = self.real
        if e is not None:
            e.wait_ge(sem.h, val)

    def op(self, fn, inc=None):
        e = self.real
        ins = fn(e) if e is not None else None
        return self._fin(ins, inc)

    def dma(self, out, in_, inc=None):
        return self.op(lambda e: e.dma_start(out=out, in_=in_), inc)


class Prog:
    def __init__(self, nc):
        self.nc = nc
        self.active = None
        self.active_eng = None
        self.S = Eng(self, "sync")
        self.G = Eng(self, "gpsimd")
        self.T = Eng(self, "tensor")
        self.V = Eng(self, "vector")
        self.C = Eng(self, "scalar")

    def begin_pass(self, name, eng, sems):
        self.active = name
        self.active_eng = eng
        for s in sems:
            s.v = 0


def build_program(cfg: Cfg, debug_stop=None, debug_taps=False):
    nc = bass.Bass("TRN2", target_bir_lowering=False, detect_race_conditions=False)
    S_, I_, IC, NQC, NT, NSEQ = cfg.S, cfg.I, cfg.IC, cfg.NQC, cfg.NT, cfg.NSEQ

    dt = nc.dram_tensor
    x_h = dt("x", [NSEQ * S_, D], F32, kind="ExternalInput").ap()
    wq_h = dt("wq", [D, D], F32, kind="ExternalInput").ap()
    wk_h = dt("wk", [D, D], F32, kind="ExternalInput").ap()
    wv_h = dt("wv", [D, D], F32, kind="ExternalInput").ap()
    wo_h = dt("wo", [D, D], F32, kind="ExternalInput").ap()
    wg_h = dt("wg", [I_, D], F32, kind="ExternalInput").ap()
    wu_h = dt("wu", [I_, D], F32, kind="ExternalInput").ap()
    wd_h = dt("wd", [D, I_], F32, kind="ExternalInput").ap()
    ct_h = dt("ct", [128, DC, NSEQ], F32, kind="ExternalInput").ap()
    wmod_h = dt("wmod", [D, 6144], F32, kind="ExternalInput").ap()
    ident_h = dt("ident", [128, 128], BF16, kind="ExternalInput").ap()
    tri_h = dt("tri", [128, 128], BF16, kind="ExternalInput").ap()
    indik_h = dt("indik", [2, 128], F32, kind="ExternalInput").ap()
    ones1_h = dt("ones1", [1, 128], F32, kind="ExternalInput").ap()
    y_h = dt("y", [NSEQ * S_, D], F32, kind="ExternalOutput").ap()
    dbg_h = dt("dbg", [128, 16, 1024], F32, kind="ExternalOutput").ap() if debug_taps else None
    wgq_h = dt("wgq_s", [128, DC, I_], BF16, kind="Internal").ap()
    wuq_h = dt("wuq_s", [128, DC, I_], BF16, kind="Internal").ap()
    wdq_h = dt("wdq_s", [128, IC, D], BF16, kind="Internal").ap()
    mod_h = dt("mod_s", [NSEQ, 6144], F32, kind="Internal").ap()

    SBUF_LIMIT = nc.SBUF_PARTITION_SIZE_BYTES
    cur = [17 * 1024]

    def at(name, shape, dtp, align=32):
        nbytes = int(np.prod(shape[1:])) * mybir.dt.size(dtp)
        cur[0] = (cur[0] + align - 1) // align * align
        t = nc.alloc_sbuf_tensor_at(name, shape, dtp, offset=cur[0])
        cur[0] += nbytes
        return t

    ident = at("ident", [128, 128], BF16)
    tri = at("tri", [128, 128], BF16)
    indik = at("indik", [2, 128], F32)
    ones1 = at("ones1", [1, 128], F32)
    ct = at("ct", [128, DC, NSEQ], F32)
    vrow = at("vrow", [NSEQ, 1024], F32)
    bc0 = at("bc0", [128, D], F32)
    bc1 = at("bc1", [128, D], F32)
    bc2 = at("bc2", [128, D], F32)
    ln_s = at("ln_s", [128, 10], F32)
    onescol = at("onescol", [128, 1], BF16)
    ones_sb = at("ones_sb", [128, 64], F32)
    act_scr = at("act_scr", [128, D], F32)
    xin = at("xin", [128, 2, D], F32)
    xn_t = at("xn_t", [128, D], BF16)
    x1_off = (cur[0] + 31) // 32 * 32
    x1 = at("x1", [128, NT, D], BF16)
    shared_end = cur[0]

    ov0 = (shared_end + 31) // 32 * 32

    cur[0] = ov0
    stW = at("stW", [128, 2, 3072], F32)
    assert cur[0] <= SBUF_LIMIT, f"phase-W overlay over budget: {cur[0]}"

    cur[0] = ov0
    h0 = at("h0", [128, DC, S_], BF16)
    qf = nc.alloc_sbuf_tensor_at("qf", [128, DC, S_], BF16, offset=ov0)
    wX = at("wX", [128, DC, D], BF16)
    kf = at("kf", [128, DC, S_], BF16)
    vt = at("vt", [128, NT, 16 * 64], BF16)
    attw_off = (cur[0] + 31) // 32 * 32
    attw = at("attw", [128, DC, QCW], BF16)
    pa = at("pa", [128, QCW], BF16)
    pb = at("pb", [128, QCW], BF16)
    densb = at("densb", [128, QCW], F32)
    recb = at("recb", [128, QCW], F32)
    ot1 = recb
    xn_u = nc.alloc_sbuf_tensor_at("xn_u", [128, D], BF16, offset=attw_off)
    xnn = nc.alloc_sbuf_tensor_at("xnn", [128, D], F32, offset=attw_off + 2048)
    att_end = cur[0]
    assert cur[0] <= SBUF_LIMIT, f"phase-ATT overlay over budget: {cur[0]}"

    cur[0] = ov0
    qU = at("qU", [128, 1024], BF16)
    qX = at("qX", [128, 1024], BF16)
    qM1 = at("qM1", [128, 1024], BF16)
    qSG = at("qSG", [128, 1024], BF16)
    qsm = at("qsm", [128, 3, 64], F32)
    qsb = at("qsb", [128, 64], BF16)
    xn2 = at("xn2", [128, D], F32)
    h1g = at("h1g", [128, D], F32)
    h1f = at("h1f", [128, D], F32)
    h1qt = at("h1qt", [128, D], BF16)
    h1qf = at("h1qf", [128, DC, QCW], BF16)
    wgb = at("wgb", [128, 2, DC, 512], BF16)
    wub = at("wub", [128, 2, DC, 512], BF16)
    wdb = at("wdb", [128, 2, 512], BF16)
    silu_s = at("silu_s", [128, 512], F32)
    h2f = at("h2f", [128, 512], F32)
    h2qt = at("h2qt", [128, 2, 512], BF16)
    h2qi = at("h2qi", [128, IC, QCW], BF16)
    o1f = at("o1f", [128, 512], F32)
    x3s = at("x3s", [128, 4, D], F32)
    ffn_end = cur[0]
    assert cur[0] <= SBUF_LIMIT, f"phase-FFN overlay over budget: {cur[0]}"

    cur[0] = max(att_end, ffn_end)
    wreg0 = (cur[0] + 31) // 32 * 32
    stW2 = at("stW2", [128, 2, 1024], F32)
    h1qf2 = nc.alloc_sbuf_tensor_at("h1qf2", [128, DC, QCW], BF16, offset=wreg0)
    qbf2 = at("qbf2", [128, 2, 1024], BF16)
    tstage2 = at("tstage2", [128, 1024], BF16)
    qsm_w = at("qsm_w", [128, 3, 64], F32)
    qsb_w = at("qsb_w", [128, 64], BF16)
    assert cur[0] <= SBUF_LIMIT, f"W-quant region over budget: {cur[0]}"
    qU_w = nc.alloc_sbuf_tensor_at("qU_w", [128, 1024], BF16, offset=x1_off + 24576)
    qX_w = nc.alloc_sbuf_tensor_at("qX_w", [128, 1024], BF16, offset=x1_off + 26624)
    qM1_w = nc.alloc_sbuf_tensor_at("qM1_w", [128, 1024], BF16, offset=x1_off + 28672)
    qSG_w = nc.alloc_sbuf_tensor_at("qSG_w", [128, 1024], BF16, offset=x1_off + 30720)

    P = [nc.alloc_psum_tensor(f"P{k}", [128, 512], F32) for k in range(7)]
    T1 = nc.alloc_psum_tensor("T1", [128, 512], BF16)
    Ttile = [T1[:, 0:128], T1[:, 128:256]]

    semctx = [nc.semaphore(nm) for nm in ["LD", "ST", "TE", "VE", "SE", "GE"]]
    handles = [c.__enter__() for c in semctx]
    LD, ST, TE, VE, SE, GE = [Sem(h) for h in handles]
    allsems = [LD, ST, TE, VE, SE, GE]
    pr = Prog(nc)

    (ln_sx, ln_m2, ln_nm1, ln_nvar, ln_is, ln_std, eps_b, exp_b, zcol,
     jcol) = (ln_s[:, k:k + 1] for k in range(10))

    def script(pr: Prog):
        S, G, T, V, C = pr.S, pr.G, pr.T, pr.V, pr.C
        dbg_slot = [0]

        def tap(ap, cols=1024):
            if not debug_taps:
                return
            k = dbg_slot[0]
            dbg_slot[0] += 1
            G.wait(VE, VE.v)
            G.wait(SE, SE.v)
            G.wait(TE, TE.v)
            G.wait(GE, GE.v)
            G.dma(dbg_h[:, k, 0:cols], ap, inc=(ST, 16))

        def v3(ap2, w):
            return ap2.rearrange("p (nb b) -> p nb b", b=16)

        def quant(src, dst, w, Q, te_guard=None):
            (U, A_, B_, SG, SM, SB) = Q
            nb = w // 16
            C.wait(GE, GE.v)
            C.wait(VE, VE.v)
            C.op(lambda e: e.activation(SG[:, 0:w], src, AF.Sign), inc=SE)
            se_sgn = SE.v
            m_, s_, r_ = (SM[:, k, 0:nb] for k in range(3))
            sb_ = SB[:, 0:nb]
            V.wait(GE, GE.v)
            hv = nb // 2
            halves = [(0, hv), (hv, nb)] if (2 <= hv < 128) else [(0, nb)]
            srcv = v3(src, w)

            def forall(fn):
                for (a, b) in halves:
                    V.op(lambda e, a=a, b=b: fn(e, slice(a, b)))

            forall(lambda e, sl: e.tensor_reduce(
                m_[:, sl], srcv[:, sl, :], AX.X, ALU.max,
                apply_absolute_value=True))
            forall(lambda e, sl: e.tensor_scalar(s_[:, sl], m_[:, sl], 1.0 / 6.0,
                                                 1e-12, ALU.mult, ALU.max))
            forall(lambda e, sl: e.reciprocal(r_[:, sl], s_[:, sl]))
            forall(lambda e, sl: e.tensor_copy(sb_[:, sl], s_[:, sl]))
            rb = r_.unsqueeze(2).broadcast_to([128, nb, 16])
            sbb = sb_.unsqueeze(2).broadcast_to([128, nb, 16])
            u = U[:, 0:w]
            u3 = v3(u, w)
            V.op(lambda e: e.tensor_tensor(u3, srcv, rb, ALU.mult))
            V.wait(SE, se_sgn)
            V.op(lambda e: e.tensor_tensor(u, u, SG[:, 0:w], ALU.mult))
            Aw, Bw = A_[:, 0:w], B_[:, 0:w]
            V.op(lambda e: e.tensor_scalar(Aw, u, 0.25, None, ALU.is_gt))
            V.op(lambda e: e.scalar_tensor_tensor(Aw, u, 0.75, Aw, ALU.is_gt, ALU.add))
            V.op(lambda e: e.scalar_tensor_tensor(Aw, u, 1.25, Aw, ALU.is_gt, ALU.add))
            V.op(lambda e: e.scalar_tensor_tensor(Aw, u, 1.75, Aw, ALU.is_gt, ALU.add))
            V.op(lambda e: e.tensor_scalar(Bw, u, 5.0, 2.0, ALU.is_gt, ALU.mult))
            V.op(lambda e: e.scalar_tensor_tensor(Bw, u, 2.5, Bw, ALU.is_gt, ALU.add))
            V.op(lambda e: e.scalar_tensor_tensor(Bw, u, 3.5, Bw, ALU.is_gt, ALU.add))
            V.op(lambda e: e.scalar_tensor_tensor(Aw, Aw, 0.5, Bw, ALU.mult, ALU.add))
            V.op(lambda e: e.tensor_tensor(Aw, Aw, SG[:, 0:w], ALU.mult))
            if te_guard is not None:
                V.wait(TE, te_guard)
            V.op(lambda e: e.tensor_tensor(v3(dst, w), v3(Aw, w), sbb, ALU.mult),
                 inc=VE)

        QW = (qU_w, qX_w, qM1_w, qSG_w, qsm_w, qsb_w)
        QF = (qU, qX, qM1, qSG, qsm, qsb)

        def transpose_blocks(src_fn, nblk, dst_fn):
            for b in range(nblk):
                par = b % 2
                T.wait(SE, SE.v)
                T.op(lambda e, b=b, par=par: e.transpose(Ttile[par], src_fn(b), ident[:]),
                     inc=TE)
                C.wait(TE, TE.v)
                C.op(lambda e, b=b, par=par: e.copy(dst_fn(b), Ttile[par]), inc=SE)

        def ln_norm(x_ap, out_ap):
            C.op(lambda e: e.activation(act_scr[:], x_ap, AF.Copy, accum_out=ln_sx))
            C.op(lambda e: e.activation(act_scr[:], x_ap, AF.Square,
                                        scale=1.0 / 32.0, bias=zcol,
                                        accum_out=ln_m2))
            C.op(lambda e: e.activation(ln_nm1, ln_sx, AF.Identity,
                                        scale=-1.0 / D, bias=zcol), inc=SE)
            V.wait(SE, SE.v)
            V.op(lambda e: e.scalar_tensor_tensor(ln_nvar, ln_nm1, ln_nm1, ln_m2,
                                                  ALU.mult, ALU.subtract), inc=VE)
            C.wait(VE, VE.v)
            C.op(lambda e: e.activation(ln_std, ln_nvar, AF.Sqrt, scale=-1.0,
                                        bias=eps_b), inc=SE)
            V.wait(SE, SE.v)
            V.op(lambda e: e.reciprocal(ln_is, ln_std))
            V.op(lambda e: e.memset(jcol, 0.0))
            V.op(lambda e: e.tensor_scalar(out_ap, x_ap, ln_nm1, ln_is,
                                           ALU.add, ALU.mult), inc=VE)
            return VE.v

        for dst, src in [(ident[:], ident_h), (tri[:], tri_h), (indik[:], indik_h),
                         (ones1[:], ones1_h), (ct[:], ct_h)]:
            S.dma(dst, src, inc=(LD, 16))
        V.op(lambda e: e.memset(eps_b, EPS))
        V.op(lambda e: e.memset(zcol, 0.0))
        V.op(lambda e: e.memset(jcol, 0.0))
        V.op(lambda e: e.memset(onescol[:], 1.0))
        V.op(lambda e: e.memset(ones_sb[:], 1.0))
        V.op(lambda e: e.memset(exp_b, -6.0), inc=VE)
        C.wait(VE, VE.v)
        C.wait(LD, LD.v)
        ctf = ct[:].rearrange("p c n -> p (c n)")
        C.op(lambda e: e.activation(ctf, ctf, AF.Silu), inc=SE)

        for c in range(DC):
            if c >= 1:
                S.wait(TE, TE.v)
            S.dma(stW[:, 0, :], wmod_h[128 * c:128 * (c + 1), 0:3072], inc=(LD, 16))
            S.dma(stW[:, 1, :], wmod_h[128 * c:128 * (c + 1), 3072:6144], inc=(LD, 16))
            T.wait(LD, LD.v)
            T.wait(SE, SE.v)
            for h in range(2):
                for n in range(6):
                    col = 3072 * h + 512 * n
                    tp = (0, 0) if h == 0 else (0, 32)
                    rows = slice(0, NSEQ) if h == 0 else slice(32, 32 + NSEQ)
                    last = (h == 1 and n == 5)
                    T.op(lambda e, col=col, tp=tp, rows=rows, c=c: e.matmul(
                        P[n][rows, :], ct[:, c, :],
                        stW[:, col // 3072, (col % 3072):(col % 3072) + 512],
                        start=(c == 0), stop=(c == DC - 1), tile_position=tp),
                        inc=(TE if last else None))
        V.wait(TE, TE.v)
        for k in range(6):
            h = k // 3
            n0 = (2 * k) % 6
            rows = slice(0, NSEQ) if h == 0 else slice(32, 32 + NSEQ)
            r0 = 0 if h == 0 else 32
            buf = act_scr[r0:r0 + NSEQ, :]
            tbf = xin[r0:r0 + NSEQ, 0, 0:1024]
            if k >= 1:
                V.wait(ST, ST.v)
            V.op(lambda e, n0=n0, rows=rows, buf=buf: e.tensor_copy(
                buf[:, 0:512], P[n0][rows, :]))
            V.op(lambda e, n0=n0, rows=rows, buf=buf: e.tensor_copy(
                buf[:, 512:1024], P[n0 + 1][rows, :]))
            if k in (1, 4):
                V.op(lambda e, buf=buf: e.tensor_scalar(buf, buf, 1.0, None, ALU.add))
            if k in (2, 5):
                V.op(lambda e, buf=buf, tbf=tbf: e.tensor_scalar(
                    tbf, buf, 0.0, 2.0, ALU.is_ge, ALU.mult))
                V.op(lambda e, tbf=tbf: e.tensor_scalar(tbf, tbf, -1.0, None, ALU.add))
                V.op(lambda e, buf=buf, tbf=tbf: e.tensor_tensor(buf, buf, tbf,
                                                                 ALU.mult))
                V.op(lambda e, buf=buf: e.tensor_scalar(buf, buf, 0.25, None, ALU.max))
                V.op(lambda e, buf=buf, tbf=tbf: e.tensor_tensor(buf, buf, tbf, ALU.mult))
            V.op(lambda e, buf=buf: e.tensor_scalar(buf, buf, 0.0, None, ALU.add),
                 inc=VE)
            G.wait(VE, VE.v)
            G.dma(mod_h[:, 1024 * k:1024 * (k + 1)], buf, inc=(ST, 16))

        if debug_stop == "mod":
            G.wait(ST, ST.v)
            return
        wchunks = []
        for icx in range(cfg.IC):
            wchunks.append((wg_h, 128 * icx, 0, D, wgq_h))
        for icx in range(cfg.IC):
            wchunks.append((wu_h, 128 * icx, 0, D, wuq_h))
        for hc in range(DC):
            for coff, cwid in ((0, 1024), (1024, 1024), (2048, 768)):
                wchunks.append((wd_h, 128 * hc, coff, cwid, wdq_h))
        wstate = {"li": 0, "ti": 0, "te_qbf": [0, 0]}

        def w_load_quant():
            i = wstate["li"]
            if i >= len(wchunks):
                return
            wstate["li"] += 1
            src_h, r0, c0, cw, dst_h = wchunks[i]
            par = i % 2
            S.wait(VE, VE.v)
            S.wait(SE, SE.v)
            S.dma(stW2[:, par, 0:cw], src_h[r0:r0 + 128, c0:c0 + cw],
                  inc=(LD, 16))
            V.wait(LD, LD.v)
            C.wait(LD, LD.v)
            quant(stW2[:, par, 0:cw], qbf2[:, par, 0:cw], cw, QW,
                  te_guard=wstate["te_qbf"][par])

        def w_transpose_store():
            j = wstate["ti"]
            if j >= wstate["li"]:
                return
            wstate["ti"] += 1
            src_h, r0, c0, cw, dst_h = wchunks[j]
            par = j % 2
            nblk = cw // 128
            T.wait(VE, VE.v)
            T.wait(ST, ST.v)
            transpose_blocks(
                lambda b, par=par: qbf2[:, par, 128 * b:128 * (b + 1)], nblk,
                lambda b: tstage2[:, 128 * b:128 * (b + 1)])
            wstate["te_qbf"][par] = TE.v
            G.wait(SE, SE.v)
            if dst_h is wdq_h:
                dst_ap = dst_h[:, (c0 // 128):(c0 // 128) + nblk, r0:r0 + 128]
            else:
                dst_ap = dst_h[:, :, r0:r0 + 128]
            G.dma(dst_ap,
                  tstage2[:, 0:cw].rearrange("p (c w) -> p c w", w=128),
                  inc=(ST, 16))

        def wq_point(s):
            if s != 0:
                return
            w_transpose_store()
            w_load_quant()

        def wq_drain(s):
            if s != 0:
                return
            while wstate["ti"] < len(wchunks):
                w_transpose_store()
                w_load_quant()

        for s in range(NSEQ):
            xrow0 = s * S_

            def expand(k, dstb):
                S.wait(TE, TE.v)
                S.wait(ST, ST.v)
                S.dma(vrow[0:1, :], mod_h[s:s + 1, 1024 * k:1024 * (k + 1)],
                      inc=(LD, 16))
                T.wait(LD, LD.v)
                for half in range(2):
                    T.wait(VE, VE.v)
                    T.op(lambda e, half=half: e.matmul(
                        P[6][:, :], ones1[:], vrow[0:1, 512 * half:512 * half + 512],
                        start=True, stop=True), inc=TE)
                    V.wait(TE, TE.v)
                    V.op(lambda e, half=half: e.tensor_copy(
                        dstb[:, 512 * half:512 * half + 512], P[6][:, :]), inc=VE)

            expand(0, bc0)
            expand(1, bc1)
            expand(2, bc2)
            if s == 0 and debug_taps:
                tap(bc0[:]); tap(bc1[:]); tap(bc2[:])

            for t in range(NT):
                par = t % 2
                S.wait(SE, SE.v)
                S.wait(VE, VE.v)
                S.dma(xin[:, par, :], x_h[xrow0 + 128 * t: xrow0 + 128 * (t + 1), :],
                      inc=(LD, 16))
                C.wait(LD, LD.v)
                V.wait(GE, GE.v)
                ln_norm(xin[:, par, :], xnn[:])
                G.wait(VE, VE.v)
                G.op(lambda e: e.tensor_tensor(xn_u[:], xnn[:], bc1[:], ALU.mult))
                G.wait(TE, TE.v)
                G.op(lambda e: e.tensor_tensor(xn_t[:], xn_u[:], bc0[:], ALU.add),
                     inc=GE)
                T.wait(GE, GE.v)
                if s == 0 and t == 0 and debug_taps:
                    tap(xn_t[:])
                transpose_blocks(
                    lambda b: xn_t[:, 128 * b:128 * (b + 1)], DC,
                    lambda b, t=t: h0[:, b, 128 * t:128 * (t + 1)])
                wq_point(s)

            def load_w(w_h, dst):
                G.wait(TE, TE.v)
                G.dma(dst[:], w_h.rearrange("(c p) w -> p c w", p=128),
                      inc=(ST, 16))

            def proj_block(wbuf, nq, dst_col_fn):
                T.wait(SE, SE.v)
                for wave in ([0, 1, 2, 3, 4, 5, 6], [7]):
                    for mc in wave:
                        bank = P[mc % 7]
                        if mc == 7:
                            C.wait(TE, TE.v)
                            for mc2 in [0, 1, 2, 3, 4, 5, 6]:
                                C.op(lambda e, mc2=mc2, nq=nq: e.copy(
                                    dst_col_fn(mc2, nq), P[mc2][:, :]), inc=SE)
                            T.wait(SE, SE.v)
                        for kc in range(DC):
                            T.op(lambda e, kc=kc, mc=mc, nq=nq, bank=bank: e.matmul(
                                bank[:, :], wbuf[:, kc, 128 * mc:128 * (mc + 1)],
                                h0[:, kc, 512 * nq:512 * (nq + 1)],
                                start=(kc == 0), stop=(kc == DC - 1)),
                                inc=(TE if kc == DC - 1 else None))
                C.wait(TE, TE.v)
                C.op(lambda e, nq=nq: e.copy(dst_col_fn(7, nq), P[0][:, :]), inc=SE)

            def proj_block_q(wbuf, nq):
                T.wait(SE, SE.v)
                T.wait(VE, VE.v)
                for mc in range(DC):
                    bank = P[mc % 7]
                    if mc == 7:
                        C.wait(TE, TE.v)
                        for mc2 in range(7):
                            C.op(lambda e, mc2=mc2: e.copy(
                                attw[:, mc2, :], P[mc2][:, :]), inc=SE)
                        T.wait(SE, SE.v)
                    for kc in range(DC):
                        T.op(lambda e, kc=kc, mc=mc, nq=nq, bank=bank: e.matmul(
                            bank[:, :], wbuf[:, kc, 128 * mc:128 * (mc + 1)],
                            h0[:, kc, 512 * nq:512 * (nq + 1)],
                            start=(kc == 0), stop=(kc == DC - 1)),
                            inc=(TE if kc == DC - 1 else None))
                C.wait(TE, TE.v)
                C.op(lambda e: e.copy(attw[:, 7, :], P[0][:, :]), inc=SE)
                V.wait(SE, SE.v)
                V.op(lambda e, nq=nq: e.tensor_copy(
                    qf[:, :, 512 * nq:512 * (nq + 1)], attw[:]), inc=VE)

            if s == 0 and debug_taps:
                tap(h0[:, 0, 0:1024])
            load_w(wk_h, wX)
            if s == 0 and debug_taps:
                tap(wX[:, 0, :])
            T.wait(ST, ST.v)
            for nq in range(S_ // 512):
                proj_block(wX, nq, lambda mc, nq: kf[:, mc, 512 * nq:512 * (nq + 1)])
                wq_point(s)
                wq_point(s)

            if s == 0 and debug_taps:
                tap(kf[:, 0, 0:1024])
            load_w(wv_h, wX)
            T.wait(ST, ST.v)
            for t in range(NT):
                for nd in range(2):
                    bank = P[(2 * t + nd) % 7]
                    if t * 2 + nd >= 7:
                        T.wait(SE, SE.v)
                    for kc in range(DC):
                        T.op(lambda e, kc=kc, t=t, nd=nd, bank=bank: e.matmul(
                            bank[:, :], h0[:, kc, 128 * t:128 * (t + 1)],
                            wX[:, kc, 512 * nd:512 * (nd + 1)],
                            start=(kc == 0), stop=(kc == DC - 1)),
                            inc=(TE if kc == DC - 1 else None))
                    C.wait(TE, TE.v)
                    C.op(lambda e, t=t, nd=nd, bank=bank: e.copy(
                        vt[:, t, 512 * nd:512 * (nd + 1)], bank[:, :]), inc=SE)
                if t % 2 == 1:
                    wq_point(s)
            T.wait(SE, SE.v)

            if s == 0 and debug_taps:
                tap(vt[:, 0, :])
            load_w(wq_h, wX)
            T.wait(ST, ST.v)
            for nq in range(S_ // 512):
                proj_block_q(wX, nq)
                wq_point(s)
                wq_point(s)

            if s == 0 and debug_taps:
                tap(qf[:, 0, 0:1024])
            load_w(wo_h, wX)
            T.wait(ST, ST.v)

            for qc in range(NQC):
                njc = 4 * qc + 4
                for p in range(NPAIR):
                    for j in range(njc):
                        vs = max(0, 128 * j - QCW * qc)
                        diag = (128 * j >= QCW * qc)
                        sA = P[0 + (j % 2)]
                        sB = P[2 + (j % 2)]
                        T.wait(SE, SE.v)
                        q_ap = qf[0:64, p, QCW * qc + vs: QCW * (qc + 1)]
                        qb_ap = qf[64:128, p, QCW * qc + vs: QCW * (qc + 1)]
                        T.op(lambda e, sA=sA, j=j, vs=vs, p=p, q_ap=q_ap, diag=diag:
                             e.matmul(sA[:, vs:QCW], kf[0:64, p, 128 * j:128 * (j + 1)],
                                      q_ap, start=True, stop=not diag,
                                      tile_position=(0, 0)))
                        if diag:
                            T.op(lambda e, sA=sA, vs=vs: e.matmul(
                                sA[:, vs:vs + 128], ident[:], tri[:],
                                start=False, stop=True))
                        T.op(lambda e, sB=sB, j=j, vs=vs, p=p, qb_ap=qb_ap, diag=diag:
                             e.matmul(sB[:, vs:QCW], kf[64:128, p, 128 * j:128 * (j + 1)],
                                      qb_ap, start=True, stop=not diag,
                                      tile_position=(64, 0)),
                             inc=(None if diag else TE))
                        if diag:
                            T.op(lambda e, sB=sB, vs=vs: e.matmul(
                                sB[:, vs:vs + 128], ident[:], tri[:],
                                start=False, stop=True), inc=TE)
                        C.wait(TE, TE.v)
                        C.op(lambda e, sA=sA, vs=vs: e.activation(
                            pa[:, vs:QCW], sA[:, vs:QCW], AF.Exp, scale=0.125,
                            bias=exp_b), inc=SE)
                        C.op(lambda e, sB=sB, vs=vs: e.activation(
                            pb[:, vs:QCW], sB[:, vs:QCW], AF.Exp, scale=0.125,
                            bias=exp_b), inc=SE)
                        T.wait(SE, SE.v)
                        T.op(lambda e, j=j, vs=vs, p=p: e.matmul(
                            P[4][0:64, vs:QCW], vt[:, j, 128 * p:128 * p + 64],
                            pa[:, vs:QCW], start=(j == 0), stop=(j == njc - 1)))
                        T.op(lambda e, j=j, vs=vs, p=p: e.matmul(
                            P[5][64:128, vs:QCW], vt[:, j, 128 * p + 64:128 * (p + 1)],
                            pb[:, vs:QCW], start=(j == 0), stop=(j == njc - 1),
                            tile_position=(0, 64)))
                        T.op(lambda e, j=j, vs=vs: e.matmul(
                            P[6][0:1, vs:QCW], onescol[:], pa[:, vs:QCW],
                            start=(j == 0), stop=(j == njc - 1)))
                        T.op(lambda e, j=j, vs=vs: e.matmul(
                            P[6][32:33, vs:QCW], onescol[:], pb[:, vs:QCW],
                            start=(j == 0), stop=(j == njc - 1),
                            tile_position=(0, 32)), inc=TE)
                    C.wait(TE, TE.v)
                    C.op(lambda e: e.copy(densb[0:1, 0:QCW], P[6][0:1, :]))
                    C.op(lambda e: e.copy(densb[32:33, 0:QCW], P[6][32:33, :]),
                         inc=SE)
                    V.wait(SE, SE.v)
                    V.op(lambda e: e.reciprocal(densb[0:1, 0:QCW], densb[0:1, 0:QCW]))
                    V.op(lambda e: e.reciprocal(densb[32:33, 0:QCW],
                                                densb[32:33, 0:QCW]), inc=VE)
                    T.wait(VE, VE.v)
                    T.op(lambda e: e.matmul(P[6][0:64, :], ones1[0:1, 0:64],
                                            densb[0:1, 0:QCW],
                                            start=True, stop=True))
                    T.op(lambda e: e.matmul(P[6][64:128, :], ones_sb[32:33, :],
                                            densb[32:33, 0:QCW],
                                            start=True, stop=True,
                                            tile_position=(32, 64)), inc=TE)
                    V.wait(TE, TE.v)
                    V.op(lambda e: e.tensor_copy(recb[:], P[6][:, :]))
                    V.op(lambda e, p=p: e.tensor_tensor(
                        attw[0:64, p, :], P[4][0:64, :], recb[0:64, :], ALU.mult))
                    V.op(lambda e, p=p: e.tensor_tensor(
                        attw[64:128, p, :], P[5][64:128, :], recb[64:128, :],
                        ALU.mult), inc=VE)
                    T.wait(VE, VE.v)
                    wq_point(s)

                if qc == NQC - 1:
                    wq_drain(s)
                for qt in range(4):
                    tglob = 4 * qc + qt
                    par = tglob % 2
                    S.wait(VE, VE.v)
                    S.wait(SE, SE.v)
                    S.dma(xin[:, par, :],
                          x_h[xrow0 + 128 * tglob: xrow0 + 128 * (tglob + 1), :],
                          inc=(LD, 16))
                    for nd in range(2):
                        T.wait(VE, VE.v)
                        for c in range(DC):
                            T.op(lambda e, c=c, qt=qt, nd=nd: e.matmul(
                                P[6][:, :], attw[:, c, 128 * qt:128 * (qt + 1)],
                                wX[:, c, 512 * nd:512 * (nd + 1)],
                                start=(c == 0), stop=(c == DC - 1)),
                                inc=(TE if c == DC - 1 else None))
                        V.wait(TE, TE.v)
                        V.wait(LD, LD.v)
                        V.op(lambda e, nd=nd: e.tensor_tensor(
                            ot1[:, :], P[6][:, :], bc2[:, 512 * nd:512 * (nd + 1)],
                            ALU.mult))
                        V.op(lambda e, nd=nd, tglob=tglob, par=par: e.tensor_tensor(
                            x1[:, tglob, 512 * nd:512 * (nd + 1)], ot1[:, :],
                            xin[:, par, 512 * nd:512 * (nd + 1)], ALU.add), inc=VE)

            if s == 0 and debug_taps:
                tap(attw[:, 0, :], 512)
                tap(pa[:], 512)
                tap(recb[:], 512)
                tap(x1[:, 0, :])
                tap(x1[:, NT - 1, :])
            expand(3, bc0)
            expand(4, bc1)
            expand(5, bc2)

            h1_prev_ln = [0]

            def h1_tile(qc2, qt2, dstbuf):
                tglob = 4 * qc2 + qt2
                C.wait(VE, h1_prev_ln[0])
                V.wait(GE, GE.v)
                h1_prev_ln[0] = ln_norm(x1[:, tglob, :], xn2[:])
                G.wait(VE, VE.v)
                G.op(lambda e: e.tensor_tensor(h1g[:], xn2[:], bc1[:], ALU.mult))
                G.wait(SE, SE.v)
                G.op(lambda e: e.tensor_tensor(h1f[:], h1g[:], bc0[:], ALU.add),
                     inc=GE)
                quant(h1f[:], h1qt[:], D, QF)
                T.wait(VE, VE.v)
                transpose_blocks(
                    lambda b: h1qt[:, 128 * b:128 * (b + 1)], DC,
                    lambda b, qt2=qt2: dstbuf[:, b, 128 * qt2:128 * (qt2 + 1)])

            HOIST = False
            CONSV = False
            h1bufs = [h1qf, h1qf2]
            for qt in range(4):
                h1_tile(0, qt, h1bufs[0])

            gucnt = [0]
            ve_guard = [VE.v] * 7
            se_guard = [SE.v] * 7
            te_guard_w = [TE.v, TE.v]
            tp_prev = [None]

            def flush_tp():
                if tp_prev[0] is None:
                    return
                pnoff, pnw, pqt, ppar2 = tp_prev[0]
                tp_prev[0] = None
                T.wait(VE, VE.v)
                transpose_blocks(
                    lambda b, ppar2=ppar2: h2qt[:, ppar2, 128 * b:128 * (b + 1)],
                    pnw // 128,
                    lambda b, pnoff=pnoff, pqt=pqt: h2qi[:, (pnoff // 128) + b,
                                                         128 * pqt:128 * (pqt + 1)])

            for qc in range(NQC):
                hsrc = h1bufs[qc % 2]
                if not HOIST and qc > 0:
                    for qt in range(4):
                        h1_tile(qc, qt, hsrc)
                for nbi, (noff, nw) in enumerate(cfg.NBL):
                    par = nbi % 2
                    S.wait(TE, TE.v)
                    S.dma(wgb[:, par, :, 0:nw], wgq_h[:, :, noff:noff + nw],
                          inc=(LD, 16))
                    S.dma(wub[:, par, :, 0:nw], wuq_h[:, :, noff:noff + nw],
                          inc=(LD, 16))
                    for qt in range(4):
                        k = gucnt[0] % 2
                        gucnt[0] += 1
                        gP = P[k]
                        uP = P[2 + k]
                        T.wait(LD, LD.v)
                        T.wait(VE, VE.v)
                        T.wait(SE, SE.v)
                        for kc in range(DC):
                            T.op(lambda e, kc=kc, qt=qt, par=par, nw=nw, gP=gP,
                                 hsrc=hsrc: e.matmul(
                                gP[:, 0:nw], hsrc[:, kc, 128 * qt:128 * (qt + 1)],
                                wgb[:, par, kc, 0:nw],
                                start=(kc == 0), stop=(kc == DC - 1)))
                        for kc in range(DC):
                            T.op(lambda e, kc=kc, qt=qt, par=par, nw=nw, uP=uP,
                                 hsrc=hsrc: e.matmul(
                                uP[:, 0:nw], hsrc[:, kc, 128 * qt:128 * (qt + 1)],
                                wub[:, par, kc, 0:nw],
                                start=(kc == 0), stop=(kc == DC - 1)),
                                inc=(TE if kc == DC - 1 else None))
                        C.wait(TE, TE.v)
                        C.op(lambda e, gP=gP, nw=nw: e.activation(
                            silu_s[:, 0:nw], gP[:, 0:nw], AF.Silu), inc=SE)
                        se_guard[k] = SE.v
                        flush_tp()
                        V.wait(SE, SE.v)
                        V.op(lambda e, uP=uP, nw=nw: e.tensor_tensor(
                            h2f[:, 0:nw], silu_s[:, 0:nw], uP[:, 0:nw], ALU.mult),
                            inc=VE)
                        ve_guard[3 + k] = VE.v
                        par2 = gucnt[0] % 2
                        quant(h2f[:, 0:nw], h2qt[:, par2, 0:nw], nw, QF)
                        tp_prev[0] = (noff, nw, qt, par2)
                    te_guard_w[par] = TE.v
                    if HOIST and qc < NQC - 1 and nbi < 4:
                        h1_tile(qc + 1, nbi, h1bufs[(qc + 1) % 2])
                flush_tp()

                for nd in range(2):
                    for icx in range(IC):
                        par = icx % 2
                        S.wait(TE, TE.v)
                        S.dma(wdb[:, par, :], wdq_h[:, icx, 512 * nd:512 * (nd + 1)],
                              inc=(LD, 16))
                        T.wait(LD, LD.v)
                        if icx == 0:
                            T.wait(VE, VE.v)
                            T.wait(SE, SE.v)
                        for qt in range(4):
                            T.op(lambda e, icx=icx, qt=qt, par=par: e.matmul(
                                P[qt][:, :], h2qi[:, icx, 128 * qt:128 * (qt + 1)],
                                wdb[:, par, :],
                                start=(icx == 0), stop=(icx == IC - 1)),
                                inc=(TE if qt == 3 else None))
                    V.wait(TE, TE.v)
                    for qt in range(4):
                        tglob = 4 * qc + qt
                        V.wait(ST, ST.v)
                        V.op(lambda e, qt=qt, nd=nd: e.tensor_tensor(
                            o1f[:, :], P[qt][:, :], bc2[:, 512 * nd:512 * (nd + 1)],
                            ALU.mult))
                        V.op(lambda e, qt=qt, nd=nd, tglob=tglob: e.tensor_tensor(
                            x3s[:, qt, 512 * nd:512 * (nd + 1)], o1f[:, :],
                            x1[:, tglob, 512 * nd:512 * (nd + 1)], ALU.add),
                            inc=VE)
                        ve_guard[qt] = VE.v
                        if nd == 1:
                            G.wait(VE, VE.v)
                            G.dma(y_h[xrow0 + 128 * tglob: xrow0 + 128 * (tglob + 1), :],
                                  x3s[:, qt, :], inc=(ST, 16))

        G.wait(ST, ST.v)

    with nc.Block() as blk:
        @blk.sync
        def _(eng):
            pr.begin_pass("sync", eng, allsems)
            script(pr)

        @blk.gpsimd
        def _(eng):
            pr.begin_pass("gpsimd", eng, allsems)
            script(pr)

        @blk.tensor
        def _(eng):
            pr.begin_pass("tensor", eng, allsems)
            script(pr)

        @blk.vector
        def _(eng):
            pr.begin_pass("vector", eng, allsems)
            script(pr)

        @blk.scalar
        def _(eng):
            pr.begin_pass("scalar", eng, allsems)
            script(pr)

    for c in semctx:
        c.__exit__(None, None, None)
    return nc


def _consts():
    ident = np.eye(128, dtype=ml_dtypes.bfloat16)
    tri = np.zeros((128, 128), np.float32)
    r = np.arange(128)
    tri[r[:, None] > r[None, :]] = NEG_BIG
    tri = tri.astype(ml_dtypes.bfloat16)
    indik = np.zeros((2, 128), np.float32)
    indik[0, 0:64] = 1.0
    indik[1, 64:128] = 1.0
    ones1 = np.ones((1, 128), np.float32)
    return ident, tri, indik, ones1


def make_in_maps(cfg, x, wq, wk, wv, wo, wg, wu, wd, c, wmod, n_cores=8):
    ident, tri, indik, ones1 = _consts()
    shared = dict(
        wq=np.ascontiguousarray(wq, dtype=np.float32),
        wk=np.ascontiguousarray(wk, dtype=np.float32),
        wv=np.ascontiguousarray(wv, dtype=np.float32),
        wo=np.ascontiguousarray(wo, dtype=np.float32),
        wg=np.ascontiguousarray(wg, dtype=np.float32),
        wu=np.ascontiguousarray(wu, dtype=np.float32),
        wd=np.ascontiguousarray(wd, dtype=np.float32),
        wmod=np.ascontiguousarray(wmod, dtype=np.float32),
        ident=ident, tri=tri, indik=indik, ones1=ones1,
    )
    c = np.asarray(c, np.float32)
    x = np.asarray(x, np.float32)
    in_maps = []
    for i in range(n_cores):
        ct = np.ascontiguousarray(
            c[cfg.NSEQ * i:cfg.NSEQ * (i + 1)].T.reshape(DC, 128, cfg.NSEQ)
            .transpose(1, 0, 2))
        xi = np.ascontiguousarray(
            x[cfg.NSEQ * i:cfg.NSEQ * (i + 1)].reshape(cfg.NSEQ * cfg.S, D))
        in_maps.append(dict(shared, x=xi, ct=ct))
    return in_maps


def kernel(L0_attn_x, L0_attn_W_Q, L0_attn_W_K, L0_attn_W_V, L0_attn_W_o,
           L0_ffn_w_gate, L0_ffn_w_up, L0_ffn_w_down, L0_c, L0_wmod):
    cfg = Cfg(S=2048, I=2816)
    nc = build_program(cfg)
    in_maps = make_in_maps(cfg, L0_attn_x, L0_attn_W_Q, L0_attn_W_K, L0_attn_W_V,
                           L0_attn_W_o, L0_ffn_w_gate, L0_ffn_w_up, L0_ffn_w_down,
                           L0_c, L0_wmod)
    res = run_bass_kernel_spmd(nc, in_maps, core_ids=list(range(8)))
    out = np.empty((16, 2048, 1024), np.float32)
    for i in range(8):
        out[2 * i:2 * i + 2] = res.results[i]["y"].reshape(2, cfg.S, D)
    return out


# version 13
# speedup vs baseline: 1.1023x; 1.1023x over previous
import numpy as np
import ml_dtypes

import concourse.bass as bass
import concourse.mybir as mybir
from concourse.bass_utils import run_bass_kernel_spmd

F32 = mybir.dt.float32
BF16 = mybir.dt.bfloat16
I32 = mybir.dt.int32
AF = mybir.ActivationFunctionType
ALU = mybir.AluOpType
AX = mybir.AxisListType

AND_M = int(np.uint32(0xFFC00000).view(np.int32))
OR_M = 0x00200000
EXP_M = 0x7F800000
NEG_BIG = -1.0e9

D = 1024
DC = 8
NPAIR = 8
QCW = 512
EPS = 1e-5


class Cfg:
    def __init__(self, S=2048, I=2816):
        self.S = S
        self.NSEQ = 2
        self.I = I
        self.IC = I // 128
        self.NQC = S // QCW
        self.NT = S // 128
        self.NBL = []
        off = 0
        while off < I:
            w = min(512, I - off)
            self.NBL.append((off, w))
            off += w


class Sem:
    def __init__(self, handle):
        self.h = handle
        self.v = 0


class Eng:
    def __init__(self, prog, name):
        self.prog = prog
        self.name = name

    @property
    def real(self):
        return self.prog.active_eng if self.prog.active == self.name else None

    def _fin(self, ins, inc):
        if inc is not None:
            sem, n = (inc, 1) if isinstance(inc, Sem) else inc
            sem.v += n
            if ins is not None:
                ins.then_inc(sem.h, n)
        return ins

    def wait(self, sem, val=None):
        if val is None:
            val = sem.v
        if val <= 0:
            return
        e = self.real
        if e is not None:
            e.wait_ge(sem.h, val)

    def op(self, fn, inc=None):
        e = self.real
        ins = fn(e) if e is not None else None
        return self._fin(ins, inc)

    def dma(self, out, in_, inc=None):
        return self.op(lambda e: e.dma_start(out=out, in_=in_), inc)


class Prog:
    def __init__(self, nc):
        self.nc = nc
        self.active = None
        self.active_eng = None
        self.S = Eng(self, "sync")
        self.G = Eng(self, "gpsimd")
        self.T = Eng(self, "tensor")
        self.V = Eng(self, "vector")
        self.C = Eng(self, "scalar")

    def begin_pass(self, name, eng, sems):
        self.active = name
        self.active_eng = eng
        for s in sems:
            s.v = 0


def build_program(cfg: Cfg, debug_stop=None, debug_taps=False):
    nc = bass.Bass("TRN2", target_bir_lowering=False, detect_race_conditions=False)
    S_, I_, IC, NQC, NT, NSEQ = cfg.S, cfg.I, cfg.IC, cfg.NQC, cfg.NT, cfg.NSEQ

    dt = nc.dram_tensor
    x_h = dt("x", [NSEQ * S_, D], F32, kind="ExternalInput").ap()
    wq_h = dt("wq", [D, D], F32, kind="ExternalInput").ap()
    wk_h = dt("wk", [D, D], F32, kind="ExternalInput").ap()
    wv_h = dt("wv", [D, D], F32, kind="ExternalInput").ap()
    wo_h = dt("wo", [D, D], F32, kind="ExternalInput").ap()
    wg_h = dt("wg", [I_, D], F32, kind="ExternalInput").ap()
    wu_h = dt("wu", [I_, D], F32, kind="ExternalInput").ap()
    wd_h = dt("wd", [D, I_], F32, kind="ExternalInput").ap()
    ct_h = dt("ct", [128, DC, NSEQ], F32, kind="ExternalInput").ap()
    wmod_h = dt("wmod", [D, 6144], F32, kind="ExternalInput").ap()
    ident_h = dt("ident", [128, 128], BF16, kind="ExternalInput").ap()
    tri_h = dt("tri", [128, 128], BF16, kind="ExternalInput").ap()
    indik_h = dt("indik", [2, 128], F32, kind="ExternalInput").ap()
    ones1_h = dt("ones1", [1, 128], F32, kind="ExternalInput").ap()
    y_h = dt("y", [NSEQ * S_, D], F32, kind="ExternalOutput").ap()
    dbg_h = dt("dbg", [128, 16, 1024], F32, kind="ExternalOutput").ap() if debug_taps else None
    wgq_h = dt("wgq_s", [128, DC, I_], BF16, kind="Internal").ap()
    wuq_h = dt("wuq_s", [128, DC, I_], BF16, kind="Internal").ap()
    wdq_h = dt("wdq_s", [128, IC, D], BF16, kind="Internal").ap()
    mod_h = dt("mod_s", [NSEQ, 6144], F32, kind="Internal").ap()

    SBUF_LIMIT = nc.SBUF_PARTITION_SIZE_BYTES
    cur = [17 * 1024]

    def at(name, shape, dtp, align=32):
        nbytes = int(np.prod(shape[1:])) * mybir.dt.size(dtp)
        cur[0] = (cur[0] + align - 1) // align * align
        t = nc.alloc_sbuf_tensor_at(name, shape, dtp, offset=cur[0])
        cur[0] += nbytes
        return t

    ident = at("ident", [128, 128], BF16)
    tri = at("tri", [128, 128], BF16)
    indik = at("indik", [2, 128], F32)
    ones1 = at("ones1", [1, 128], F32)
    ct = at("ct", [128, DC, NSEQ], F32)
    vrow = at("vrow", [NSEQ, 1024], F32)
    bc0 = at("bc0", [128, D], F32)
    bc1 = at("bc1", [128, D], F32)
    bc2 = at("bc2", [128, D], F32)
    ln_s = at("ln_s", [128, 10], F32)
    onescol = at("onescol", [128, 1], BF16)
    ones_sb = at("ones_sb", [128, 64], F32)
    act_scr = at("act_scr", [128, D], F32)
    xin = at("xin", [128, 2, D], F32)
    xn_t = at("xn_t", [128, D], BF16)
    x1_off = (cur[0] + 31) // 32 * 32
    x1 = at("x1", [128, NT, D], BF16)
    shared_end = cur[0]

    ov0 = (shared_end + 31) // 32 * 32

    cur[0] = ov0
    stW = at("stW", [128, 2, 3072], F32)
    assert cur[0] <= SBUF_LIMIT, f"phase-W overlay over budget: {cur[0]}"

    cur[0] = ov0
    h0 = at("h0", [128, DC, S_], BF16)
    qf = nc.alloc_sbuf_tensor_at("qf", [128, DC, S_], BF16, offset=ov0)
    wX = at("wX", [128, DC, D], BF16)
    kf = at("kf", [128, DC, S_], BF16)
    vt = at("vt", [128, NT, 16 * 64], BF16)
    attw_off = (cur[0] + 31) // 32 * 32
    attw = at("attw", [128, DC, QCW], BF16)
    pa = at("pa", [128, QCW], BF16)
    pb = at("pb", [128, QCW], BF16)
    densb = at("densb", [128, QCW], F32)
    recb = at("recb", [128, QCW], F32)
    ot1 = recb
    xn_u = nc.alloc_sbuf_tensor_at("xn_u", [128, D], BF16, offset=attw_off)
    xnn = nc.alloc_sbuf_tensor_at("xnn", [128, D], F32, offset=attw_off + 2048)
    att_end = cur[0]
    assert cur[0] <= SBUF_LIMIT, f"phase-ATT overlay over budget: {cur[0]}"

    cur[0] = ov0
    qU = at("qU", [128, 1024], BF16)
    qX = at("qX", [128, 1024], BF16)
    qM1 = at("qM1", [128, 1024], BF16)
    qSG = at("qSG", [128, 1024], BF16)
    qsm = at("qsm", [128, 3, 64], F32)
    qsb = at("qsb", [128, 64], BF16)
    xn2 = at("xn2", [128, D], F32)
    h1g = at("h1g", [128, D], F32)
    h1f = at("h1f", [128, D], F32)
    h1qt = at("h1qt", [128, D], BF16)
    h1qf = at("h1qf", [128, DC, QCW], BF16)
    wgb = at("wgb", [128, 2, DC, 512], BF16)
    wub = at("wub", [128, 2, DC, 512], BF16)
    wdb = at("wdb", [128, 2, 512], BF16)
    silu_s = at("silu_s", [128, 512], F32)
    h2f = at("h2f", [128, 512], F32)
    h2qt = at("h2qt", [128, 2, 512], BF16)
    h2qi = at("h2qi", [128, IC, QCW], BF16)
    o1f = at("o1f", [128, 512], F32)
    x3s = at("x3s", [128, 4, D], F32)
    ffn_end = cur[0]
    assert cur[0] <= SBUF_LIMIT, f"phase-FFN overlay over budget: {cur[0]}"

    cur[0] = max(att_end, ffn_end)
    wreg0 = (cur[0] + 31) // 32 * 32
    stW2 = at("stW2", [128, 2, 1024], F32)
    h1qf2 = nc.alloc_sbuf_tensor_at("h1qf2", [128, DC, QCW], BF16, offset=wreg0)
    qbf2 = at("qbf2", [128, 2, 1024], BF16)
    tstage2 = at("tstage2", [128, 1024], BF16)
    qsm_w = at("qsm_w", [128, 3, 64], F32)
    qsb_w = at("qsb_w", [128, 64], BF16)
    assert cur[0] <= SBUF_LIMIT, f"W-quant region over budget: {cur[0]}"
    qU_w = nc.alloc_sbuf_tensor_at("qU_w", [128, 1024], BF16, offset=x1_off + 24576)
    qX_w = nc.alloc_sbuf_tensor_at("qX_w", [128, 1024], BF16, offset=x1_off + 26624)
    qM1_w = nc.alloc_sbuf_tensor_at("qM1_w", [128, 1024], BF16, offset=x1_off + 28672)
    qSG_w = nc.alloc_sbuf_tensor_at("qSG_w", [128, 1024], BF16, offset=x1_off + 30720)

    P = [nc.alloc_psum_tensor(f"P{k}", [128, 512], F32) for k in range(7)]
    T1 = nc.alloc_psum_tensor("T1", [128, 512], BF16)
    Ttile = [T1[:, 0:128], T1[:, 128:256]]

    semctx = [nc.semaphore(nm) for nm in ["LD", "ST", "TE", "VE", "SE", "GE"]]
    handles = [c.__enter__() for c in semctx]
    LD, ST, TE, VE, SE, GE = [Sem(h) for h in handles]
    allsems = [LD, ST, TE, VE, SE, GE]
    pr = Prog(nc)

    (ln_sx, ln_m2, ln_nm1, ln_nvar, ln_is, ln_std, eps_b, exp_b, zcol,
     jcol) = (ln_s[:, k:k + 1] for k in range(10))

    def script(pr: Prog):
        S, G, T, V, C = pr.S, pr.G, pr.T, pr.V, pr.C
        dbg_slot = [0]

        def tap(ap, cols=1024):
            if not debug_taps:
                return
            k = dbg_slot[0]
            dbg_slot[0] += 1
            G.wait(VE, VE.v)
            G.wait(SE, SE.v)
            G.wait(TE, TE.v)
            G.wait(GE, GE.v)
            G.dma(dbg_h[:, k, 0:cols], ap, inc=(ST, 16))

        def v3(ap2, w):
            return ap2.rearrange("p (nb b) -> p nb b", b=16)

        def quant(src, dst, w, Q, te_guard=None):
            (U, A_, B_, SG, SM, SB) = Q
            nb = w // 16
            C.wait(GE, GE.v)
            C.wait(VE, VE.v)
            C.op(lambda e: e.activation(SG[:, 0:w], src, AF.Sign), inc=SE)
            se_sgn = SE.v
            m_, s_, r_ = (SM[:, k, 0:nb] for k in range(3))
            sb_ = SB[:, 0:nb]
            V.wait(GE, GE.v)
            hv = nb // 2
            halves = [(0, hv), (hv, nb)] if (2 <= hv < 128) else [(0, nb)]
            srcv = v3(src, w)

            def forall(fn):
                for (a, b) in halves:
                    V.op(lambda e, a=a, b=b: fn(e, slice(a, b)))

            forall(lambda e, sl: e.tensor_reduce(
                m_[:, sl], srcv[:, sl, :], AX.X, ALU.max,
                apply_absolute_value=True))
            forall(lambda e, sl: e.tensor_scalar(s_[:, sl], m_[:, sl], 1.0 / 6.0,
                                                 1e-12, ALU.mult, ALU.max))
            forall(lambda e, sl: e.reciprocal(r_[:, sl], s_[:, sl]))
            forall(lambda e, sl: e.tensor_copy(sb_[:, sl], s_[:, sl]))
            rb = r_.unsqueeze(2).broadcast_to([128, nb, 16])
            sbb = sb_.unsqueeze(2).broadcast_to([128, nb, 16])
            u = U[:, 0:w]
            u3 = v3(u, w)
            V.op(lambda e: e.tensor_tensor(u3, srcv, rb, ALU.mult))
            V.wait(SE, se_sgn)
            V.op(lambda e: e.tensor_tensor(u, u, SG[:, 0:w], ALU.mult))
            Aw, Bw = A_[:, 0:w], B_[:, 0:w]
            V.op(lambda e: e.tensor_scalar(Aw, u, 0.25, None, ALU.is_gt))
            V.op(lambda e: e.scalar_tensor_tensor(Aw, u, 0.75, Aw, ALU.is_gt, ALU.add))
            V.op(lambda e: e.scalar_tensor_tensor(Aw, u, 1.25, Aw, ALU.is_gt, ALU.add))
            V.op(lambda e: e.scalar_tensor_tensor(Aw, u, 1.75, Aw, ALU.is_gt, ALU.add))
            V.op(lambda e: e.tensor_scalar(Bw, u, 5.0, 2.0, ALU.is_gt, ALU.mult))
            V.op(lambda e: e.scalar_tensor_tensor(Bw, u, 2.5, Bw, ALU.is_gt, ALU.add))
            V.op(lambda e: e.scalar_tensor_tensor(Bw, u, 3.5, Bw, ALU.is_gt, ALU.add))
            V.op(lambda e: e.scalar_tensor_tensor(Aw, Aw, 0.5, Bw, ALU.mult, ALU.add))
            V.op(lambda e: e.tensor_tensor(Aw, Aw, SG[:, 0:w], ALU.mult))
            if te_guard is not None:
                V.wait(TE, te_guard)
            V.op(lambda e: e.tensor_tensor(v3(dst, w), v3(Aw, w), sbb, ALU.mult),
                 inc=VE)

        QW = (qU_w, qX_w, qM1_w, qSG_w, qsm_w, qsb_w)
        QF = (qU, qX, qM1, qSG, qsm, qsb)

        def transpose_blocks(src_fn, nblk, dst_fn):
            for b in range(nblk):
                par = b % 2
                T.wait(SE, SE.v)
                T.op(lambda e, b=b, par=par: e.transpose(Ttile[par], src_fn(b), ident[:]),
                     inc=TE)
                C.wait(TE, TE.v)
                C.op(lambda e, b=b, par=par: e.copy(dst_fn(b), Ttile[par]), inc=SE)

        def ln_norm(x_ap, out_ap):
            C.op(lambda e: e.activation(act_scr[:], x_ap, AF.Copy, accum_out=ln_sx))
            C.op(lambda e: e.activation(act_scr[:], x_ap, AF.Square,
                                        scale=1.0 / 32.0, bias=zcol,
                                        accum_out=ln_m2))
            C.op(lambda e: e.activation(ln_nm1, ln_sx, AF.Identity,
                                        scale=-1.0 / D, bias=zcol), inc=SE)
            V.wait(SE, SE.v)
            V.op(lambda e: e.scalar_tensor_tensor(ln_nvar, ln_nm1, ln_nm1, ln_m2,
                                                  ALU.mult, ALU.subtract), inc=VE)
            C.wait(VE, VE.v)
            C.op(lambda e: e.activation(ln_std, ln_nvar, AF.Sqrt, scale=-1.0,
                                        bias=eps_b), inc=SE)
            V.wait(SE, SE.v)
            V.op(lambda e: e.reciprocal(ln_is, ln_std))
            V.op(lambda e: e.memset(jcol, 0.0))
            V.op(lambda e: e.tensor_scalar(out_ap, x_ap, ln_nm1, ln_is,
                                           ALU.add, ALU.mult), inc=VE)
            return VE.v

        for dst, src in [(ident[:], ident_h), (tri[:], tri_h), (indik[:], indik_h),
                         (ones1[:], ones1_h), (ct[:], ct_h)]:
            S.dma(dst, src, inc=(LD, 16))
        V.op(lambda e: e.memset(eps_b, EPS))
        V.op(lambda e: e.memset(zcol, 0.0))
        V.op(lambda e: e.memset(jcol, 0.0))
        V.op(lambda e: e.memset(onescol[:], 1.0))
        V.op(lambda e: e.memset(ones_sb[:], 1.0))
        V.op(lambda e: e.memset(exp_b, -6.0), inc=VE)
        C.wait(VE, VE.v)
        C.wait(LD, LD.v)
        ctf = ct[:].rearrange("p c n -> p (c n)")
        C.op(lambda e: e.activation(ctf, ctf, AF.Silu), inc=SE)

        for c in range(DC):
            if c >= 1:
                S.wait(TE, TE.v)
            S.dma(stW[:, 0, :], wmod_h[128 * c:128 * (c + 1), 0:3072], inc=(LD, 16))
            S.dma(stW[:, 1, :], wmod_h[128 * c:128 * (c + 1), 3072:6144], inc=(LD, 16))
            T.wait(LD, LD.v)
            T.wait(SE, SE.v)
            for h in range(2):
                for n in range(6):
                    col = 3072 * h + 512 * n
                    tp = (0, 0) if h == 0 else (0, 32)
                    rows = slice(0, NSEQ) if h == 0 else slice(32, 32 + NSEQ)
                    last = (h == 1 and n == 5)
                    T.op(lambda e, col=col, tp=tp, rows=rows, c=c: e.matmul(
                        P[n][rows, :], ct[:, c, :],
                        stW[:, col // 3072, (col % 3072):(col % 3072) + 512],
                        start=(c == 0), stop=(c == DC - 1), tile_position=tp),
                        inc=(TE if last else None))
        V.wait(TE, TE.v)
        for k in range(6):
            h = k // 3
            n0 = (2 * k) % 6
            rows = slice(0, NSEQ) if h == 0 else slice(32, 32 + NSEQ)
            r0 = 0 if h == 0 else 32
            buf = act_scr[r0:r0 + NSEQ, :]
            tbf = xin[r0:r0 + NSEQ, 0, 0:1024]
            if k >= 1:
                V.wait(ST, ST.v)
            V.op(lambda e, n0=n0, rows=rows, buf=buf: e.tensor_copy(
                buf[:, 0:512], P[n0][rows, :]))
            V.op(lambda e, n0=n0, rows=rows, buf=buf: e.tensor_copy(
                buf[:, 512:1024], P[n0 + 1][rows, :]))
            if k in (1, 4):
                V.op(lambda e, buf=buf: e.tensor_scalar(buf, buf, 1.0, None, ALU.add))
            if k in (2, 5):
                V.op(lambda e, buf=buf, tbf=tbf: e.tensor_scalar(
                    tbf, buf, 0.0, 2.0, ALU.is_ge, ALU.mult))
                V.op(lambda e, tbf=tbf: e.tensor_scalar(tbf, tbf, -1.0, None, ALU.add))
                V.op(lambda e, buf=buf, tbf=tbf: e.tensor_tensor(buf, buf, tbf,
                                                                 ALU.mult))
                V.op(lambda e, buf=buf: e.tensor_scalar(buf, buf, 0.25, None, ALU.max))
                V.op(lambda e, buf=buf, tbf=tbf: e.tensor_tensor(buf, buf, tbf, ALU.mult))
            V.op(lambda e, buf=buf: e.tensor_scalar(buf, buf, 0.0, None, ALU.add),
                 inc=VE)
            G.wait(VE, VE.v)
            G.dma(mod_h[:, 1024 * k:1024 * (k + 1)], buf, inc=(ST, 16))

        if debug_stop == "mod":
            G.wait(ST, ST.v)
            return
        wchunks = []
        for icx in range(cfg.IC):
            wchunks.append((wg_h, 128 * icx, 0, D, wgq_h))
        for icx in range(cfg.IC):
            wchunks.append((wu_h, 128 * icx, 0, D, wuq_h))
        for hc in range(DC):
            for coff, cwid in ((0, 1024), (1024, 1024), (2048, 768)):
                wchunks.append((wd_h, 128 * hc, coff, cwid, wdq_h))
        wstate = {"li": 0, "ti": 0, "te_qbf": [0, 0]}

        def w_load_quant():
            i = wstate["li"]
            if i >= len(wchunks):
                return
            wstate["li"] += 1
            src_h, r0, c0, cw, dst_h = wchunks[i]
            par = i % 2
            S.wait(VE, VE.v)
            S.wait(SE, SE.v)
            S.dma(stW2[:, par, 0:cw], src_h[r0:r0 + 128, c0:c0 + cw],
                  inc=(LD, 16))
            V.wait(LD, LD.v)
            C.wait(LD, LD.v)
            quant(stW2[:, par, 0:cw], qbf2[:, par, 0:cw], cw, QW,
                  te_guard=wstate["te_qbf"][par])

        def w_transpose_store():
            j = wstate["ti"]
            if j >= wstate["li"]:
                return
            wstate["ti"] += 1
            src_h, r0, c0, cw, dst_h = wchunks[j]
            par = j % 2
            nblk = cw // 128
            T.wait(VE, VE.v)
            T.wait(ST, ST.v)
            transpose_blocks(
                lambda b, par=par: qbf2[:, par, 128 * b:128 * (b + 1)], nblk,
                lambda b: tstage2[:, 128 * b:128 * (b + 1)])
            wstate["te_qbf"][par] = TE.v
            G.wait(SE, SE.v)
            if dst_h is wdq_h:
                dst_ap = dst_h[:, (c0 // 128):(c0 // 128) + nblk, r0:r0 + 128]
            else:
                dst_ap = dst_h[:, :, r0:r0 + 128]
            G.dma(dst_ap,
                  tstage2[:, 0:cw].rearrange("p (c w) -> p c w", w=128),
                  inc=(ST, 16))

        def wq_point(s):
            if s != 0:
                return
            w_transpose_store()
            w_load_quant()

        def wq_drain(s):
            if s != 0:
                return
            while wstate["ti"] < len(wchunks):
                w_transpose_store()
                w_load_quant()

        for s in range(NSEQ):
            xrow0 = s * S_

            def expand(k, dstb):
                S.wait(TE, TE.v)
                S.wait(ST, ST.v)
                S.dma(vrow[0:1, :], mod_h[s:s + 1, 1024 * k:1024 * (k + 1)],
                      inc=(LD, 16))
                T.wait(LD, LD.v)
                for half in range(2):
                    T.wait(VE, VE.v)
                    T.op(lambda e, half=half: e.matmul(
                        P[6][:, :], ones1[:], vrow[0:1, 512 * half:512 * half + 512],
                        start=True, stop=True), inc=TE)
                    V.wait(TE, TE.v)
                    V.op(lambda e, half=half: e.tensor_copy(
                        dstb[:, 512 * half:512 * half + 512], P[6][:, :]), inc=VE)

            expand(0, bc0)
            expand(1, bc1)
            expand(2, bc2)
            if s == 0 and debug_taps:
                tap(bc0[:]); tap(bc1[:]); tap(bc2[:])

            for t in range(NT):
                par = t % 2
                S.wait(SE, SE.v)
                S.wait(VE, VE.v)
                S.dma(xin[:, par, :], x_h[xrow0 + 128 * t: xrow0 + 128 * (t + 1), :],
                      inc=(LD, 16))
                C.wait(LD, LD.v)
                V.wait(GE, GE.v)
                ln_norm(xin[:, par, :], xnn[:])
                G.wait(VE, VE.v)
                G.op(lambda e: e.tensor_tensor(xn_u[:], xnn[:], bc1[:], ALU.mult))
                G.wait(TE, TE.v)
                G.op(lambda e: e.tensor_tensor(xn_t[:], xn_u[:], bc0[:], ALU.add),
                     inc=GE)
                T.wait(GE, GE.v)
                if s == 0 and t == 0 and debug_taps:
                    tap(xn_t[:])
                transpose_blocks(
                    lambda b: xn_t[:, 128 * b:128 * (b + 1)], DC,
                    lambda b, t=t: h0[:, b, 128 * t:128 * (t + 1)])
                wq_point(s)

            def load_w(w_h, dst):
                G.wait(TE, TE.v)
                G.dma(dst[:], w_h.rearrange("(c p) w -> p c w", p=128),
                      inc=(ST, 16))

            def proj_block(wbuf, nq, dst_col_fn):
                T.wait(SE, SE.v)
                for wave in ([0, 1, 2, 3, 4, 5, 6], [7]):
                    for mc in wave:
                        bank = P[mc % 7]
                        if mc == 7:
                            C.wait(TE, TE.v)
                            for mc2 in [0, 1, 2, 3, 4, 5, 6]:
                                C.op(lambda e, mc2=mc2, nq=nq: e.copy(
                                    dst_col_fn(mc2, nq), P[mc2][:, :]), inc=SE)
                            T.wait(SE, SE.v)
                        for kc in range(DC):
                            T.op(lambda e, kc=kc, mc=mc, nq=nq, bank=bank: e.matmul(
                                bank[:, :], wbuf[:, kc, 128 * mc:128 * (mc + 1)],
                                h0[:, kc, 512 * nq:512 * (nq + 1)],
                                start=(kc == 0), stop=(kc == DC - 1)),
                                inc=(TE if kc == DC - 1 else None))
                C.wait(TE, TE.v)
                C.op(lambda e, nq=nq: e.copy(dst_col_fn(7, nq), P[0][:, :]), inc=SE)

            def proj_block_q(wbuf, nq):
                T.wait(SE, SE.v)
                T.wait(VE, VE.v)
                for mc in range(DC):
                    bank = P[mc % 7]
                    if mc == 7:
                        C.wait(TE, TE.v)
                        for mc2 in range(7):
                            C.op(lambda e, mc2=mc2: e.copy(
                                attw[:, mc2, :], P[mc2][:, :]), inc=SE)
                        T.wait(SE, SE.v)
                    for kc in range(DC):
                        T.op(lambda e, kc=kc, mc=mc, nq=nq, bank=bank: e.matmul(
                            bank[:, :], wbuf[:, kc, 128 * mc:128 * (mc + 1)],
                            h0[:, kc, 512 * nq:512 * (nq + 1)],
                            start=(kc == 0), stop=(kc == DC - 1)),
                            inc=(TE if kc == DC - 1 else None))
                C.wait(TE, TE.v)
                C.op(lambda e: e.copy(attw[:, 7, :], P[0][:, :]), inc=SE)
                V.wait(SE, SE.v)
                V.op(lambda e, nq=nq: e.tensor_copy(
                    qf[:, :, 512 * nq:512 * (nq + 1)], attw[:]), inc=VE)

            if s == 0 and debug_taps:
                tap(h0[:, 0, 0:1024])
            load_w(wk_h, wX)
            if s == 0 and debug_taps:
                tap(wX[:, 0, :])
            T.wait(ST, ST.v)
            for nq in range(S_ // 512):
                proj_block(wX, nq, lambda mc, nq: kf[:, mc, 512 * nq:512 * (nq + 1)])
                wq_point(s)
                wq_point(s)

            if s == 0 and debug_taps:
                tap(kf[:, 0, 0:1024])
            load_w(wv_h, wX)
            T.wait(ST, ST.v)
            for t in range(NT):
                for nd in range(2):
                    bank = P[(2 * t + nd) % 7]
                    if t * 2 + nd >= 7:
                        T.wait(SE, SE.v)
                    for kc in range(DC):
                        T.op(lambda e, kc=kc, t=t, nd=nd, bank=bank: e.matmul(
                            bank[:, :], h0[:, kc, 128 * t:128 * (t + 1)],
                            wX[:, kc, 512 * nd:512 * (nd + 1)],
                            start=(kc == 0), stop=(kc == DC - 1)),
                            inc=(TE if kc == DC - 1 else None))
                    C.wait(TE, TE.v)
                    C.op(lambda e, t=t, nd=nd, bank=bank: e.copy(
                        vt[:, t, 512 * nd:512 * (nd + 1)], bank[:, :]), inc=SE)
                if t % 2 == 1:
                    wq_point(s)
            T.wait(SE, SE.v)

            if s == 0 and debug_taps:
                tap(vt[:, 0, :])
            load_w(wq_h, wX)
            T.wait(ST, ST.v)
            for nq in range(S_ // 512):
                proj_block_q(wX, nq)
                wq_point(s)
                wq_point(s)

            if s == 0 and debug_taps:
                tap(qf[:, 0, 0:1024])
            load_w(wo_h, wX)
            T.wait(ST, ST.v)

            for qc in range(NQC):
                njc = 4 * qc + 4
                for p in range(NPAIR):
                    for j in range(njc):
                        vs = max(0, 128 * j - QCW * qc)
                        diag = (128 * j >= QCW * qc)
                        sA = P[0 + (j % 2)]
                        sB = P[2 + (j % 2)]
                        T.wait(SE, SE.v)
                        q_ap = qf[0:64, p, QCW * qc + vs: QCW * (qc + 1)]
                        qb_ap = qf[64:128, p, QCW * qc + vs: QCW * (qc + 1)]
                        T.op(lambda e, sA=sA, j=j, vs=vs, p=p, q_ap=q_ap, diag=diag:
                             e.matmul(sA[:, vs:QCW], kf[0:64, p, 128 * j:128 * (j + 1)],
                                      q_ap, start=True, stop=not diag,
                                      tile_position=(0, 0)))
                        if diag:
                            T.op(lambda e, sA=sA, vs=vs: e.matmul(
                                sA[:, vs:vs + 128], ident[:], tri[:],
                                start=False, stop=True))
                        T.op(lambda e, sB=sB, j=j, vs=vs, p=p, qb_ap=qb_ap, diag=diag:
                             e.matmul(sB[:, vs:QCW], kf[64:128, p, 128 * j:128 * (j + 1)],
                                      qb_ap, start=True, stop=not diag,
                                      tile_position=(64, 0)),
                             inc=(None if diag else TE))
                        if diag:
                            T.op(lambda e, sB=sB, vs=vs: e.matmul(
                                sB[:, vs:vs + 128], ident[:], tri[:],
                                start=False, stop=True), inc=TE)
                        C.wait(TE, TE.v)
                        C.op(lambda e, sA=sA, vs=vs: e.activation(
                            pa[:, vs:QCW], sA[:, vs:QCW], AF.Exp, scale=0.125,
                            bias=exp_b), inc=SE)
                        C.op(lambda e, sB=sB, vs=vs: e.activation(
                            pb[:, vs:QCW], sB[:, vs:QCW], AF.Exp, scale=0.125,
                            bias=exp_b), inc=SE)
                        T.wait(SE, SE.v)
                        T.op(lambda e, j=j, vs=vs, p=p: e.matmul(
                            P[4][0:64, vs:QCW], vt[:, j, 128 * p:128 * p + 64],
                            pa[:, vs:QCW], start=(j == 0), stop=(j == njc - 1)))
                        T.op(lambda e, j=j, vs=vs, p=p: e.matmul(
                            P[5][64:128, vs:QCW], vt[:, j, 128 * p + 64:128 * (p + 1)],
                            pb[:, vs:QCW], start=(j == 0), stop=(j == njc - 1),
                            tile_position=(0, 64)))
                        T.op(lambda e, j=j, vs=vs: e.matmul(
                            P[6][0:1, vs:QCW], onescol[:], pa[:, vs:QCW],
                            start=(j == 0), stop=(j == njc - 1)))
                        T.op(lambda e, j=j, vs=vs: e.matmul(
                            P[6][32:33, vs:QCW], onescol[:], pb[:, vs:QCW],
                            start=(j == 0), stop=(j == njc - 1),
                            tile_position=(0, 32)), inc=TE)
                    C.wait(TE, TE.v)
                    C.op(lambda e: e.copy(densb[0:1, 0:QCW], P[6][0:1, :]))
                    C.op(lambda e: e.copy(densb[32:33, 0:QCW], P[6][32:33, :]),
                         inc=SE)
                    V.wait(SE, SE.v)
                    V.op(lambda e: e.reciprocal(densb[0:1, 0:QCW], densb[0:1, 0:QCW]))
                    V.op(lambda e: e.reciprocal(densb[32:33, 0:QCW],
                                                densb[32:33, 0:QCW]), inc=VE)
                    T.wait(VE, VE.v)
                    T.op(lambda e: e.matmul(P[6][0:64, :], ones1[0:1, 0:64],
                                            densb[0:1, 0:QCW],
                                            start=True, stop=True))
                    T.op(lambda e: e.matmul(P[6][64:128, :], ones_sb[32:33, :],
                                            densb[32:33, 0:QCW],
                                            start=True, stop=True,
                                            tile_position=(32, 64)), inc=TE)
                    V.wait(TE, TE.v)
                    V.op(lambda e: e.tensor_copy(recb[:], P[6][:, :]))
                    V.op(lambda e, p=p: e.tensor_tensor(
                        attw[0:64, p, :], P[4][0:64, :], recb[0:64, :], ALU.mult))
                    V.op(lambda e, p=p: e.tensor_tensor(
                        attw[64:128, p, :], P[5][64:128, :], recb[64:128, :],
                        ALU.mult), inc=VE)
                    T.wait(VE, VE.v)
                    wq_point(s)

                if qc == NQC - 1:
                    wq_drain(s)
                for qt in range(4):
                    tglob = 4 * qc + qt
                    par = tglob % 2
                    S.wait(VE, VE.v)
                    S.wait(SE, SE.v)
                    S.dma(xin[:, par, :],
                          x_h[xrow0 + 128 * tglob: xrow0 + 128 * (tglob + 1), :],
                          inc=(LD, 16))
                    for nd in range(2):
                        T.wait(VE, VE.v)
                        for c in range(DC):
                            T.op(lambda e, c=c, qt=qt, nd=nd: e.matmul(
                                P[6][:, :], attw[:, c, 128 * qt:128 * (qt + 1)],
                                wX[:, c, 512 * nd:512 * (nd + 1)],
                                start=(c == 0), stop=(c == DC - 1)),
                                inc=(TE if c == DC - 1 else None))
                        V.wait(TE, TE.v)
                        V.wait(LD, LD.v)
                        V.op(lambda e, nd=nd: e.tensor_tensor(
                            ot1[:, :], P[6][:, :], bc2[:, 512 * nd:512 * (nd + 1)],
                            ALU.mult))
                        V.op(lambda e, nd=nd, tglob=tglob, par=par: e.tensor_tensor(
                            x1[:, tglob, 512 * nd:512 * (nd + 1)], ot1[:, :],
                            xin[:, par, 512 * nd:512 * (nd + 1)], ALU.add), inc=VE)

            if s == 0 and debug_taps:
                tap(attw[:, 0, :], 512)
                tap(pa[:], 512)
                tap(recb[:], 512)
                tap(x1[:, 0, :])
                tap(x1[:, NT - 1, :])
            expand(3, bc0)
            expand(4, bc1)
            expand(5, bc2)

            h1_prev_ln = [0]

            def h1_tile(qc2, qt2, dstbuf):
                tglob = 4 * qc2 + qt2
                C.wait(VE, h1_prev_ln[0])
                V.wait(GE, GE.v)
                h1_prev_ln[0] = ln_norm(x1[:, tglob, :], xn2[:])
                G.wait(VE, VE.v)
                G.op(lambda e: e.tensor_tensor(h1g[:], xn2[:], bc1[:], ALU.mult))
                G.wait(SE, SE.v)
                G.op(lambda e: e.tensor_tensor(h1f[:], h1g[:], bc0[:], ALU.add),
                     inc=GE)
                quant(h1f[:], h1qt[:], D, QF)
                T.wait(VE, VE.v)
                transpose_blocks(
                    lambda b: h1qt[:, 128 * b:128 * (b + 1)], DC,
                    lambda b, qt2=qt2: dstbuf[:, b, 128 * qt2:128 * (qt2 + 1)])

            HOIST = False
            CONSV = False
            h1bufs = [h1qf, h1qf2]
            for qt in range(4):
                h1_tile(0, qt, h1bufs[0])
            h1_se = [SE.v]

            gucnt = [0]
            ve_guard = [VE.v] * 7
            se_guard = [SE.v] * 7
            h2f_ve = [VE.v]
            te_guard_w = [TE.v, TE.v]
            tp_prev = [None]

            def flush_tp():
                if tp_prev[0] is None:
                    return
                pnoff, pnw, pqt, ppar2, q_ve = tp_prev[0]
                tp_prev[0] = None
                T.wait(VE, q_ve)
                transpose_blocks(
                    lambda b, ppar2=ppar2: h2qt[:, ppar2, 128 * b:128 * (b + 1)],
                    pnw // 128,
                    lambda b, pnoff=pnoff, pqt=pqt: h2qi[:, (pnoff // 128) + b,
                                                         128 * pqt:128 * (pqt + 1)])

            for qc in range(NQC):
                hsrc = h1bufs[qc % 2]
                if not HOIST and qc > 0:
                    for qt in range(4):
                        h1_tile(qc, qt, hsrc)
                    h1_se[0] = SE.v
                for nbi, (noff, nw) in enumerate(cfg.NBL):
                    par = nbi % 2
                    S.wait(TE, TE.v)
                    S.dma(wgb[:, par, :, 0:nw], wgq_h[:, :, noff:noff + nw],
                          inc=(LD, 16))
                    S.dma(wub[:, par, :, 0:nw], wuq_h[:, :, noff:noff + nw],
                          inc=(LD, 16))
                    for qt in range(4):
                        k = gucnt[0] % 2
                        gucnt[0] += 1
                        gP = P[k]
                        uP = P[2 + k]
                        T.wait(LD, LD.v)
                        T.wait(VE, max(ve_guard[k], ve_guard[2 + k]))
                        T.wait(SE, max(se_guard[k], h1_se[0]))
                        for kc in range(DC):
                            T.op(lambda e, kc=kc, qt=qt, par=par, nw=nw, gP=gP,
                                 hsrc=hsrc: e.matmul(
                                gP[:, 0:nw], hsrc[:, kc, 128 * qt:128 * (qt + 1)],
                                wgb[:, par, kc, 0:nw],
                                start=(kc == 0), stop=(kc == DC - 1)))
                        for kc in range(DC):
                            T.op(lambda e, kc=kc, qt=qt, par=par, nw=nw, uP=uP,
                                 hsrc=hsrc: e.matmul(
                                uP[:, 0:nw], hsrc[:, kc, 128 * qt:128 * (qt + 1)],
                                wub[:, par, kc, 0:nw],
                                start=(kc == 0), stop=(kc == DC - 1)),
                                inc=(TE if kc == DC - 1 else None))
                        C.wait(TE, TE.v)
                        C.wait(VE, h2f_ve[0])
                        C.op(lambda e, gP=gP, nw=nw: e.activation(
                            silu_s[:, 0:nw], gP[:, 0:nw], AF.Silu), inc=SE)
                        se_guard[k] = SE.v
                        flush_tp()
                        V.wait(SE, se_guard[k])
                        V.op(lambda e, uP=uP, nw=nw: e.tensor_tensor(
                            h2f[:, 0:nw], silu_s[:, 0:nw], uP[:, 0:nw], ALU.mult),
                            inc=VE)
                        ve_guard[2 + k] = VE.v
                        h2f_ve[0] = VE.v
                        par2 = gucnt[0] % 2
                        quant(h2f[:, 0:nw], h2qt[:, par2, 0:nw], nw, QF)
                        tp_prev[0] = (noff, nw, qt, par2, VE.v)
                    te_guard_w[par] = TE.v
                    if HOIST and qc < NQC - 1 and nbi < 4:
                        h1_tile(qc + 1, nbi, h1bufs[(qc + 1) % 2])
                flush_tp()

                for nd in range(2):
                    for icx in range(IC):
                        par = icx % 2
                        S.wait(TE, TE.v)
                        S.dma(wdb[:, par, :], wdq_h[:, icx, 512 * nd:512 * (nd + 1)],
                              inc=(LD, 16))
                        T.wait(LD, LD.v)
                        if icx == 0:
                            T.wait(VE, VE.v)
                            T.wait(SE, SE.v)
                        for qt in range(4):
                            T.op(lambda e, icx=icx, qt=qt, par=par: e.matmul(
                                P[qt][:, :], h2qi[:, icx, 128 * qt:128 * (qt + 1)],
                                wdb[:, par, :],
                                start=(icx == 0), stop=(icx == IC - 1)),
                                inc=(TE if qt == 3 else None))
                    V.wait(TE, TE.v)
                    for qt in range(4):
                        tglob = 4 * qc + qt
                        V.wait(ST, ST.v)
                        V.op(lambda e, qt=qt, nd=nd: e.tensor_tensor(
                            o1f[:, :], P[qt][:, :], bc2[:, 512 * nd:512 * (nd + 1)],
                            ALU.mult))
                        V.op(lambda e, qt=qt, nd=nd, tglob=tglob: e.tensor_tensor(
                            x3s[:, qt, 512 * nd:512 * (nd + 1)], o1f[:, :],
                            x1[:, tglob, 512 * nd:512 * (nd + 1)], ALU.add),
                            inc=VE)
                        ve_guard[qt] = VE.v
                        if nd == 1:
                            G.wait(VE, VE.v)
                            G.dma(y_h[xrow0 + 128 * tglob: xrow0 + 128 * (tglob + 1), :],
                                  x3s[:, qt, :], inc=(ST, 16))

        G.wait(ST, ST.v)

    with nc.Block() as blk:
        @blk.sync
        def _(eng):
            pr.begin_pass("sync", eng, allsems)
            script(pr)

        @blk.gpsimd
        def _(eng):
            pr.begin_pass("gpsimd", eng, allsems)
            script(pr)

        @blk.tensor
        def _(eng):
            pr.begin_pass("tensor", eng, allsems)
            script(pr)

        @blk.vector
        def _(eng):
            pr.begin_pass("vector", eng, allsems)
            script(pr)

        @blk.scalar
        def _(eng):
            pr.begin_pass("scalar", eng, allsems)
            script(pr)

    for c in semctx:
        c.__exit__(None, None, None)
    return nc


def _consts():
    ident = np.eye(128, dtype=ml_dtypes.bfloat16)
    tri = np.zeros((128, 128), np.float32)
    r = np.arange(128)
    tri[r[:, None] > r[None, :]] = NEG_BIG
    tri = tri.astype(ml_dtypes.bfloat16)
    indik = np.zeros((2, 128), np.float32)
    indik[0, 0:64] = 1.0
    indik[1, 64:128] = 1.0
    ones1 = np.ones((1, 128), np.float32)
    return ident, tri, indik, ones1


def make_in_maps(cfg, x, wq, wk, wv, wo, wg, wu, wd, c, wmod, n_cores=8):
    ident, tri, indik, ones1 = _consts()
    shared = dict(
        wq=np.ascontiguousarray(wq, dtype=np.float32),
        wk=np.ascontiguousarray(wk, dtype=np.float32),
        wv=np.ascontiguousarray(wv, dtype=np.float32),
        wo=np.ascontiguousarray(wo, dtype=np.float32),
        wg=np.ascontiguousarray(wg, dtype=np.float32),
        wu=np.ascontiguousarray(wu, dtype=np.float32),
        wd=np.ascontiguousarray(wd, dtype=np.float32),
        wmod=np.ascontiguousarray(wmod, dtype=np.float32),
        ident=ident, tri=tri, indik=indik, ones1=ones1,
    )
    c = np.asarray(c, np.float32)
    x = np.asarray(x, np.float32)
    in_maps = []
    for i in range(n_cores):
        ct = np.ascontiguousarray(
            c[cfg.NSEQ * i:cfg.NSEQ * (i + 1)].T.reshape(DC, 128, cfg.NSEQ)
            .transpose(1, 0, 2))
        xi = np.ascontiguousarray(
            x[cfg.NSEQ * i:cfg.NSEQ * (i + 1)].reshape(cfg.NSEQ * cfg.S, D))
        in_maps.append(dict(shared, x=xi, ct=ct))
    return in_maps


def kernel(L0_attn_x, L0_attn_W_Q, L0_attn_W_K, L0_attn_W_V, L0_attn_W_o,
           L0_ffn_w_gate, L0_ffn_w_up, L0_ffn_w_down, L0_c, L0_wmod):
    cfg = Cfg(S=2048, I=2816)
    nc = build_program(cfg)
    in_maps = make_in_maps(cfg, L0_attn_x, L0_attn_W_Q, L0_attn_W_K, L0_attn_W_V,
                           L0_attn_W_o, L0_ffn_w_gate, L0_ffn_w_up, L0_ffn_w_down,
                           L0_c, L0_wmod)
    res = run_bass_kernel_spmd(nc, in_maps, core_ids=list(range(8)))
    out = np.empty((16, 2048, 1024), np.float32)
    for i in range(8):
        out[2 * i:2 * i + 2] = res.results[i]["y"].reshape(2, cfg.S, D)
    return out


# version 14
# speedup vs baseline: 1.2354x; 1.1207x over previous
import numpy as np
import ml_dtypes

import concourse.bass as bass
import concourse.mybir as mybir
from concourse.bass_utils import run_bass_kernel_spmd

F32 = mybir.dt.float32
BF16 = mybir.dt.bfloat16
I32 = mybir.dt.int32
AF = mybir.ActivationFunctionType
ALU = mybir.AluOpType
AX = mybir.AxisListType

AND_M = int(np.uint32(0xFFC00000).view(np.int32))
OR_M = 0x00200000
EXP_M = 0x7F800000
NEG_BIG = -1.0e9

D = 1024
DC = 8
NPAIR = 8
QCW = 512
EPS = 1e-5


class Cfg:
    def __init__(self, S=2048, I=2816):
        self.S = S
        self.NSEQ = 2
        self.I = I
        self.IC = I // 128
        self.NQC = S // QCW
        self.NT = S // 128
        self.NBL = []
        off = 0
        while off < I:
            w = min(512, I - off)
            self.NBL.append((off, w))
            off += w


class Sem:
    def __init__(self, handle):
        self.h = handle
        self.v = 0


class Eng:
    def __init__(self, prog, name):
        self.prog = prog
        self.name = name

    @property
    def real(self):
        return self.prog.active_eng if self.prog.active == self.name else None

    def _fin(self, ins, inc):
        if inc is not None:
            sem, n = (inc, 1) if isinstance(inc, Sem) else inc
            sem.v += n
            if ins is not None:
                ins.then_inc(sem.h, n)
        return ins

    def wait(self, sem, val=None):
        if val is None:
            val = sem.v
        if val <= 0:
            return
        e = self.real
        if e is not None:
            e.wait_ge(sem.h, val)

    def op(self, fn, inc=None):
        e = self.real
        ins = fn(e) if e is not None else None
        return self._fin(ins, inc)

    def dma(self, out, in_, inc=None):
        return self.op(lambda e: e.dma_start(out=out, in_=in_), inc)


class Prog:
    def __init__(self, nc):
        self.nc = nc
        self.active = None
        self.active_eng = None
        self.S = Eng(self, "sync")
        self.G = Eng(self, "gpsimd")
        self.T = Eng(self, "tensor")
        self.V = Eng(self, "vector")
        self.C = Eng(self, "scalar")

    def begin_pass(self, name, eng, sems):
        self.active = name
        self.active_eng = eng
        for s in sems:
            s.v = 0


def build_program(cfg: Cfg, debug_stop=None, debug_taps=False):
    nc = bass.Bass("TRN2", target_bir_lowering=False, detect_race_conditions=False)
    S_, I_, IC, NQC, NT, NSEQ = cfg.S, cfg.I, cfg.IC, cfg.NQC, cfg.NT, cfg.NSEQ

    dt = nc.dram_tensor
    x_h = dt("x", [NSEQ * S_, D], F32, kind="ExternalInput").ap()
    wq_h = dt("wq", [D, D], F32, kind="ExternalInput").ap()
    wk_h = dt("wk", [D, D], F32, kind="ExternalInput").ap()
    wv_h = dt("wv", [D, D], F32, kind="ExternalInput").ap()
    wo_h = dt("wo", [D, D], F32, kind="ExternalInput").ap()
    wg_h = dt("wg", [I_, D], F32, kind="ExternalInput").ap()
    wu_h = dt("wu", [I_, D], F32, kind="ExternalInput").ap()
    wd_h = dt("wd", [D, I_], F32, kind="ExternalInput").ap()
    ct_h = dt("ct", [128, DC, NSEQ], F32, kind="ExternalInput").ap()
    wmod_h = dt("wmod", [D, 6144], F32, kind="ExternalInput").ap()
    ident_h = dt("ident", [128, 128], BF16, kind="ExternalInput").ap()
    tri_h = dt("tri", [128, 128], BF16, kind="ExternalInput").ap()
    indik_h = dt("indik", [2, 128], F32, kind="ExternalInput").ap()
    ones1_h = dt("ones1", [1, 128], F32, kind="ExternalInput").ap()
    y_h = dt("y", [NSEQ * S_, D], F32, kind="ExternalOutput").ap()
    dbg_h = dt("dbg", [128, 16, 1024], F32, kind="ExternalOutput").ap() if debug_taps else None
    wgq_h = dt("wgq_s", [128, DC, I_], BF16, kind="Internal").ap()
    wuq_h = dt("wuq_s", [128, DC, I_], BF16, kind="Internal").ap()
    wdq_h = dt("wdq_s", [128, IC, D], BF16, kind="Internal").ap()
    mod_h = dt("mod_s", [NSEQ, 6144], F32, kind="Internal").ap()

    SBUF_LIMIT = nc.SBUF_PARTITION_SIZE_BYTES
    cur = [17 * 1024]

    def at(name, shape, dtp, align=32):
        nbytes = int(np.prod(shape[1:])) * mybir.dt.size(dtp)
        cur[0] = (cur[0] + align - 1) // align * align
        t = nc.alloc_sbuf_tensor_at(name, shape, dtp, offset=cur[0])
        cur[0] += nbytes
        return t

    ident = at("ident", [128, 128], BF16)
    tri = at("tri", [128, 128], BF16)
    indik = at("indik", [2, 128], F32)
    ones1 = at("ones1", [1, 128], F32)
    ct = at("ct", [128, DC, NSEQ], F32)
    vrow = at("vrow", [NSEQ, 1024], F32)
    bc0 = at("bc0", [128, D], F32)
    bc1 = at("bc1", [128, D], F32)
    bc2 = at("bc2", [128, D], F32)
    ln_s = at("ln_s", [128, 10], F32)
    onescol = at("onescol", [128, 1], BF16)
    ones_sb = at("ones_sb", [128, 64], F32)
    act_scr = at("act_scr", [128, D], F32)
    xin = at("xin", [128, 2, D], F32)
    xn_t = at("xn_t", [128, D], BF16)
    x1_off = (cur[0] + 31) // 32 * 32
    x1 = at("x1", [128, NT, D], BF16)
    shared_end = cur[0]

    ov0 = (shared_end + 31) // 32 * 32

    cur[0] = ov0
    stW = at("stW", [128, 2, 3072], F32)
    assert cur[0] <= SBUF_LIMIT, f"phase-W overlay over budget: {cur[0]}"

    cur[0] = ov0
    h0 = at("h0", [128, DC, S_], BF16)
    qf = nc.alloc_sbuf_tensor_at("qf", [128, DC, S_], BF16, offset=ov0)
    wX = at("wX", [128, DC, D], BF16)
    kf = at("kf", [128, DC, S_], BF16)
    vt = at("vt", [128, NT, 16 * 64], BF16)
    attw_off = (cur[0] + 31) // 32 * 32
    attw = at("attw", [128, DC, QCW], BF16)
    pa = at("pa", [128, QCW], BF16)
    pb = at("pb", [128, QCW], BF16)
    densb = at("densb", [128, QCW], F32)
    recb = at("recb", [128, QCW], F32)
    ot1 = recb
    xn_u = nc.alloc_sbuf_tensor_at("xn_u", [128, D], BF16, offset=attw_off)
    xnn = nc.alloc_sbuf_tensor_at("xnn", [128, D], F32, offset=attw_off + 2048)
    att_end = cur[0]
    assert cur[0] <= SBUF_LIMIT, f"phase-ATT overlay over budget: {cur[0]}"

    cur[0] = ov0
    qU = at("qU", [128, 1024], BF16)
    qX = at("qX", [128, 1024], BF16)
    qM1 = at("qM1", [128, 1024], BF16)
    qSG = at("qSG", [128, 1024], BF16)
    qsm = at("qsm", [128, 3, 64], F32)
    qsb = at("qsb", [128, 64], BF16)
    xn2 = at("xn2", [128, D], F32)
    h1g = at("h1g", [128, D], F32)
    h1f = at("h1f", [128, D], F32)
    h1qt = at("h1qt", [128, D], BF16)
    h1qf = at("h1qf", [128, DC, QCW], BF16)
    wgb = at("wgb", [128, 2, DC, 512], BF16)
    wub = at("wub", [128, 2, DC, 512], BF16)
    wdb = at("wdb", [128, 2, 512], BF16)
    silu_s = at("silu_s", [128, 512], F32)
    h2f = at("h2f", [128, 512], F32)
    h2qt = at("h2qt", [128, 2, 512], BF16)
    h2qi = at("h2qi", [128, IC, QCW], BF16)
    o1f = at("o1f", [128, 512], F32)
    x3s = at("x3s", [128, 4, D], F32)
    ffn_end = cur[0]
    assert cur[0] <= SBUF_LIMIT, f"phase-FFN overlay over budget: {cur[0]}"

    cur[0] = max(att_end, ffn_end)
    wreg0 = (cur[0] + 31) // 32 * 32
    stW2 = at("stW2", [128, 2, 1024], F32)
    h1qf2 = nc.alloc_sbuf_tensor_at("h1qf2", [128, DC, QCW], BF16, offset=wreg0)
    qbf2 = at("qbf2", [128, 2, 1024], BF16)
    tstage2 = at("tstage2", [128, 1024], BF16)
    qsm_w = at("qsm_w", [128, 3, 64], F32)
    qsb_w = at("qsb_w", [128, 64], BF16)
    assert cur[0] <= SBUF_LIMIT, f"W-quant region over budget: {cur[0]}"
    qU_w = nc.alloc_sbuf_tensor_at("qU_w", [128, 1024], BF16, offset=x1_off + 24576)
    qX_w = nc.alloc_sbuf_tensor_at("qX_w", [128, 1024], BF16, offset=x1_off + 26624)
    qM1_w = nc.alloc_sbuf_tensor_at("qM1_w", [128, 1024], BF16, offset=x1_off + 28672)
    qSG_w = nc.alloc_sbuf_tensor_at("qSG_w", [128, 1024], BF16, offset=x1_off + 30720)

    P = [nc.alloc_psum_tensor(f"P{k}", [128, 512], F32) for k in range(7)]
    T1 = nc.alloc_psum_tensor("T1", [128, 512], BF16)
    Ttile = [T1[:, 0:128], T1[:, 128:256]]

    semctx = [nc.semaphore(nm) for nm in ["LD", "ST", "TE", "VE", "SE", "GE"]]
    handles = [c.__enter__() for c in semctx]
    LD, ST, TE, VE, SE, GE = [Sem(h) for h in handles]
    allsems = [LD, ST, TE, VE, SE, GE]
    pr = Prog(nc)

    (ln_sx, ln_m2, ln_nm1, ln_nvar, ln_is, ln_std, eps_b, exp_b, zcol,
     jcol) = (ln_s[:, k:k + 1] for k in range(10))

    def script(pr: Prog):
        S, G, T, V, C = pr.S, pr.G, pr.T, pr.V, pr.C
        dbg_slot = [0]

        def tap(ap, cols=1024):
            if not debug_taps:
                return
            k = dbg_slot[0]
            dbg_slot[0] += 1
            G.wait(VE, VE.v)
            G.wait(SE, SE.v)
            G.wait(TE, TE.v)
            G.wait(GE, GE.v)
            G.dma(dbg_h[:, k, 0:cols], ap, inc=(ST, 16))

        def v3(ap2, w):
            return ap2.rearrange("p (nb b) -> p nb b", b=16)

        def quant(src, dst, w, Q, te_guard=None):
            (U, A_, B_, SG, SM, SB) = Q
            nb = w // 16
            C.wait(GE, GE.v)
            C.wait(VE, VE.v)
            C.op(lambda e: e.activation(SG[:, 0:w], src, AF.Sign), inc=SE)
            se_sgn = SE.v
            m_, s_, r_ = (SM[:, k, 0:nb] for k in range(3))
            sb_ = SB[:, 0:nb]
            V.wait(GE, GE.v)
            hv = nb // 2
            halves = [(0, hv), (hv, nb)] if (2 <= hv < 128) else [(0, nb)]
            srcv = v3(src, w)

            def forall(fn):
                for (a, b) in halves:
                    V.op(lambda e, a=a, b=b: fn(e, slice(a, b)))

            forall(lambda e, sl: e.tensor_reduce(
                m_[:, sl], srcv[:, sl, :], AX.X, ALU.max,
                apply_absolute_value=True))
            forall(lambda e, sl: e.tensor_scalar(s_[:, sl], m_[:, sl], 1.0 / 6.0,
                                                 1e-12, ALU.mult, ALU.max))
            forall(lambda e, sl: e.reciprocal(r_[:, sl], s_[:, sl]))
            forall(lambda e, sl: e.tensor_copy(sb_[:, sl], s_[:, sl]))
            rb = r_.unsqueeze(2).broadcast_to([128, nb, 16])
            sbb = sb_.unsqueeze(2).broadcast_to([128, nb, 16])
            u = U[:, 0:w]
            u3 = v3(u, w)
            V.op(lambda e: e.tensor_tensor(u3, srcv, rb, ALU.mult))
            V.wait(SE, se_sgn)
            V.op(lambda e: e.tensor_tensor(u, u, SG[:, 0:w], ALU.mult))
            Aw, Bw = A_[:, 0:w], B_[:, 0:w]
            V.op(lambda e: e.tensor_scalar(Aw, u, 0.25, None, ALU.is_gt))
            V.op(lambda e: e.scalar_tensor_tensor(Aw, u, 0.75, Aw, ALU.is_gt, ALU.add))
            V.op(lambda e: e.scalar_tensor_tensor(Aw, u, 1.25, Aw, ALU.is_gt, ALU.add))
            V.op(lambda e: e.scalar_tensor_tensor(Aw, u, 1.75, Aw, ALU.is_gt, ALU.add))
            V.op(lambda e: e.tensor_scalar(Bw, u, 5.0, 2.0, ALU.is_gt, ALU.mult))
            V.op(lambda e: e.scalar_tensor_tensor(Bw, u, 2.5, Bw, ALU.is_gt, ALU.add))
            V.op(lambda e: e.scalar_tensor_tensor(Bw, u, 3.5, Bw, ALU.is_gt, ALU.add))
            V.op(lambda e: e.scalar_tensor_tensor(Aw, Aw, 0.5, Bw, ALU.mult, ALU.add))
            V.op(lambda e: e.tensor_tensor(Aw, Aw, SG[:, 0:w], ALU.mult))
            if te_guard is not None:
                V.wait(TE, te_guard)
            V.op(lambda e: e.tensor_tensor(v3(dst, w), v3(Aw, w), sbb, ALU.mult),
                 inc=VE)

        QW = (qU_w, qX_w, qM1_w, qSG_w, qsm_w, qsb_w)
        QF = (qU, qX, qM1, qSG, qsm, qsb)

        def transpose_blocks(src_fn, nblk, dst_fn):
            for b in range(nblk):
                par = b % 2
                T.wait(SE, SE.v)
                T.op(lambda e, b=b, par=par: e.transpose(Ttile[par], src_fn(b), ident[:]),
                     inc=TE)
                C.wait(TE, TE.v)
                C.op(lambda e, b=b, par=par: e.copy(dst_fn(b), Ttile[par]), inc=SE)

        def ln_norm(x_ap, out_ap):
            C.op(lambda e: e.activation(act_scr[:], x_ap, AF.Copy, accum_out=ln_sx))
            C.op(lambda e: e.activation(act_scr[:], x_ap, AF.Square,
                                        scale=1.0 / 32.0, bias=zcol,
                                        accum_out=ln_m2))
            C.op(lambda e: e.activation(ln_nm1, ln_sx, AF.Identity,
                                        scale=-1.0 / D, bias=zcol), inc=SE)
            V.wait(SE, SE.v)
            V.op(lambda e: e.scalar_tensor_tensor(ln_nvar, ln_nm1, ln_nm1, ln_m2,
                                                  ALU.mult, ALU.subtract), inc=VE)
            C.wait(VE, VE.v)
            C.op(lambda e: e.activation(ln_std, ln_nvar, AF.Sqrt, scale=-1.0,
                                        bias=eps_b), inc=SE)
            V.wait(SE, SE.v)
            V.op(lambda e: e.reciprocal(ln_is, ln_std))
            V.op(lambda e: e.memset(jcol, 0.0))
            V.op(lambda e: e.tensor_scalar(out_ap, x_ap, ln_nm1, ln_is,
                                           ALU.add, ALU.mult), inc=VE)
            return VE.v

        for dst, src in [(ident[:], ident_h), (tri[:], tri_h), (indik[:], indik_h),
                         (ones1[:], ones1_h), (ct[:], ct_h)]:
            S.dma(dst, src, inc=(LD, 16))
        V.op(lambda e: e.memset(eps_b, EPS))
        V.op(lambda e: e.memset(zcol, 0.0))
        V.op(lambda e: e.memset(jcol, 0.0))
        V.op(lambda e: e.memset(onescol[:], 1.0))
        V.op(lambda e: e.memset(ones_sb[:], 1.0))
        V.op(lambda e: e.memset(exp_b, -6.0), inc=VE)
        C.wait(VE, VE.v)
        C.wait(LD, LD.v)
        ctf = ct[:].rearrange("p c n -> p (c n)")
        C.op(lambda e: e.activation(ctf, ctf, AF.Silu), inc=SE)

        for c in range(DC):
            if c >= 1:
                S.wait(TE, TE.v)
            S.dma(stW[:, 0, :], wmod_h[128 * c:128 * (c + 1), 0:3072], inc=(LD, 16))
            S.dma(stW[:, 1, :], wmod_h[128 * c:128 * (c + 1), 3072:6144], inc=(LD, 16))
            T.wait(LD, LD.v)
            T.wait(SE, SE.v)
            for h in range(2):
                for n in range(6):
                    col = 3072 * h + 512 * n
                    tp = (0, 0) if h == 0 else (0, 32)
                    rows = slice(0, NSEQ) if h == 0 else slice(32, 32 + NSEQ)
                    last = (h == 1 and n == 5)
                    T.op(lambda e, col=col, tp=tp, rows=rows, c=c: e.matmul(
                        P[n][rows, :], ct[:, c, :],
                        stW[:, col // 3072, (col % 3072):(col % 3072) + 512],
                        start=(c == 0), stop=(c == DC - 1), tile_position=tp),
                        inc=(TE if last else None))
        V.wait(TE, TE.v)
        for k in range(6):
            h = k // 3
            n0 = (2 * k) % 6
            rows = slice(0, NSEQ) if h == 0 else slice(32, 32 + NSEQ)
            r0 = 0 if h == 0 else 32
            buf = act_scr[r0:r0 + NSEQ, :]
            tbf = xin[r0:r0 + NSEQ, 0, 0:1024]
            if k >= 1:
                V.wait(ST, ST.v)
            V.op(lambda e, n0=n0, rows=rows, buf=buf: e.tensor_copy(
                buf[:, 0:512], P[n0][rows, :]))
            V.op(lambda e, n0=n0, rows=rows, buf=buf: e.tensor_copy(
                buf[:, 512:1024], P[n0 + 1][rows, :]))
            if k in (1, 4):
                V.op(lambda e, buf=buf: e.tensor_scalar(buf, buf, 1.0, None, ALU.add))
            if k in (2, 5):
                V.op(lambda e, buf=buf, tbf=tbf: e.tensor_scalar(
                    tbf, buf, 0.0, 2.0, ALU.is_ge, ALU.mult))
                V.op(lambda e, tbf=tbf: e.tensor_scalar(tbf, tbf, -1.0, None, ALU.add))
                V.op(lambda e, buf=buf, tbf=tbf: e.tensor_tensor(buf, buf, tbf,
                                                                 ALU.mult))
                V.op(lambda e, buf=buf: e.tensor_scalar(buf, buf, 0.25, None, ALU.max))
                V.op(lambda e, buf=buf, tbf=tbf: e.tensor_tensor(buf, buf, tbf, ALU.mult))
            V.op(lambda e, buf=buf: e.tensor_scalar(buf, buf, 0.0, None, ALU.add),
                 inc=VE)
            G.wait(VE, VE.v)
            G.dma(mod_h[:, 1024 * k:1024 * (k + 1)], buf, inc=(ST, 16))

        if debug_stop == "mod":
            G.wait(ST, ST.v)
            return
        wchunks = []
        for icx in range(cfg.IC):
            wchunks.append((wg_h, 128 * icx, 0, D, wgq_h))
        for icx in range(cfg.IC):
            wchunks.append((wu_h, 128 * icx, 0, D, wuq_h))
        for hc in range(DC):
            for coff, cwid in ((0, 1024), (1024, 1024), (2048, 768)):
                wchunks.append((wd_h, 128 * hc, coff, cwid, wdq_h))
        wstate = {"li": 0, "ti": 0, "te_qbf": [0, 0]}

        def w_load_quant():
            i = wstate["li"]
            if i >= len(wchunks):
                return
            wstate["li"] += 1
            src_h, r0, c0, cw, dst_h = wchunks[i]
            par = i % 2
            S.wait(VE, VE.v)
            S.wait(SE, SE.v)
            S.dma(stW2[:, par, 0:cw], src_h[r0:r0 + 128, c0:c0 + cw],
                  inc=(LD, 16))
            V.wait(LD, LD.v)
            C.wait(LD, LD.v)
            quant(stW2[:, par, 0:cw], qbf2[:, par, 0:cw], cw, QW,
                  te_guard=wstate["te_qbf"][par])

        def w_transpose_store():
            j = wstate["ti"]
            if j >= wstate["li"]:
                return
            wstate["ti"] += 1
            src_h, r0, c0, cw, dst_h = wchunks[j]
            par = j % 2
            nblk = cw // 128
            T.wait(VE, VE.v)
            T.wait(ST, ST.v)
            transpose_blocks(
                lambda b, par=par: qbf2[:, par, 128 * b:128 * (b + 1)], nblk,
                lambda b: tstage2[:, 128 * b:128 * (b + 1)])
            wstate["te_qbf"][par] = TE.v
            G.wait(SE, SE.v)
            if dst_h is wdq_h:
                dst_ap = dst_h[:, (c0 // 128):(c0 // 128) + nblk, r0:r0 + 128]
            else:
                dst_ap = dst_h[:, :, r0:r0 + 128]
            G.dma(dst_ap,
                  tstage2[:, 0:cw].rearrange("p (c w) -> p c w", w=128),
                  inc=(ST, 16))

        def wq_point(s):
            if s != 0:
                return
            w_transpose_store()
            w_load_quant()

        def wq_drain(s):
            if s != 0:
                return
            while wstate["ti"] < len(wchunks):
                w_transpose_store()
                w_load_quant()

        for s in range(NSEQ):
            xrow0 = s * S_

            def expand(k, dstb):
                S.wait(TE, TE.v)
                S.wait(ST, ST.v)
                S.dma(vrow[0:1, :], mod_h[s:s + 1, 1024 * k:1024 * (k + 1)],
                      inc=(LD, 16))
                T.wait(LD, LD.v)
                for half in range(2):
                    T.wait(VE, VE.v)
                    T.op(lambda e, half=half: e.matmul(
                        P[6][:, :], ones1[:], vrow[0:1, 512 * half:512 * half + 512],
                        start=True, stop=True), inc=TE)
                    V.wait(TE, TE.v)
                    V.op(lambda e, half=half: e.tensor_copy(
                        dstb[:, 512 * half:512 * half + 512], P[6][:, :]), inc=VE)

            expand(0, bc0)
            expand(1, bc1)
            expand(2, bc2)
            if s == 0 and debug_taps:
                tap(bc0[:]); tap(bc1[:]); tap(bc2[:])

            for t in range(NT):
                par = t % 2
                S.wait(SE, SE.v)
                S.wait(VE, VE.v)
                S.dma(xin[:, par, :], x_h[xrow0 + 128 * t: xrow0 + 128 * (t + 1), :],
                      inc=(LD, 16))
                C.wait(LD, LD.v)
                V.wait(GE, GE.v)
                ln_norm(xin[:, par, :], xnn[:])
                G.wait(VE, VE.v)
                G.op(lambda e: e.tensor_tensor(xn_u[:], xnn[:], bc1[:], ALU.mult))
                G.wait(TE, TE.v)
                G.op(lambda e: e.tensor_tensor(xn_t[:], xn_u[:], bc0[:], ALU.add),
                     inc=GE)
                T.wait(GE, GE.v)
                if s == 0 and t == 0 and debug_taps:
                    tap(xn_t[:])
                transpose_blocks(
                    lambda b: xn_t[:, 128 * b:128 * (b + 1)], DC,
                    lambda b, t=t: h0[:, b, 128 * t:128 * (t + 1)])
                wq_point(s)

            def load_w(w_h, dst):
                G.wait(TE, TE.v)
                G.dma(dst[:], w_h.rearrange("(c p) w -> p c w", p=128),
                      inc=(ST, 16))

            def proj_block(wbuf, nq, dst_col_fn):
                T.wait(SE, SE.v)
                for wave in ([0, 1, 2, 3, 4, 5, 6], [7]):
                    for mc in wave:
                        bank = P[mc % 7]
                        if mc == 7:
                            C.wait(TE, TE.v)
                            for mc2 in [0, 1, 2, 3, 4, 5, 6]:
                                C.op(lambda e, mc2=mc2, nq=nq: e.copy(
                                    dst_col_fn(mc2, nq), P[mc2][:, :]), inc=SE)
                            T.wait(SE, SE.v)
                        for kc in range(DC):
                            T.op(lambda e, kc=kc, mc=mc, nq=nq, bank=bank: e.matmul(
                                bank[:, :], wbuf[:, kc, 128 * mc:128 * (mc + 1)],
                                h0[:, kc, 512 * nq:512 * (nq + 1)],
                                start=(kc == 0), stop=(kc == DC - 1)),
                                inc=(TE if kc == DC - 1 else None))
                C.wait(TE, TE.v)
                C.op(lambda e, nq=nq: e.copy(dst_col_fn(7, nq), P[0][:, :]), inc=SE)

            def proj_block_q(wbuf, nq):
                T.wait(SE, SE.v)
                T.wait(VE, VE.v)
                for mc in range(DC):
                    bank = P[mc % 7]
                    if mc == 7:
                        C.wait(TE, TE.v)
                        for mc2 in range(7):
                            C.op(lambda e, mc2=mc2: e.copy(
                                attw[:, mc2, :], P[mc2][:, :]), inc=SE)
                        T.wait(SE, SE.v)
                    for kc in range(DC):
                        T.op(lambda e, kc=kc, mc=mc, nq=nq, bank=bank: e.matmul(
                            bank[:, :], wbuf[:, kc, 128 * mc:128 * (mc + 1)],
                            h0[:, kc, 512 * nq:512 * (nq + 1)],
                            start=(kc == 0), stop=(kc == DC - 1)),
                            inc=(TE if kc == DC - 1 else None))
                C.wait(TE, TE.v)
                C.op(lambda e: e.copy(attw[:, 7, :], P[0][:, :]), inc=SE)
                V.wait(SE, SE.v)
                V.op(lambda e, nq=nq: e.tensor_copy(
                    qf[:, :, 512 * nq:512 * (nq + 1)], attw[:]), inc=VE)

            if s == 0 and debug_taps:
                tap(h0[:, 0, 0:1024])
            load_w(wk_h, wX)
            if s == 0 and debug_taps:
                tap(wX[:, 0, :])
            T.wait(ST, ST.v)
            for nq in range(S_ // 512):
                proj_block(wX, nq, lambda mc, nq: kf[:, mc, 512 * nq:512 * (nq + 1)])
                wq_point(s)
                wq_point(s)

            if s == 0 and debug_taps:
                tap(kf[:, 0, 0:1024])
            load_w(wv_h, wX)
            T.wait(ST, ST.v)
            for t in range(NT):
                for nd in range(2):
                    bank = P[(2 * t + nd) % 7]
                    if t * 2 + nd >= 7:
                        T.wait(SE, SE.v)
                    for kc in range(DC):
                        T.op(lambda e, kc=kc, t=t, nd=nd, bank=bank: e.matmul(
                            bank[:, :], h0[:, kc, 128 * t:128 * (t + 1)],
                            wX[:, kc, 512 * nd:512 * (nd + 1)],
                            start=(kc == 0), stop=(kc == DC - 1)),
                            inc=(TE if kc == DC - 1 else None))
                    C.wait(TE, TE.v)
                    C.op(lambda e, t=t, nd=nd, bank=bank: e.copy(
                        vt[:, t, 512 * nd:512 * (nd + 1)], bank[:, :]), inc=SE)
                if t % 2 == 1:
                    wq_point(s)
            T.wait(SE, SE.v)

            if s == 0 and debug_taps:
                tap(vt[:, 0, :])
            load_w(wq_h, wX)
            T.wait(ST, ST.v)
            for nq in range(S_ // 512):
                proj_block_q(wX, nq)
                wq_point(s)
                wq_point(s)

            if s == 0 and debug_taps:
                tap(qf[:, 0, 0:1024])
            load_w(wo_h, wX)
            T.wait(ST, ST.v)

            for qc in range(NQC):
                njc = 4 * qc + 4
                for p in range(NPAIR):
                    for j in range(njc):
                        vs = max(0, 128 * j - QCW * qc)
                        diag = (128 * j >= QCW * qc)
                        sA = P[0 + (j % 2)]
                        sB = P[2 + (j % 2)]
                        T.wait(SE, SE.v)
                        q_ap = qf[0:64, p, QCW * qc + vs: QCW * (qc + 1)]
                        qb_ap = qf[64:128, p, QCW * qc + vs: QCW * (qc + 1)]
                        T.op(lambda e, sA=sA, j=j, vs=vs, p=p, q_ap=q_ap, diag=diag:
                             e.matmul(sA[:, vs:QCW], kf[0:64, p, 128 * j:128 * (j + 1)],
                                      q_ap, start=True, stop=not diag,
                                      tile_position=(0, 0)))
                        if diag:
                            T.op(lambda e, sA=sA, vs=vs: e.matmul(
                                sA[:, vs:vs + 128], ident[:], tri[:],
                                start=False, stop=True))
                        T.op(lambda e, sB=sB, j=j, vs=vs, p=p, qb_ap=qb_ap, diag=diag:
                             e.matmul(sB[:, vs:QCW], kf[64:128, p, 128 * j:128 * (j + 1)],
                                      qb_ap, start=True, stop=not diag,
                                      tile_position=(64, 0)),
                             inc=(None if diag else TE))
                        if diag:
                            T.op(lambda e, sB=sB, vs=vs: e.matmul(
                                sB[:, vs:vs + 128], ident[:], tri[:],
                                start=False, stop=True), inc=TE)
                        C.wait(TE, TE.v)
                        C.op(lambda e, sA=sA, vs=vs: e.activation(
                            pa[:, vs:QCW], sA[:, vs:QCW], AF.Exp, scale=0.125,
                            bias=exp_b), inc=SE)
                        C.op(lambda e, sB=sB, vs=vs: e.activation(
                            pb[:, vs:QCW], sB[:, vs:QCW], AF.Exp, scale=0.125,
                            bias=exp_b), inc=SE)
                        T.wait(SE, SE.v)
                        T.op(lambda e, j=j, vs=vs, p=p: e.matmul(
                            P[4][0:64, vs:QCW], vt[:, j, 128 * p:128 * p + 64],
                            pa[:, vs:QCW], start=(j == 0), stop=(j == njc - 1)))
                        T.op(lambda e, j=j, vs=vs, p=p: e.matmul(
                            P[5][64:128, vs:QCW], vt[:, j, 128 * p + 64:128 * (p + 1)],
                            pb[:, vs:QCW], start=(j == 0), stop=(j == njc - 1),
                            tile_position=(0, 64)))
                        T.op(lambda e, j=j, vs=vs: e.matmul(
                            P[6][0:1, vs:QCW], onescol[:], pa[:, vs:QCW],
                            start=(j == 0), stop=(j == njc - 1)))
                        T.op(lambda e, j=j, vs=vs: e.matmul(
                            P[6][32:33, vs:QCW], onescol[:], pb[:, vs:QCW],
                            start=(j == 0), stop=(j == njc - 1),
                            tile_position=(0, 32)), inc=TE)
                    C.wait(TE, TE.v)
                    C.op(lambda e: e.copy(densb[0:1, 0:QCW], P[6][0:1, :]))
                    C.op(lambda e: e.copy(densb[32:33, 0:QCW], P[6][32:33, :]),
                         inc=SE)
                    V.wait(SE, SE.v)
                    V.op(lambda e: e.reciprocal(densb[0:1, 0:QCW], densb[0:1, 0:QCW]))
                    V.op(lambda e: e.reciprocal(densb[32:33, 0:QCW],
                                                densb[32:33, 0:QCW]), inc=VE)
                    T.wait(VE, VE.v)
                    T.op(lambda e: e.matmul(P[6][0:64, :], ones1[0:1, 0:64],
                                            densb[0:1, 0:QCW],
                                            start=True, stop=True))
                    T.op(lambda e: e.matmul(P[6][64:128, :], ones_sb[32:33, :],
                                            densb[32:33, 0:QCW],
                                            start=True, stop=True,
                                            tile_position=(32, 64)), inc=TE)
                    V.wait(TE, TE.v)
                    V.op(lambda e: e.tensor_copy(recb[:], P[6][:, :]))
                    V.op(lambda e, p=p: e.tensor_tensor(
                        attw[0:64, p, :], P[4][0:64, :], recb[0:64, :], ALU.mult))
                    V.op(lambda e, p=p: e.tensor_tensor(
                        attw[64:128, p, :], P[5][64:128, :], recb[64:128, :],
                        ALU.mult), inc=VE)
                    T.wait(VE, VE.v)
                    wq_point(s)

                if qc == NQC - 1:
                    wq_drain(s)
                for qt in range(4):
                    tglob = 4 * qc + qt
                    par = tglob % 2
                    S.wait(VE, VE.v)
                    S.wait(SE, SE.v)
                    S.dma(xin[:, par, :],
                          x_h[xrow0 + 128 * tglob: xrow0 + 128 * (tglob + 1), :],
                          inc=(LD, 16))
                    for nd in range(2):
                        T.wait(VE, VE.v)
                        for c in range(DC):
                            T.op(lambda e, c=c, qt=qt, nd=nd: e.matmul(
                                P[6][:, :], attw[:, c, 128 * qt:128 * (qt + 1)],
                                wX[:, c, 512 * nd:512 * (nd + 1)],
                                start=(c == 0), stop=(c == DC - 1)),
                                inc=(TE if c == DC - 1 else None))
                        V.wait(TE, TE.v)
                        V.wait(LD, LD.v)
                        V.op(lambda e, nd=nd: e.tensor_tensor(
                            ot1[:, :], P[6][:, :], bc2[:, 512 * nd:512 * (nd + 1)],
                            ALU.mult))
                        V.op(lambda e, nd=nd, tglob=tglob, par=par: e.tensor_tensor(
                            x1[:, tglob, 512 * nd:512 * (nd + 1)], ot1[:, :],
                            xin[:, par, 512 * nd:512 * (nd + 1)], ALU.add), inc=VE)

            if s == 0 and debug_taps:
                tap(attw[:, 0, :], 512)
                tap(pa[:], 512)
                tap(recb[:], 512)
                tap(x1[:, 0, :])
                tap(x1[:, NT - 1, :])
            expand(3, bc0)
            expand(4, bc1)
            expand(5, bc2)

            h1_prev_ln = [0]

            def h1_tile(qc2, qt2, dstbuf):
                tglob = 4 * qc2 + qt2
                C.wait(VE, h1_prev_ln[0])
                V.wait(GE, GE.v)
                h1_prev_ln[0] = ln_norm(x1[:, tglob, :], xn2[:])
                G.wait(VE, VE.v)
                G.op(lambda e: e.tensor_tensor(h1g[:], xn2[:], bc1[:], ALU.mult))
                G.wait(SE, SE.v)
                G.op(lambda e: e.tensor_tensor(h1f[:], h1g[:], bc0[:], ALU.add),
                     inc=GE)
                quant(h1f[:], h1qt[:], D, QF)
                T.wait(VE, VE.v)
                transpose_blocks(
                    lambda b: h1qt[:, 128 * b:128 * (b + 1)], DC,
                    lambda b, qt2=qt2: dstbuf[:, b, 128 * qt2:128 * (qt2 + 1)])

            HOIST = False
            CONSV = False
            h1bufs = [h1qf, h1qf2]
            for qt in range(4):
                h1_tile(0, qt, h1bufs[0])
            h1_se = [SE.v]

            gucnt = [0]
            ve_guard = [VE.v] * 7
            se_guard = [SE.v] * 7
            h2f_ve = [VE.v]
            blk_te = [TE.v, TE.v]
            dp_te = [TE.v, TE.v]
            tp_prev = [None]

            def flush_tp():
                if tp_prev[0] is None:
                    return
                pnoff, pnw, pqt, ppar2, q_ve = tp_prev[0]
                tp_prev[0] = None
                T.wait(VE, q_ve)
                transpose_blocks(
                    lambda b, ppar2=ppar2: h2qt[:, ppar2, 128 * b:128 * (b + 1)],
                    pnw // 128,
                    lambda b, pnoff=pnoff, pqt=pqt: h2qi[:, (pnoff // 128) + b,
                                                         128 * pqt:128 * (pqt + 1)])

            for qc in range(NQC):
                hsrc = h1bufs[qc % 2]
                if not HOIST and qc > 0:
                    for qt in range(4):
                        h1_tile(qc, qt, hsrc)
                    h1_se[0] = SE.v
                for nbi, (noff, nw) in enumerate(cfg.NBL):
                    par = nbi % 2
                    S.wait(TE, blk_te[par])
                    S.dma(wgb[:, par, :, 0:nw], wgq_h[:, :, noff:noff + nw],
                          inc=(LD, 16))
                    S.dma(wub[:, par, :, 0:nw], wuq_h[:, :, noff:noff + nw],
                          inc=(LD, 16))
                    for qt in range(4):
                        k = gucnt[0] % 2
                        gucnt[0] += 1
                        gP = P[k]
                        uP = P[2 + k]
                        T.wait(LD, LD.v)
                        T.wait(VE, max(ve_guard[k], ve_guard[2 + k]))
                        T.wait(SE, max(se_guard[k], h1_se[0]))
                        for kc in range(DC):
                            T.op(lambda e, kc=kc, qt=qt, par=par, nw=nw, gP=gP,
                                 hsrc=hsrc: e.matmul(
                                gP[:, 0:nw], hsrc[:, kc, 128 * qt:128 * (qt + 1)],
                                wgb[:, par, kc, 0:nw],
                                start=(kc == 0), stop=(kc == DC - 1)))
                        for kc in range(DC):
                            T.op(lambda e, kc=kc, qt=qt, par=par, nw=nw, uP=uP,
                                 hsrc=hsrc: e.matmul(
                                uP[:, 0:nw], hsrc[:, kc, 128 * qt:128 * (qt + 1)],
                                wub[:, par, kc, 0:nw],
                                start=(kc == 0), stop=(kc == DC - 1)),
                                inc=(TE if kc == DC - 1 else None))
                        C.wait(TE, TE.v)
                        C.wait(VE, h2f_ve[0])
                        C.op(lambda e, gP=gP, nw=nw: e.activation(
                            silu_s[:, 0:nw], gP[:, 0:nw], AF.Silu), inc=SE)
                        se_guard[k] = SE.v
                        flush_tp()
                        V.wait(SE, se_guard[k])
                        V.op(lambda e, uP=uP, nw=nw: e.tensor_tensor(
                            h2f[:, 0:nw], silu_s[:, 0:nw], uP[:, 0:nw], ALU.mult),
                            inc=VE)
                        ve_guard[2 + k] = VE.v
                        h2f_ve[0] = VE.v
                        par2 = gucnt[0] % 2
                        quant(h2f[:, 0:nw], h2qt[:, par2, 0:nw], nw, QF)
                        tp_prev[0] = (noff, nw, qt, par2, VE.v)
                    blk_te[par] = TE.v
                    if HOIST and qc < NQC - 1 and nbi < 4:
                        h1_tile(qc + 1, nbi, h1bufs[(qc + 1) % 2])
                flush_tp()

                for nd in range(2):
                    for icx in range(IC):
                        par = icx % 2
                        S.wait(TE, dp_te[par])
                        S.dma(wdb[:, par, :], wdq_h[:, icx, 512 * nd:512 * (nd + 1)],
                              inc=(LD, 16))
                        T.wait(LD, LD.v)
                        if icx == 0:
                            T.wait(VE, VE.v)
                            T.wait(SE, SE.v)
                        for qt in range(4):
                            T.op(lambda e, icx=icx, qt=qt, par=par: e.matmul(
                                P[qt][:, :], h2qi[:, icx, 128 * qt:128 * (qt + 1)],
                                wdb[:, par, :],
                                start=(icx == 0), stop=(icx == IC - 1)),
                                inc=(TE if qt == 3 else None))
                        dp_te[par] = TE.v
                    V.wait(TE, TE.v)
                    for qt in range(4):
                        tglob = 4 * qc + qt
                        V.wait(ST, ST.v)
                        V.op(lambda e, qt=qt, nd=nd: e.tensor_tensor(
                            o1f[:, :], P[qt][:, :], bc2[:, 512 * nd:512 * (nd + 1)],
                            ALU.mult))
                        V.op(lambda e, qt=qt, nd=nd, tglob=tglob: e.tensor_tensor(
                            x3s[:, qt, 512 * nd:512 * (nd + 1)], o1f[:, :],
                            x1[:, tglob, 512 * nd:512 * (nd + 1)], ALU.add),
                            inc=VE)
                        ve_guard[qt] = VE.v
                        if nd == 1:
                            G.wait(VE, VE.v)
                            G.dma(y_h[xrow0 + 128 * tglob: xrow0 + 128 * (tglob + 1), :],
                                  x3s[:, qt, :], inc=(ST, 16))

        G.wait(ST, ST.v)

    with nc.Block() as blk:
        @blk.sync
        def _(eng):
            pr.begin_pass("sync", eng, allsems)
            script(pr)

        @blk.gpsimd
        def _(eng):
            pr.begin_pass("gpsimd", eng, allsems)
            script(pr)

        @blk.tensor
        def _(eng):
            pr.begin_pass("tensor", eng, allsems)
            script(pr)

        @blk.vector
        def _(eng):
            pr.begin_pass("vector", eng, allsems)
            script(pr)

        @blk.scalar
        def _(eng):
            pr.begin_pass("scalar", eng, allsems)
            script(pr)

    for c in semctx:
        c.__exit__(None, None, None)
    return nc


def _consts():
    ident = np.eye(128, dtype=ml_dtypes.bfloat16)
    tri = np.zeros((128, 128), np.float32)
    r = np.arange(128)
    tri[r[:, None] > r[None, :]] = NEG_BIG
    tri = tri.astype(ml_dtypes.bfloat16)
    indik = np.zeros((2, 128), np.float32)
    indik[0, 0:64] = 1.0
    indik[1, 64:128] = 1.0
    ones1 = np.ones((1, 128), np.float32)
    return ident, tri, indik, ones1


def make_in_maps(cfg, x, wq, wk, wv, wo, wg, wu, wd, c, wmod, n_cores=8):
    ident, tri, indik, ones1 = _consts()
    shared = dict(
        wq=np.ascontiguousarray(wq, dtype=np.float32),
        wk=np.ascontiguousarray(wk, dtype=np.float32),
        wv=np.ascontiguousarray(wv, dtype=np.float32),
        wo=np.ascontiguousarray(wo, dtype=np.float32),
        wg=np.ascontiguousarray(wg, dtype=np.float32),
        wu=np.ascontiguousarray(wu, dtype=np.float32),
        wd=np.ascontiguousarray(wd, dtype=np.float32),
        wmod=np.ascontiguousarray(wmod, dtype=np.float32),
        ident=ident, tri=tri, indik=indik, ones1=ones1,
    )
    c = np.asarray(c, np.float32)
    x = np.asarray(x, np.float32)
    in_maps = []
    for i in range(n_cores):
        ct = np.ascontiguousarray(
            c[cfg.NSEQ * i:cfg.NSEQ * (i + 1)].T.reshape(DC, 128, cfg.NSEQ)
            .transpose(1, 0, 2))
        xi = np.ascontiguousarray(
            x[cfg.NSEQ * i:cfg.NSEQ * (i + 1)].reshape(cfg.NSEQ * cfg.S, D))
        in_maps.append(dict(shared, x=xi, ct=ct))
    return in_maps


def kernel(L0_attn_x, L0_attn_W_Q, L0_attn_W_K, L0_attn_W_V, L0_attn_W_o,
           L0_ffn_w_gate, L0_ffn_w_up, L0_ffn_w_down, L0_c, L0_wmod):
    cfg = Cfg(S=2048, I=2816)
    nc = build_program(cfg)
    in_maps = make_in_maps(cfg, L0_attn_x, L0_attn_W_Q, L0_attn_W_K, L0_attn_W_V,
                           L0_attn_W_o, L0_ffn_w_gate, L0_ffn_w_up, L0_ffn_w_down,
                           L0_c, L0_wmod)
    res = run_bass_kernel_spmd(nc, in_maps, core_ids=list(range(8)))
    out = np.empty((16, 2048, 1024), np.float32)
    for i in range(8):
        out[2 * i:2 * i + 2] = res.results[i]["y"].reshape(2, cfg.S, D)
    return out


# version 15
# speedup vs baseline: 1.3878x; 1.1234x over previous
import numpy as np
import ml_dtypes

import concourse.bass as bass
import concourse.mybir as mybir
from concourse.bass_utils import run_bass_kernel_spmd

F32 = mybir.dt.float32
BF16 = mybir.dt.bfloat16
I32 = mybir.dt.int32
AF = mybir.ActivationFunctionType
ALU = mybir.AluOpType
AX = mybir.AxisListType

AND_M = int(np.uint32(0xFFC00000).view(np.int32))
OR_M = 0x00200000
EXP_M = 0x7F800000
NEG_BIG = -1.0e9

D = 1024
DC = 8
NPAIR = 8
QCW = 512
EPS = 1e-5


class Cfg:
    def __init__(self, S=2048, I=2816):
        self.S = S
        self.NSEQ = 2
        self.I = I
        self.IC = I // 128
        self.NQC = S // QCW
        self.NT = S // 128
        self.NBL = []
        off = 0
        while off < I:
            w = min(512, I - off)
            self.NBL.append((off, w))
            off += w


class Sem:
    def __init__(self, handle):
        self.h = handle
        self.v = 0


class Eng:
    def __init__(self, prog, name):
        self.prog = prog
        self.name = name

    @property
    def real(self):
        return self.prog.active_eng if self.prog.active == self.name else None

    def _fin(self, ins, inc):
        if inc is not None:
            sem, n = (inc, 1) if isinstance(inc, Sem) else inc
            sem.v += n
            if ins is not None:
                ins.then_inc(sem.h, n)
        return ins

    def wait(self, sem, val=None):
        if val is None:
            val = sem.v
        if val <= 0:
            return
        e = self.real
        if e is not None:
            e.wait_ge(sem.h, val)

    def op(self, fn, inc=None):
        e = self.real
        ins = fn(e) if e is not None else None
        return self._fin(ins, inc)

    def dma(self, out, in_, inc=None):
        return self.op(lambda e: e.dma_start(out=out, in_=in_), inc)


class Prog:
    def __init__(self, nc):
        self.nc = nc
        self.active = None
        self.active_eng = None
        self.S = Eng(self, "sync")
        self.G = Eng(self, "gpsimd")
        self.T = Eng(self, "tensor")
        self.V = Eng(self, "vector")
        self.C = Eng(self, "scalar")

    def begin_pass(self, name, eng, sems):
        self.active = name
        self.active_eng = eng
        for s in sems:
            s.v = 0


def build_program(cfg: Cfg, debug_stop=None, debug_taps=False):
    nc = bass.Bass("TRN2", target_bir_lowering=False, detect_race_conditions=False)
    S_, I_, IC, NQC, NT, NSEQ = cfg.S, cfg.I, cfg.IC, cfg.NQC, cfg.NT, cfg.NSEQ

    dt = nc.dram_tensor
    x_h = dt("x", [NSEQ * S_, D], F32, kind="ExternalInput").ap()
    wq_h = dt("wq", [D, D], F32, kind="ExternalInput").ap()
    wk_h = dt("wk", [D, D], F32, kind="ExternalInput").ap()
    wv_h = dt("wv", [D, D], F32, kind="ExternalInput").ap()
    wo_h = dt("wo", [D, D], F32, kind="ExternalInput").ap()
    wg_h = dt("wg", [I_, D], F32, kind="ExternalInput").ap()
    wu_h = dt("wu", [I_, D], F32, kind="ExternalInput").ap()
    wd_h = dt("wd", [D, I_], F32, kind="ExternalInput").ap()
    ct_h = dt("ct", [128, DC, NSEQ], F32, kind="ExternalInput").ap()
    wmod_h = dt("wmod", [D, 6144], F32, kind="ExternalInput").ap()
    ident_h = dt("ident", [128, 128], BF16, kind="ExternalInput").ap()
    tri_h = dt("tri", [128, 128], BF16, kind="ExternalInput").ap()
    indik_h = dt("indik", [2, 128], F32, kind="ExternalInput").ap()
    ones1_h = dt("ones1", [1, 128], F32, kind="ExternalInput").ap()
    y_h = dt("y", [NSEQ * S_, D], F32, kind="ExternalOutput").ap()
    dbg_h = dt("dbg", [128, 16, 1024], F32, kind="ExternalOutput").ap() if debug_taps else None
    wgq_h = dt("wgq_s", [128, DC, I_], BF16, kind="Internal").ap()
    wuq_h = dt("wuq_s", [128, DC, I_], BF16, kind="Internal").ap()
    wdq_h = dt("wdq_s", [128, IC, D], BF16, kind="Internal").ap()
    mod_h = dt("mod_s", [NSEQ, 6144], F32, kind="Internal").ap()

    SBUF_LIMIT = nc.SBUF_PARTITION_SIZE_BYTES
    cur = [17 * 1024]

    def at(name, shape, dtp, align=32):
        nbytes = int(np.prod(shape[1:])) * mybir.dt.size(dtp)
        cur[0] = (cur[0] + align - 1) // align * align
        t = nc.alloc_sbuf_tensor_at(name, shape, dtp, offset=cur[0])
        cur[0] += nbytes
        return t

    ident = at("ident", [128, 128], BF16)
    tri = at("tri", [128, 128], BF16)
    indik = at("indik", [2, 128], F32)
    ones1 = at("ones1", [1, 128], F32)
    ct = at("ct", [128, DC, NSEQ], F32)
    vrow = at("vrow", [NSEQ, 1024], F32)
    bc0 = at("bc0", [128, D], F32)
    bc1 = at("bc1", [128, D], F32)
    bc2 = at("bc2", [128, D], F32)
    ln_s = at("ln_s", [128, 10], F32)
    onescol = at("onescol", [128, 1], BF16)
    ones_sb = at("ones_sb", [128, 64], F32)
    act_scr = at("act_scr", [128, D], F32)
    xin = at("xin", [128, 2, D], F32)
    xn_t = at("xn_t", [128, D], BF16)
    x1_off = (cur[0] + 31) // 32 * 32
    x1 = at("x1", [128, NT, D], BF16)
    shared_end = cur[0]

    ov0 = (shared_end + 31) // 32 * 32

    cur[0] = ov0
    stW = at("stW", [128, 2, 3072], F32)
    assert cur[0] <= SBUF_LIMIT, f"phase-W overlay over budget: {cur[0]}"

    cur[0] = ov0
    h0 = at("h0", [128, DC, S_], BF16)
    qf = nc.alloc_sbuf_tensor_at("qf", [128, DC, S_], BF16, offset=ov0)
    wX = at("wX", [128, DC, D], BF16)
    kf = at("kf", [128, DC, S_], BF16)
    vt = at("vt", [128, NT, 16 * 64], BF16)
    attw_off = (cur[0] + 31) // 32 * 32
    attw = at("attw", [128, DC, QCW], BF16)
    pa = at("pa", [128, 2, QCW], BF16)
    pb = at("pb", [128, 2, QCW], BF16)
    densb = at("densb", [128, QCW], F32)
    recb = at("recb", [128, QCW], F32)
    ot1 = recb
    xn_u = nc.alloc_sbuf_tensor_at("xn_u", [128, D], BF16, offset=attw_off)
    xnn = nc.alloc_sbuf_tensor_at("xnn", [128, D], F32, offset=attw_off + 2048)
    att_end = cur[0]
    assert cur[0] <= SBUF_LIMIT, f"phase-ATT overlay over budget: {cur[0]}"

    cur[0] = ov0
    qU = at("qU", [128, 1024], BF16)
    qX = at("qX", [128, 1024], BF16)
    qM1 = at("qM1", [128, 1024], BF16)
    qSG = at("qSG", [128, 1024], BF16)
    qsm = at("qsm", [128, 3, 64], F32)
    qsb = at("qsb", [128, 64], BF16)
    xn2 = at("xn2", [128, D], F32)
    h1g = at("h1g", [128, D], F32)
    h1f = at("h1f", [128, D], F32)
    h1qt = at("h1qt", [128, D], BF16)
    h1qf = at("h1qf", [128, DC, QCW], BF16)
    wgb = at("wgb", [128, 2, DC, 512], BF16)
    wub = at("wub", [128, 2, DC, 512], BF16)
    wdb = at("wdb", [128, 2, 512], BF16)
    silu_s = at("silu_s", [128, 512], F32)
    h2f = at("h2f", [128, 512], F32)
    h2qt = at("h2qt", [128, 2, 512], BF16)
    h2qi = at("h2qi", [128, IC, QCW], BF16)
    o1f = at("o1f", [128, 512], F32)
    x3s = at("x3s", [128, 4, D], F32)
    ffn_end = cur[0]
    assert cur[0] <= SBUF_LIMIT, f"phase-FFN overlay over budget: {cur[0]}"

    cur[0] = max(att_end, ffn_end)
    wreg0 = (cur[0] + 31) // 32 * 32
    stW2 = at("stW2", [128, 2, 1024], F32)
    h1qf2 = nc.alloc_sbuf_tensor_at("h1qf2", [128, DC, QCW], BF16, offset=wreg0)
    qbf2 = at("qbf2", [128, 2, 1024], BF16)
    tstage2 = at("tstage2", [128, 1024], BF16)
    qsm_w = at("qsm_w", [128, 3, 64], F32)
    qsb_w = at("qsb_w", [128, 64], BF16)
    assert cur[0] <= SBUF_LIMIT, f"W-quant region over budget: {cur[0]}"
    qU_w = nc.alloc_sbuf_tensor_at("qU_w", [128, 1024], BF16, offset=x1_off + 24576)
    qX_w = nc.alloc_sbuf_tensor_at("qX_w", [128, 1024], BF16, offset=x1_off + 26624)
    qM1_w = nc.alloc_sbuf_tensor_at("qM1_w", [128, 1024], BF16, offset=x1_off + 28672)
    qSG_w = nc.alloc_sbuf_tensor_at("qSG_w", [128, 1024], BF16, offset=x1_off + 30720)

    P = [nc.alloc_psum_tensor(f"P{k}", [128, 512], F32) for k in range(7)]
    T1 = nc.alloc_psum_tensor("T1", [128, 512], BF16)
    Ttile = [T1[:, 0:128], T1[:, 128:256]]

    semctx = [nc.semaphore(nm) for nm in ["LD", "ST", "TE", "VE", "SE", "GE"]]
    handles = [c.__enter__() for c in semctx]
    LD, ST, TE, VE, SE, GE = [Sem(h) for h in handles]
    allsems = [LD, ST, TE, VE, SE, GE]
    pr = Prog(nc)

    (ln_sx, ln_m2, ln_nm1, ln_nvar, ln_is, ln_std, eps_b, exp_b, zcol,
     jcol) = (ln_s[:, k:k + 1] for k in range(10))

    def script(pr: Prog):
        S, G, T, V, C = pr.S, pr.G, pr.T, pr.V, pr.C
        dbg_slot = [0]

        def tap(ap, cols=1024):
            if not debug_taps:
                return
            k = dbg_slot[0]
            dbg_slot[0] += 1
            G.wait(VE, VE.v)
            G.wait(SE, SE.v)
            G.wait(TE, TE.v)
            G.wait(GE, GE.v)
            G.dma(dbg_h[:, k, 0:cols], ap, inc=(ST, 16))

        def v3(ap2, w):
            return ap2.rearrange("p (nb b) -> p nb b", b=16)

        def quant(src, dst, w, Q, te_guard=None):
            (U, A_, B_, SG, SM, SB) = Q
            nb = w // 16
            C.wait(GE, GE.v)
            C.wait(VE, VE.v)
            C.op(lambda e: e.activation(SG[:, 0:w], src, AF.Sign), inc=SE)
            se_sgn = SE.v
            m_, s_, r_ = (SM[:, k, 0:nb] for k in range(3))
            sb_ = SB[:, 0:nb]
            V.wait(GE, GE.v)
            hv = nb // 2
            halves = [(0, hv), (hv, nb)] if (2 <= hv < 128) else [(0, nb)]
            srcv = v3(src, w)

            def forall(fn):
                for (a, b) in halves:
                    V.op(lambda e, a=a, b=b: fn(e, slice(a, b)))

            forall(lambda e, sl: e.tensor_reduce(
                m_[:, sl], srcv[:, sl, :], AX.X, ALU.max,
                apply_absolute_value=True))
            forall(lambda e, sl: e.tensor_scalar(s_[:, sl], m_[:, sl], 1.0 / 6.0,
                                                 1e-12, ALU.mult, ALU.max))
            forall(lambda e, sl: e.reciprocal(r_[:, sl], s_[:, sl]))
            forall(lambda e, sl: e.tensor_copy(sb_[:, sl], s_[:, sl]))
            rb = r_.unsqueeze(2).broadcast_to([128, nb, 16])
            sbb = sb_.unsqueeze(2).broadcast_to([128, nb, 16])
            u = U[:, 0:w]
            u3 = v3(u, w)
            V.op(lambda e: e.tensor_tensor(u3, srcv, rb, ALU.mult))
            V.wait(SE, se_sgn)
            V.op(lambda e: e.tensor_tensor(u, u, SG[:, 0:w], ALU.mult))
            Aw, Bw = A_[:, 0:w], B_[:, 0:w]
            V.op(lambda e: e.tensor_scalar(Aw, u, 0.25, None, ALU.is_gt))
            V.op(lambda e: e.scalar_tensor_tensor(Aw, u, 0.75, Aw, ALU.is_gt, ALU.add))
            V.op(lambda e: e.scalar_tensor_tensor(Aw, u, 1.25, Aw, ALU.is_gt, ALU.add))
            V.op(lambda e: e.scalar_tensor_tensor(Aw, u, 1.75, Aw, ALU.is_gt, ALU.add))
            V.op(lambda e: e.tensor_scalar(Bw, u, 5.0, 2.0, ALU.is_gt, ALU.mult))
            V.op(lambda e: e.scalar_tensor_tensor(Bw, u, 2.5, Bw, ALU.is_gt, ALU.add))
            V.op(lambda e: e.scalar_tensor_tensor(Bw, u, 3.5, Bw, ALU.is_gt, ALU.add))
            V.op(lambda e: e.scalar_tensor_tensor(Aw, Aw, 0.5, Bw, ALU.mult, ALU.add))
            V.op(lambda e: e.tensor_tensor(Aw, Aw, SG[:, 0:w], ALU.mult))
            if te_guard is not None:
                V.wait(TE, te_guard)
            V.op(lambda e: e.tensor_tensor(v3(dst, w), v3(Aw, w), sbb, ALU.mult),
                 inc=VE)

        QW = (qU_w, qX_w, qM1_w, qSG_w, qsm_w, qsb_w)
        QF = (qU, qX, qM1, qSG, qsm, qsb)

        def transpose_blocks(src_fn, nblk, dst_fn):
            for b in range(nblk):
                par = b % 2
                T.wait(SE, SE.v)
                T.op(lambda e, b=b, par=par: e.transpose(Ttile[par], src_fn(b), ident[:]),
                     inc=TE)
                C.wait(TE, TE.v)
                C.op(lambda e, b=b, par=par: e.copy(dst_fn(b), Ttile[par]), inc=SE)

        def ln_norm(x_ap, out_ap):
            C.op(lambda e: e.activation(act_scr[:], x_ap, AF.Copy, accum_out=ln_sx))
            C.op(lambda e: e.activation(act_scr[:], x_ap, AF.Square,
                                        scale=1.0 / 32.0, bias=zcol,
                                        accum_out=ln_m2))
            C.op(lambda e: e.activation(ln_nm1, ln_sx, AF.Identity,
                                        scale=-1.0 / D, bias=zcol), inc=SE)
            V.wait(SE, SE.v)
            V.op(lambda e: e.scalar_tensor_tensor(ln_nvar, ln_nm1, ln_nm1, ln_m2,
                                                  ALU.mult, ALU.subtract), inc=VE)
            C.wait(VE, VE.v)
            C.op(lambda e: e.activation(ln_std, ln_nvar, AF.Sqrt, scale=-1.0,
                                        bias=eps_b), inc=SE)
            V.wait(SE, SE.v)
            V.op(lambda e: e.reciprocal(ln_is, ln_std))
            V.op(lambda e: e.memset(jcol, 0.0))
            V.op(lambda e: e.tensor_scalar(out_ap, x_ap, ln_nm1, ln_is,
                                           ALU.add, ALU.mult), inc=VE)
            return VE.v

        for dst, src in [(ident[:], ident_h), (tri[:], tri_h), (indik[:], indik_h),
                         (ones1[:], ones1_h), (ct[:], ct_h)]:
            S.dma(dst, src, inc=(LD, 16))
        V.op(lambda e: e.memset(eps_b, EPS))
        V.op(lambda e: e.memset(zcol, 0.0))
        V.op(lambda e: e.memset(jcol, 0.0))
        V.op(lambda e: e.memset(onescol[:], 1.0))
        V.op(lambda e: e.memset(ones_sb[:], 1.0))
        V.op(lambda e: e.memset(exp_b, -6.0), inc=VE)
        C.wait(VE, VE.v)
        C.wait(LD, LD.v)
        ctf = ct[:].rearrange("p c n -> p (c n)")
        C.op(lambda e: e.activation(ctf, ctf, AF.Silu), inc=SE)

        for c in range(DC):
            if c >= 1:
                S.wait(TE, TE.v)
            S.dma(stW[:, 0, :], wmod_h[128 * c:128 * (c + 1), 0:3072], inc=(LD, 16))
            S.dma(stW[:, 1, :], wmod_h[128 * c:128 * (c + 1), 3072:6144], inc=(LD, 16))
            T.wait(LD, LD.v)
            T.wait(SE, SE.v)
            for h in range(2):
                for n in range(6):
                    col = 3072 * h + 512 * n
                    tp = (0, 0) if h == 0 else (0, 32)
                    rows = slice(0, NSEQ) if h == 0 else slice(32, 32 + NSEQ)
                    last = (h == 1 and n == 5)
                    T.op(lambda e, col=col, tp=tp, rows=rows, c=c: e.matmul(
                        P[n][rows, :], ct[:, c, :],
                        stW[:, col // 3072, (col % 3072):(col % 3072) + 512],
                        start=(c == 0), stop=(c == DC - 1), tile_position=tp),
                        inc=(TE if last else None))
        V.wait(TE, TE.v)
        for k in range(6):
            h = k // 3
            n0 = (2 * k) % 6
            rows = slice(0, NSEQ) if h == 0 else slice(32, 32 + NSEQ)
            r0 = 0 if h == 0 else 32
            buf = act_scr[r0:r0 + NSEQ, :]
            tbf = xin[r0:r0 + NSEQ, 0, 0:1024]
            if k >= 1:
                V.wait(ST, ST.v)
            V.op(lambda e, n0=n0, rows=rows, buf=buf: e.tensor_copy(
                buf[:, 0:512], P[n0][rows, :]))
            V.op(lambda e, n0=n0, rows=rows, buf=buf: e.tensor_copy(
                buf[:, 512:1024], P[n0 + 1][rows, :]))
            if k in (1, 4):
                V.op(lambda e, buf=buf: e.tensor_scalar(buf, buf, 1.0, None, ALU.add))
            if k in (2, 5):
                V.op(lambda e, buf=buf, tbf=tbf: e.tensor_scalar(
                    tbf, buf, 0.0, 2.0, ALU.is_ge, ALU.mult))
                V.op(lambda e, tbf=tbf: e.tensor_scalar(tbf, tbf, -1.0, None, ALU.add))
                V.op(lambda e, buf=buf, tbf=tbf: e.tensor_tensor(buf, buf, tbf,
                                                                 ALU.mult))
                V.op(lambda e, buf=buf: e.tensor_scalar(buf, buf, 0.25, None, ALU.max))
                V.op(lambda e, buf=buf, tbf=tbf: e.tensor_tensor(buf, buf, tbf, ALU.mult))
            V.op(lambda e, buf=buf: e.tensor_scalar(buf, buf, 0.0, None, ALU.add),
                 inc=VE)
            G.wait(VE, VE.v)
            G.dma(mod_h[:, 1024 * k:1024 * (k + 1)], buf, inc=(ST, 16))

        if debug_stop == "mod":
            G.wait(ST, ST.v)
            return
        wchunks = []
        for icx in range(cfg.IC):
            wchunks.append((wg_h, 128 * icx, 0, D, wgq_h))
        for icx in range(cfg.IC):
            wchunks.append((wu_h, 128 * icx, 0, D, wuq_h))
        for hc in range(DC):
            for coff, cwid in ((0, 1024), (1024, 1024), (2048, 768)):
                wchunks.append((wd_h, 128 * hc, coff, cwid, wdq_h))
        wstate = {"li": 0, "ti": 0, "te_qbf": [0, 0]}

        def w_load_quant():
            i = wstate["li"]
            if i >= len(wchunks):
                return
            wstate["li"] += 1
            src_h, r0, c0, cw, dst_h = wchunks[i]
            par = i % 2
            S.wait(VE, VE.v)
            S.wait(SE, SE.v)
            S.dma(stW2[:, par, 0:cw], src_h[r0:r0 + 128, c0:c0 + cw],
                  inc=(LD, 16))
            V.wait(LD, LD.v)
            C.wait(LD, LD.v)
            quant(stW2[:, par, 0:cw], qbf2[:, par, 0:cw], cw, QW,
                  te_guard=wstate["te_qbf"][par])

        def w_transpose_store():
            j = wstate["ti"]
            if j >= wstate["li"]:
                return
            wstate["ti"] += 1
            src_h, r0, c0, cw, dst_h = wchunks[j]
            par = j % 2
            nblk = cw // 128
            T.wait(VE, VE.v)
            T.wait(ST, ST.v)
            transpose_blocks(
                lambda b, par=par: qbf2[:, par, 128 * b:128 * (b + 1)], nblk,
                lambda b: tstage2[:, 128 * b:128 * (b + 1)])
            wstate["te_qbf"][par] = TE.v
            G.wait(SE, SE.v)
            if dst_h is wdq_h:
                dst_ap = dst_h[:, (c0 // 128):(c0 // 128) + nblk, r0:r0 + 128]
            else:
                dst_ap = dst_h[:, :, r0:r0 + 128]
            G.dma(dst_ap,
                  tstage2[:, 0:cw].rearrange("p (c w) -> p c w", w=128),
                  inc=(ST, 16))

        def wq_point(s):
            if s != 0:
                return
            w_transpose_store()
            w_load_quant()

        def wq_drain(s):
            if s != 0:
                return
            while wstate["ti"] < len(wchunks):
                w_transpose_store()
                w_load_quant()

        for s in range(NSEQ):
            xrow0 = s * S_

            def expand(k, dstb):
                S.wait(TE, TE.v)
                S.wait(ST, ST.v)
                S.dma(vrow[0:1, :], mod_h[s:s + 1, 1024 * k:1024 * (k + 1)],
                      inc=(LD, 16))
                T.wait(LD, LD.v)
                for half in range(2):
                    T.wait(VE, VE.v)
                    T.op(lambda e, half=half: e.matmul(
                        P[6][:, :], ones1[:], vrow[0:1, 512 * half:512 * half + 512],
                        start=True, stop=True), inc=TE)
                    V.wait(TE, TE.v)
                    V.op(lambda e, half=half: e.tensor_copy(
                        dstb[:, 512 * half:512 * half + 512], P[6][:, :]), inc=VE)

            expand(0, bc0)
            expand(1, bc1)
            expand(2, bc2)
            if s == 0 and debug_taps:
                tap(bc0[:]); tap(bc1[:]); tap(bc2[:])

            for t in range(NT):
                par = t % 2
                S.wait(SE, SE.v)
                S.wait(VE, VE.v)
                S.dma(xin[:, par, :], x_h[xrow0 + 128 * t: xrow0 + 128 * (t + 1), :],
                      inc=(LD, 16))
                C.wait(LD, LD.v)
                V.wait(GE, GE.v)
                ln_norm(xin[:, par, :], xnn[:])
                G.wait(VE, VE.v)
                G.op(lambda e: e.tensor_tensor(xn_u[:], xnn[:], bc1[:], ALU.mult))
                G.wait(TE, TE.v)
                G.op(lambda e: e.tensor_tensor(xn_t[:], xn_u[:], bc0[:], ALU.add),
                     inc=GE)
                T.wait(GE, GE.v)
                if s == 0 and t == 0 and debug_taps:
                    tap(xn_t[:])
                transpose_blocks(
                    lambda b: xn_t[:, 128 * b:128 * (b + 1)], DC,
                    lambda b, t=t: h0[:, b, 128 * t:128 * (t + 1)])
                wq_point(s)

            def load_w(w_h, dst):
                G.wait(TE, TE.v)
                G.dma(dst[:], w_h.rearrange("(c p) w -> p c w", p=128),
                      inc=(ST, 16))

            def proj_block(wbuf, nq, dst_col_fn):
                T.wait(SE, SE.v)
                for wave in ([0, 1, 2, 3, 4, 5, 6], [7]):
                    for mc in wave:
                        bank = P[mc % 7]
                        if mc == 7:
                            C.wait(TE, TE.v)
                            for mc2 in [0, 1, 2, 3, 4, 5, 6]:
                                C.op(lambda e, mc2=mc2, nq=nq: e.copy(
                                    dst_col_fn(mc2, nq), P[mc2][:, :]), inc=SE)
                            T.wait(SE, SE.v)
                        for kc in range(DC):
                            T.op(lambda e, kc=kc, mc=mc, nq=nq, bank=bank: e.matmul(
                                bank[:, :], wbuf[:, kc, 128 * mc:128 * (mc + 1)],
                                h0[:, kc, 512 * nq:512 * (nq + 1)],
                                start=(kc == 0), stop=(kc == DC - 1)),
                                inc=(TE if kc == DC - 1 else None))
                C.wait(TE, TE.v)
                C.op(lambda e, nq=nq: e.copy(dst_col_fn(7, nq), P[0][:, :]), inc=SE)

            def proj_block_q(wbuf, nq):
                T.wait(SE, SE.v)
                T.wait(VE, VE.v)
                for mc in range(DC):
                    bank = P[mc % 7]
                    if mc == 7:
                        C.wait(TE, TE.v)
                        for mc2 in range(7):
                            C.op(lambda e, mc2=mc2: e.copy(
                                attw[:, mc2, :], P[mc2][:, :]), inc=SE)
                        T.wait(SE, SE.v)
                    for kc in range(DC):
                        T.op(lambda e, kc=kc, mc=mc, nq=nq, bank=bank: e.matmul(
                            bank[:, :], wbuf[:, kc, 128 * mc:128 * (mc + 1)],
                            h0[:, kc, 512 * nq:512 * (nq + 1)],
                            start=(kc == 0), stop=(kc == DC - 1)),
                            inc=(TE if kc == DC - 1 else None))
                C.wait(TE, TE.v)
                C.op(lambda e: e.copy(attw[:, 7, :], P[0][:, :]), inc=SE)
                V.wait(SE, SE.v)
                V.op(lambda e, nq=nq: e.tensor_copy(
                    qf[:, :, 512 * nq:512 * (nq + 1)], attw[:]), inc=VE)

            if s == 0 and debug_taps:
                tap(h0[:, 0, 0:1024])
            load_w(wk_h, wX)
            if s == 0 and debug_taps:
                tap(wX[:, 0, :])
            T.wait(ST, ST.v)
            for nq in range(S_ // 512):
                proj_block(wX, nq, lambda mc, nq: kf[:, mc, 512 * nq:512 * (nq + 1)])
                wq_point(s)
                wq_point(s)

            if s == 0 and debug_taps:
                tap(kf[:, 0, 0:1024])
            load_w(wv_h, wX)
            T.wait(ST, ST.v)
            for t in range(NT):
                for nd in range(2):
                    bank = P[(2 * t + nd) % 7]
                    if t * 2 + nd >= 7:
                        T.wait(SE, SE.v)
                    for kc in range(DC):
                        T.op(lambda e, kc=kc, t=t, nd=nd, bank=bank: e.matmul(
                            bank[:, :], h0[:, kc, 128 * t:128 * (t + 1)],
                            wX[:, kc, 512 * nd:512 * (nd + 1)],
                            start=(kc == 0), stop=(kc == DC - 1)),
                            inc=(TE if kc == DC - 1 else None))
                    C.wait(TE, TE.v)
                    C.op(lambda e, t=t, nd=nd, bank=bank: e.copy(
                        vt[:, t, 512 * nd:512 * (nd + 1)], bank[:, :]), inc=SE)
                if t % 2 == 1:
                    wq_point(s)
            T.wait(SE, SE.v)

            if s == 0 and debug_taps:
                tap(vt[:, 0, :])
            load_w(wq_h, wX)
            T.wait(ST, ST.v)
            for nq in range(S_ // 512):
                proj_block_q(wX, nq)
                wq_point(s)
                wq_point(s)

            if s == 0 and debug_taps:
                tap(qf[:, 0, 0:1024])
            load_w(wo_h, wX)
            T.wait(ST, ST.v)

            att_se = [SE.v, SE.v]
            att_pv_te = [TE.v, TE.v]
            for qc in range(NQC):
                njc = 4 * qc + 4
                for p in range(NPAIR):
                    exp_se = {}
                    sc_te = {}

                    def scores(j):
                        par = j % 2
                        vs = max(0, 128 * j - QCW * qc)
                        diag = (128 * j >= QCW * qc)
                        sA = P[0 + par]
                        sB = P[2 + par]
                        T.wait(SE, att_se[par])
                        q_ap = qf[0:64, p, QCW * qc + vs: QCW * (qc + 1)]
                        qb_ap = qf[64:128, p, QCW * qc + vs: QCW * (qc + 1)]
                        T.op(lambda e, sA=sA, j=j, vs=vs, p=p, q_ap=q_ap:
                             e.matmul(sA[:, vs:QCW], kf[0:64, p, 128 * j:128 * (j + 1)],
                                      q_ap, start=True, stop=not diag,
                                      tile_position=(0, 0)))
                        if diag:
                            T.op(lambda e, sA=sA, vs=vs: e.matmul(
                                sA[:, vs:vs + 128], ident[:], tri[:],
                                start=False, stop=True))
                        T.op(lambda e, sB=sB, j=j, vs=vs, p=p, qb_ap=qb_ap:
                             e.matmul(sB[:, vs:QCW], kf[64:128, p, 128 * j:128 * (j + 1)],
                                      qb_ap, start=True, stop=not diag,
                                      tile_position=(64, 0)),
                             inc=(None if diag else TE))
                        if diag:
                            T.op(lambda e, sB=sB, vs=vs: e.matmul(
                                sB[:, vs:vs + 128], ident[:], tri[:],
                                start=False, stop=True), inc=TE)
                        sc_te[j] = TE.v

                    def expj(j):
                        par = j % 2
                        vs = max(0, 128 * j - QCW * qc)
                        sA = P[0 + par]
                        sB = P[2 + par]
                        C.wait(TE, max(sc_te[j], att_pv_te[par]))
                        C.op(lambda e, sA=sA, vs=vs, par=par: e.activation(
                            pa[:, par, vs:QCW], sA[:, vs:QCW], AF.Exp, scale=0.125,
                            bias=exp_b), inc=SE)
                        C.op(lambda e, sB=sB, vs=vs, par=par: e.activation(
                            pb[:, par, vs:QCW], sB[:, vs:QCW], AF.Exp, scale=0.125,
                            bias=exp_b), inc=SE)
                        exp_se[j] = SE.v
                        att_se[par] = SE.v

                    def pv(j):
                        par = j % 2
                        vs = max(0, 128 * j - QCW * qc)
                        T.wait(SE, exp_se[j])
                        T.op(lambda e, j=j, vs=vs, p=p, par=par: e.matmul(
                            P[4][0:64, vs:QCW], vt[:, j, 128 * p:128 * p + 64],
                            pa[:, par, vs:QCW], start=(j == 0), stop=(j == njc - 1)))
                        T.op(lambda e, j=j, vs=vs, p=p, par=par: e.matmul(
                            P[5][64:128, vs:QCW], vt[:, j, 128 * p + 64:128 * (p + 1)],
                            pb[:, par, vs:QCW], start=(j == 0), stop=(j == njc - 1),
                            tile_position=(0, 64)))
                        T.op(lambda e, j=j, vs=vs, par=par: e.matmul(
                            P[6][0:1, vs:QCW], onescol[:], pa[:, par, vs:QCW],
                            start=(j == 0), stop=(j == njc - 1)))
                        T.op(lambda e, j=j, vs=vs, par=par: e.matmul(
                            P[6][32:33, vs:QCW], onescol[:], pb[:, par, vs:QCW],
                            start=(j == 0), stop=(j == njc - 1),
                            tile_position=(0, 32)), inc=TE)
                        att_pv_te[par] = TE.v

                    for j in range(njc):
                        scores(j)
                        expj(j)
                        if j > 0:
                            pv(j - 1)
                    pv(njc - 1)
                    C.wait(TE, TE.v)
                    C.op(lambda e: e.copy(densb[0:1, 0:QCW], P[6][0:1, :]))
                    C.op(lambda e: e.copy(densb[32:33, 0:QCW], P[6][32:33, :]),
                         inc=SE)
                    V.wait(SE, SE.v)
                    V.op(lambda e: e.reciprocal(densb[0:1, 0:QCW], densb[0:1, 0:QCW]))
                    V.op(lambda e: e.reciprocal(densb[32:33, 0:QCW],
                                                densb[32:33, 0:QCW]), inc=VE)
                    T.wait(VE, VE.v)
                    T.op(lambda e: e.matmul(P[6][0:64, :], ones1[0:1, 0:64],
                                            densb[0:1, 0:QCW],
                                            start=True, stop=True))
                    T.op(lambda e: e.matmul(P[6][64:128, :], ones_sb[32:33, :],
                                            densb[32:33, 0:QCW],
                                            start=True, stop=True,
                                            tile_position=(32, 64)), inc=TE)
                    V.wait(TE, TE.v)
                    V.op(lambda e: e.tensor_copy(recb[:], P[6][:, :]))
                    V.op(lambda e, p=p: e.tensor_tensor(
                        attw[0:64, p, :], P[4][0:64, :], recb[0:64, :], ALU.mult))
                    V.op(lambda e, p=p: e.tensor_tensor(
                        attw[64:128, p, :], P[5][64:128, :], recb[64:128, :],
                        ALU.mult), inc=VE)
                    T.wait(VE, VE.v)
                    wq_point(s)

                if qc == NQC - 1:
                    wq_drain(s)
                for qt in range(4):
                    tglob = 4 * qc + qt
                    par = tglob % 2
                    S.wait(VE, VE.v)
                    S.wait(SE, SE.v)
                    S.dma(xin[:, par, :],
                          x_h[xrow0 + 128 * tglob: xrow0 + 128 * (tglob + 1), :],
                          inc=(LD, 16))
                    for nd in range(2):
                        T.wait(VE, VE.v)
                        for c in range(DC):
                            T.op(lambda e, c=c, qt=qt, nd=nd: e.matmul(
                                P[6][:, :], attw[:, c, 128 * qt:128 * (qt + 1)],
                                wX[:, c, 512 * nd:512 * (nd + 1)],
                                start=(c == 0), stop=(c == DC - 1)),
                                inc=(TE if c == DC - 1 else None))
                        V.wait(TE, TE.v)
                        V.wait(LD, LD.v)
                        V.op(lambda e, nd=nd: e.tensor_tensor(
                            ot1[:, :], P[6][:, :], bc2[:, 512 * nd:512 * (nd + 1)],
                            ALU.mult))
                        V.op(lambda e, nd=nd, tglob=tglob, par=par: e.tensor_tensor(
                            x1[:, tglob, 512 * nd:512 * (nd + 1)], ot1[:, :],
                            xin[:, par, 512 * nd:512 * (nd + 1)], ALU.add), inc=VE)

            if s == 0 and debug_taps:
                tap(attw[:, 0, :], 512)
                tap(pa[:], 512)
                tap(recb[:], 512)
                tap(x1[:, 0, :])
                tap(x1[:, NT - 1, :])
            expand(3, bc0)
            expand(4, bc1)
            expand(5, bc2)

            h1_prev_ln = [0]

            def h1_tile(qc2, qt2, dstbuf):
                tglob = 4 * qc2 + qt2
                C.wait(VE, h1_prev_ln[0])
                V.wait(GE, GE.v)
                h1_prev_ln[0] = ln_norm(x1[:, tglob, :], xn2[:])
                G.wait(VE, VE.v)
                G.op(lambda e: e.tensor_tensor(h1g[:], xn2[:], bc1[:], ALU.mult))
                G.wait(SE, SE.v)
                G.op(lambda e: e.tensor_tensor(h1f[:], h1g[:], bc0[:], ALU.add),
                     inc=GE)
                quant(h1f[:], h1qt[:], D, QF)
                T.wait(VE, VE.v)
                transpose_blocks(
                    lambda b: h1qt[:, 128 * b:128 * (b + 1)], DC,
                    lambda b, qt2=qt2: dstbuf[:, b, 128 * qt2:128 * (qt2 + 1)])

            HOIST = False
            CONSV = False
            h1bufs = [h1qf, h1qf2]
            for qt in range(4):
                h1_tile(0, qt, h1bufs[0])
            h1_se = [SE.v]

            gucnt = [0]
            ve_guard = [VE.v] * 7
            se_guard = [SE.v] * 7
            h2f_ve = [VE.v]
            blk_te = [TE.v, TE.v]
            dp_te = [TE.v, TE.v]
            tp_prev = [None]

            def flush_tp():
                if tp_prev[0] is None:
                    return
                pnoff, pnw, pqt, ppar2, q_ve = tp_prev[0]
                tp_prev[0] = None
                T.wait(VE, q_ve)
                transpose_blocks(
                    lambda b, ppar2=ppar2: h2qt[:, ppar2, 128 * b:128 * (b + 1)],
                    pnw // 128,
                    lambda b, pnoff=pnoff, pqt=pqt: h2qi[:, (pnoff // 128) + b,
                                                         128 * pqt:128 * (pqt + 1)])

            for qc in range(NQC):
                hsrc = h1bufs[qc % 2]
                if not HOIST and qc > 0:
                    for qt in range(4):
                        h1_tile(qc, qt, hsrc)
                    h1_se[0] = SE.v
                for nbi, (noff, nw) in enumerate(cfg.NBL):
                    par = nbi % 2
                    S.wait(TE, blk_te[par])
                    S.dma(wgb[:, par, :, 0:nw], wgq_h[:, :, noff:noff + nw],
                          inc=(LD, 16))
                    S.dma(wub[:, par, :, 0:nw], wuq_h[:, :, noff:noff + nw],
                          inc=(LD, 16))
                    for qt in range(4):
                        k = gucnt[0] % 2
                        gucnt[0] += 1
                        gP = P[k]
                        uP = P[2 + k]
                        T.wait(LD, LD.v)
                        T.wait(VE, max(ve_guard[k], ve_guard[2 + k]))
                        T.wait(SE, max(se_guard[k], h1_se[0]))
                        for kc in range(DC):
                            T.op(lambda e, kc=kc, qt=qt, par=par, nw=nw, gP=gP,
                                 hsrc=hsrc: e.matmul(
                                gP[:, 0:nw], hsrc[:, kc, 128 * qt:128 * (qt + 1)],
                                wgb[:, par, kc, 0:nw],
                                start=(kc == 0), stop=(kc == DC - 1)))
                        for kc in range(DC):
                            T.op(lambda e, kc=kc, qt=qt, par=par, nw=nw, uP=uP,
                                 hsrc=hsrc: e.matmul(
                                uP[:, 0:nw], hsrc[:, kc, 128 * qt:128 * (qt + 1)],
                                wub[:, par, kc, 0:nw],
                                start=(kc == 0), stop=(kc == DC - 1)),
                                inc=(TE if kc == DC - 1 else None))
                        C.wait(TE, TE.v)
                        C.wait(VE, h2f_ve[0])
                        C.op(lambda e, gP=gP, nw=nw: e.activation(
                            silu_s[:, 0:nw], gP[:, 0:nw], AF.Silu), inc=SE)
                        se_guard[k] = SE.v
                        flush_tp()
                        V.wait(SE, se_guard[k])
                        V.op(lambda e, uP=uP, nw=nw: e.tensor_tensor(
                            h2f[:, 0:nw], silu_s[:, 0:nw], uP[:, 0:nw], ALU.mult),
                            inc=VE)
                        ve_guard[2 + k] = VE.v
                        h2f_ve[0] = VE.v
                        par2 = gucnt[0] % 2
                        quant(h2f[:, 0:nw], h2qt[:, par2, 0:nw], nw, QF)
                        tp_prev[0] = (noff, nw, qt, par2, VE.v)
                    blk_te[par] = TE.v
                    if HOIST and qc < NQC - 1 and nbi < 4:
                        h1_tile(qc + 1, nbi, h1bufs[(qc + 1) % 2])
                flush_tp()

                for nd in range(2):
                    for icx in range(IC):
                        par = icx % 2
                        S.wait(TE, dp_te[par])
                        S.dma(wdb[:, par, :], wdq_h[:, icx, 512 * nd:512 * (nd + 1)],
                              inc=(LD, 16))
                        T.wait(LD, LD.v)
                        if icx == 0:
                            T.wait(VE, VE.v)
                            T.wait(SE, SE.v)
                        for qt in range(4):
                            T.op(lambda e, icx=icx, qt=qt, par=par: e.matmul(
                                P[qt][:, :], h2qi[:, icx, 128 * qt:128 * (qt + 1)],
                                wdb[:, par, :],
                                start=(icx == 0), stop=(icx == IC - 1)),
                                inc=(TE if qt == 3 else None))
                        dp_te[par] = TE.v
                    V.wait(TE, TE.v)
                    for qt in range(4):
                        tglob = 4 * qc + qt
                        V.wait(ST, ST.v)
                        V.op(lambda e, qt=qt, nd=nd: e.tensor_tensor(
                            o1f[:, :], P[qt][:, :], bc2[:, 512 * nd:512 * (nd + 1)],
                            ALU.mult))
                        V.op(lambda e, qt=qt, nd=nd, tglob=tglob: e.tensor_tensor(
                            x3s[:, qt, 512 * nd:512 * (nd + 1)], o1f[:, :],
                            x1[:, tglob, 512 * nd:512 * (nd + 1)], ALU.add),
                            inc=VE)
                        ve_guard[qt] = VE.v
                        if nd == 1:
                            G.wait(VE, VE.v)
                            G.dma(y_h[xrow0 + 128 * tglob: xrow0 + 128 * (tglob + 1), :],
                                  x3s[:, qt, :], inc=(ST, 16))

        G.wait(ST, ST.v)

    with nc.Block() as blk:
        @blk.sync
        def _(eng):
            pr.begin_pass("sync", eng, allsems)
            script(pr)

        @blk.gpsimd
        def _(eng):
            pr.begin_pass("gpsimd", eng, allsems)
            script(pr)

        @blk.tensor
        def _(eng):
            pr.begin_pass("tensor", eng, allsems)
            script(pr)

        @blk.vector
        def _(eng):
            pr.begin_pass("vector", eng, allsems)
            script(pr)

        @blk.scalar
        def _(eng):
            pr.begin_pass("scalar", eng, allsems)
            script(pr)

    for c in semctx:
        c.__exit__(None, None, None)
    return nc


def _consts():
    ident = np.eye(128, dtype=ml_dtypes.bfloat16)
    tri = np.zeros((128, 128), np.float32)
    r = np.arange(128)
    tri[r[:, None] > r[None, :]] = NEG_BIG
    tri = tri.astype(ml_dtypes.bfloat16)
    indik = np.zeros((2, 128), np.float32)
    indik[0, 0:64] = 1.0
    indik[1, 64:128] = 1.0
    ones1 = np.ones((1, 128), np.float32)
    return ident, tri, indik, ones1


def make_in_maps(cfg, x, wq, wk, wv, wo, wg, wu, wd, c, wmod, n_cores=8):
    ident, tri, indik, ones1 = _consts()
    shared = dict(
        wq=np.ascontiguousarray(wq, dtype=np.float32),
        wk=np.ascontiguousarray(wk, dtype=np.float32),
        wv=np.ascontiguousarray(wv, dtype=np.float32),
        wo=np.ascontiguousarray(wo, dtype=np.float32),
        wg=np.ascontiguousarray(wg, dtype=np.float32),
        wu=np.ascontiguousarray(wu, dtype=np.float32),
        wd=np.ascontiguousarray(wd, dtype=np.float32),
        wmod=np.ascontiguousarray(wmod, dtype=np.float32),
        ident=ident, tri=tri, indik=indik, ones1=ones1,
    )
    c = np.asarray(c, np.float32)
    x = np.asarray(x, np.float32)
    in_maps = []
    for i in range(n_cores):
        ct = np.ascontiguousarray(
            c[cfg.NSEQ * i:cfg.NSEQ * (i + 1)].T.reshape(DC, 128, cfg.NSEQ)
            .transpose(1, 0, 2))
        xi = np.ascontiguousarray(
            x[cfg.NSEQ * i:cfg.NSEQ * (i + 1)].reshape(cfg.NSEQ * cfg.S, D))
        in_maps.append(dict(shared, x=xi, ct=ct))
    return in_maps


def kernel(L0_attn_x, L0_attn_W_Q, L0_attn_W_K, L0_attn_W_V, L0_attn_W_o,
           L0_ffn_w_gate, L0_ffn_w_up, L0_ffn_w_down, L0_c, L0_wmod):
    cfg = Cfg(S=2048, I=2816)
    nc = build_program(cfg)
    in_maps = make_in_maps(cfg, L0_attn_x, L0_attn_W_Q, L0_attn_W_K, L0_attn_W_V,
                           L0_attn_W_o, L0_ffn_w_gate, L0_ffn_w_up, L0_ffn_w_down,
                           L0_c, L0_wmod)
    res = run_bass_kernel_spmd(nc, in_maps, core_ids=list(range(8)))
    out = np.empty((16, 2048, 1024), np.float32)
    for i in range(8):
        out[2 * i:2 * i + 2] = res.results[i]["y"].reshape(2, cfg.S, D)
    return out


# version 16
# speedup vs baseline: 1.4396x; 1.0373x over previous
import numpy as np
import ml_dtypes

import concourse.bass as bass
import concourse.mybir as mybir
from concourse.bass_utils import run_bass_kernel_spmd

F32 = mybir.dt.float32
BF16 = mybir.dt.bfloat16
I32 = mybir.dt.int32
AF = mybir.ActivationFunctionType
ALU = mybir.AluOpType
AX = mybir.AxisListType

AND_M = int(np.uint32(0xFFC00000).view(np.int32))
OR_M = 0x00200000
EXP_M = 0x7F800000
NEG_BIG = -1.0e9

D = 1024
DC = 8
NPAIR = 8
QCW = 512
EPS = 1e-5


class Cfg:
    def __init__(self, S=2048, I=2816):
        self.S = S
        self.NSEQ = 2
        self.I = I
        self.IC = I // 128
        self.NQC = S // QCW
        self.NT = S // 128
        self.NBL = []
        off = 0
        while off < I:
            w = min(512, I - off)
            self.NBL.append((off, w))
            off += w


class Sem:
    def __init__(self, handle):
        self.h = handle
        self.v = 0


class Eng:
    def __init__(self, prog, name):
        self.prog = prog
        self.name = name

    @property
    def real(self):
        return self.prog.active_eng if self.prog.active == self.name else None

    def _fin(self, ins, inc):
        if inc is not None:
            sem, n = (inc, 1) if isinstance(inc, Sem) else inc
            sem.v += n
            if ins is not None:
                ins.then_inc(sem.h, n)
        return ins

    def wait(self, sem, val=None):
        if val is None:
            val = sem.v
        if val <= 0:
            return
        e = self.real
        if e is not None:
            e.wait_ge(sem.h, val)

    def op(self, fn, inc=None):
        e = self.real
        ins = fn(e) if e is not None else None
        return self._fin(ins, inc)

    def dma(self, out, in_, inc=None):
        return self.op(lambda e: e.dma_start(out=out, in_=in_), inc)


class Prog:
    def __init__(self, nc):
        self.nc = nc
        self.active = None
        self.active_eng = None
        self.S = Eng(self, "sync")
        self.G = Eng(self, "gpsimd")
        self.T = Eng(self, "tensor")
        self.V = Eng(self, "vector")
        self.C = Eng(self, "scalar")

    def begin_pass(self, name, eng, sems):
        self.active = name
        self.active_eng = eng
        for s in sems:
            s.v = 0


def build_program(cfg: Cfg, debug_stop=None, debug_taps=False):
    nc = bass.Bass("TRN2", target_bir_lowering=False, detect_race_conditions=False)
    S_, I_, IC, NQC, NT, NSEQ = cfg.S, cfg.I, cfg.IC, cfg.NQC, cfg.NT, cfg.NSEQ

    dt = nc.dram_tensor
    x_h = dt("x", [NSEQ * S_, D], F32, kind="ExternalInput").ap()
    wq_h = dt("wq", [D, D], F32, kind="ExternalInput").ap()
    wk_h = dt("wk", [D, D], F32, kind="ExternalInput").ap()
    wv_h = dt("wv", [D, D], F32, kind="ExternalInput").ap()
    wo_h = dt("wo", [D, D], F32, kind="ExternalInput").ap()
    wg_h = dt("wg", [I_, D], F32, kind="ExternalInput").ap()
    wu_h = dt("wu", [I_, D], F32, kind="ExternalInput").ap()
    wd_h = dt("wd", [D, I_], F32, kind="ExternalInput").ap()
    ct_h = dt("ct", [128, DC, NSEQ], F32, kind="ExternalInput").ap()
    wmod_h = dt("wmod", [D, 6144], F32, kind="ExternalInput").ap()
    ident_h = dt("ident", [128, 128], BF16, kind="ExternalInput").ap()
    tri_h = dt("tri", [128, 128], BF16, kind="ExternalInput").ap()
    indik_h = dt("indik", [2, 128], F32, kind="ExternalInput").ap()
    ones1_h = dt("ones1", [1, 128], F32, kind="ExternalInput").ap()
    y_h = dt("y", [NSEQ * S_, D], F32, kind="ExternalOutput").ap()
    dbg_h = dt("dbg", [128, 16, 1024], F32, kind="ExternalOutput").ap() if debug_taps else None
    wgq_h = dt("wgq_s", [128, DC, I_], BF16, kind="Internal").ap()
    wuq_h = dt("wuq_s", [128, DC, I_], BF16, kind="Internal").ap()
    wdq_h = dt("wdq_s", [128, IC, D], BF16, kind="Internal").ap()
    mod_h = dt("mod_s", [NSEQ, 6144], F32, kind="Internal").ap()

    SBUF_LIMIT = nc.SBUF_PARTITION_SIZE_BYTES
    cur = [17 * 1024]

    def at(name, shape, dtp, align=32):
        nbytes = int(np.prod(shape[1:])) * mybir.dt.size(dtp)
        cur[0] = (cur[0] + align - 1) // align * align
        t = nc.alloc_sbuf_tensor_at(name, shape, dtp, offset=cur[0])
        cur[0] += nbytes
        return t

    ident = at("ident", [128, 128], BF16)
    tri = at("tri", [128, 128], BF16)
    indik = at("indik", [2, 128], F32)
    ones1 = at("ones1", [1, 128], F32)
    ct = at("ct", [128, DC, NSEQ], F32)
    vrow = at("vrow", [NSEQ, 1024], F32)
    bc0 = at("bc0", [128, D], F32)
    bc1 = at("bc1", [128, D], F32)
    bc2 = at("bc2", [128, D], F32)
    ln_s = at("ln_s", [128, 10], F32)
    onescol = at("onescol", [128, 1], BF16)
    ones_sb = at("ones_sb", [128, 64], F32)
    act_scr = at("act_scr", [128, D], F32)
    xin = at("xin", [128, 2, D], F32)
    xn_t = at("xn_t", [128, D], BF16)
    x1_off = (cur[0] + 31) // 32 * 32
    x1 = at("x1", [128, NT, D], BF16)
    shared_end = cur[0]

    ov0 = (shared_end + 31) // 32 * 32

    cur[0] = ov0
    stW = at("stW", [128, 2, 3072], F32)
    assert cur[0] <= SBUF_LIMIT, f"phase-W overlay over budget: {cur[0]}"

    cur[0] = ov0
    h0 = at("h0", [128, DC, S_], BF16)
    qf = nc.alloc_sbuf_tensor_at("qf", [128, DC, S_], BF16, offset=ov0)
    wX = at("wX", [128, DC, D], BF16)
    kf = at("kf", [128, DC, S_], BF16)
    vt = at("vt", [128, NT, 16 * 64], BF16)
    attw_off = (cur[0] + 31) // 32 * 32
    attw = at("attw", [128, DC, QCW], BF16)
    pa = at("pa", [128, 2, QCW], BF16)
    pb = at("pb", [128, 2, QCW], BF16)
    densb = at("densb", [128, QCW], F32)
    recb = at("recb", [128, QCW], F32)
    ot1 = recb
    xn_u = nc.alloc_sbuf_tensor_at("xn_u", [128, D], BF16, offset=attw_off)
    xnn = nc.alloc_sbuf_tensor_at("xnn", [128, D], F32, offset=attw_off + 2048)
    att_end = cur[0]
    assert cur[0] <= SBUF_LIMIT, f"phase-ATT overlay over budget: {cur[0]}"

    cur[0] = ov0
    qU = at("qU", [128, 1024], BF16)
    qX = at("qX", [128, 1024], BF16)
    qM1 = at("qM1", [128, 1024], BF16)
    qSG = at("qSG", [128, 1024], BF16)
    qsm = at("qsm", [128, 3, 64], F32)
    qsb = at("qsb", [128, 64], BF16)
    xn2 = at("xn2", [128, D], F32)
    h1g = at("h1g", [128, D], F32)
    h1f = at("h1f", [128, D], F32)
    h1qt = at("h1qt", [128, D], BF16)
    h1qf = at("h1qf", [128, DC, QCW], BF16)
    wgb = at("wgb", [128, 2, DC, 512], BF16)
    wub = at("wub", [128, 2, DC, 512], BF16)
    wdb = at("wdb", [128, 2, 512], BF16)
    silu_s = at("silu_s", [128, 512], F32)
    h2f = at("h2f", [128, 512], F32)
    h2qt = at("h2qt", [128, 2, 512], BF16)
    h2qi = at("h2qi", [128, IC, QCW], BF16)
    o1f = at("o1f", [128, 512], F32)
    x3s = at("x3s", [128, 4, D], F32)
    ffn_end = cur[0]
    assert cur[0] <= SBUF_LIMIT, f"phase-FFN overlay over budget: {cur[0]}"

    cur[0] = max(att_end, ffn_end)
    wreg0 = (cur[0] + 31) // 32 * 32
    stW2 = at("stW2", [128, 2, 1024], F32)
    h1qf2 = nc.alloc_sbuf_tensor_at("h1qf2", [128, DC, QCW], BF16, offset=wreg0)
    qbf2 = at("qbf2", [128, 2, 1024], BF16)
    tstage2 = at("tstage2", [128, 1024], BF16)
    qsm_w = at("qsm_w", [128, 3, 64], F32)
    qsb_w = at("qsb_w", [128, 64], BF16)
    assert cur[0] <= SBUF_LIMIT, f"W-quant region over budget: {cur[0]}"
    qU_w = nc.alloc_sbuf_tensor_at("qU_w", [128, 1024], BF16, offset=x1_off + 24576)
    qX_w = nc.alloc_sbuf_tensor_at("qX_w", [128, 1024], BF16, offset=x1_off + 26624)
    qM1_w = nc.alloc_sbuf_tensor_at("qM1_w", [128, 1024], BF16, offset=x1_off + 28672)
    qSG_w = nc.alloc_sbuf_tensor_at("qSG_w", [128, 1024], BF16, offset=x1_off + 30720)

    P = [nc.alloc_psum_tensor(f"P{k}", [128, 512], F32) for k in range(7)]
    T1 = nc.alloc_psum_tensor("T1", [128, 512], BF16)
    Ttile = [T1[:, 0:128], T1[:, 128:256]]

    semctx = [nc.semaphore(nm) for nm in ["LD", "ST", "TE", "VE", "SE", "GE"]]
    handles = [c.__enter__() for c in semctx]
    LD, ST, TE, VE, SE, GE = [Sem(h) for h in handles]
    allsems = [LD, ST, TE, VE, SE, GE]
    pr = Prog(nc)

    (ln_sx, ln_m2, ln_nm1, ln_nvar, ln_is, ln_std, eps_b, exp_b, zcol,
     jcol) = (ln_s[:, k:k + 1] for k in range(10))

    def script(pr: Prog):
        S, G, T, V, C = pr.S, pr.G, pr.T, pr.V, pr.C
        dbg_slot = [0]

        def tap(ap, cols=1024):
            if not debug_taps:
                return
            k = dbg_slot[0]
            dbg_slot[0] += 1
            G.wait(VE, VE.v)
            G.wait(SE, SE.v)
            G.wait(TE, TE.v)
            G.wait(GE, GE.v)
            G.dma(dbg_h[:, k, 0:cols], ap, inc=(ST, 16))

        def v3(ap2, w):
            return ap2.rearrange("p (nb b) -> p nb b", b=16)

        def quant(src, dst, w, Q, te_guard=None):
            (U, A_, B_, SG, SM, SB) = Q
            nb = w // 16
            C.wait(GE, GE.v)
            C.wait(VE, VE.v)
            C.op(lambda e: e.activation(SG[:, 0:w], src, AF.Sign), inc=SE)
            se_sgn = SE.v
            m_, s_, r_ = (SM[:, k, 0:nb] for k in range(3))
            sb_ = SB[:, 0:nb]
            V.wait(GE, GE.v)
            hv = nb // 2
            halves = [(0, hv), (hv, nb)] if (2 <= hv < 128) else [(0, nb)]
            srcv = v3(src, w)

            def forall(fn):
                for (a, b) in halves:
                    V.op(lambda e, a=a, b=b: fn(e, slice(a, b)))

            forall(lambda e, sl: e.tensor_reduce(
                m_[:, sl], srcv[:, sl, :], AX.X, ALU.max,
                apply_absolute_value=True))
            forall(lambda e, sl: e.tensor_scalar(s_[:, sl], m_[:, sl], 1.0 / 6.0,
                                                 1e-12, ALU.mult, ALU.max))
            forall(lambda e, sl: e.reciprocal(r_[:, sl], s_[:, sl]))
            forall(lambda e, sl: e.tensor_copy(sb_[:, sl], s_[:, sl]))
            rb = r_.unsqueeze(2).broadcast_to([128, nb, 16])
            sbb = sb_.unsqueeze(2).broadcast_to([128, nb, 16])
            u = U[:, 0:w]
            u3 = v3(u, w)
            V.op(lambda e: e.tensor_tensor(u3, srcv, rb, ALU.mult))
            V.wait(SE, se_sgn)
            V.op(lambda e: e.tensor_tensor(u, u, SG[:, 0:w], ALU.mult))
            Aw, Bw = A_[:, 0:w], B_[:, 0:w]
            V.op(lambda e: e.tensor_scalar(Aw, u, 0.25, None, ALU.is_gt))
            V.op(lambda e: e.scalar_tensor_tensor(Aw, u, 0.75, Aw, ALU.is_gt, ALU.add))
            V.op(lambda e: e.scalar_tensor_tensor(Aw, u, 1.25, Aw, ALU.is_gt, ALU.add))
            V.op(lambda e: e.scalar_tensor_tensor(Aw, u, 1.75, Aw, ALU.is_gt, ALU.add))
            V.op(lambda e: e.tensor_scalar(Bw, u, 5.0, 2.0, ALU.is_gt, ALU.mult))
            V.op(lambda e: e.scalar_tensor_tensor(Bw, u, 2.5, Bw, ALU.is_gt, ALU.add))
            V.op(lambda e: e.scalar_tensor_tensor(Bw, u, 3.5, Bw, ALU.is_gt, ALU.add))
            V.op(lambda e: e.scalar_tensor_tensor(Aw, Aw, 0.5, Bw, ALU.mult, ALU.add))
            V.op(lambda e: e.tensor_tensor(Aw, Aw, SG[:, 0:w], ALU.mult))
            if te_guard is not None:
                V.wait(TE, te_guard)
            V.op(lambda e: e.tensor_tensor(v3(dst, w), v3(Aw, w), sbb, ALU.mult),
                 inc=VE)

        QW = (qU_w, qX_w, qM1_w, qSG_w, qsm_w, qsb_w)
        QF = (qU, qX, qM1, qSG, qsm, qsb)

        def transpose_blocks(src_fn, nblk, dst_fn):
            for b in range(nblk):
                par = b % 2
                T.wait(SE, SE.v)
                T.op(lambda e, b=b, par=par: e.transpose(Ttile[par], src_fn(b), ident[:]),
                     inc=TE)
                C.wait(TE, TE.v)
                C.op(lambda e, b=b, par=par: e.copy(dst_fn(b), Ttile[par]), inc=SE)

        def ln_norm(x_ap, out_ap):
            C.op(lambda e: e.activation(act_scr[:], x_ap, AF.Copy, accum_out=ln_sx))
            C.op(lambda e: e.activation(act_scr[:], x_ap, AF.Square,
                                        scale=1.0 / 32.0, bias=zcol,
                                        accum_out=ln_m2))
            C.op(lambda e: e.activation(ln_nm1, ln_sx, AF.Identity,
                                        scale=-1.0 / D, bias=zcol), inc=SE)
            V.wait(SE, SE.v)
            V.op(lambda e: e.scalar_tensor_tensor(ln_nvar, ln_nm1, ln_nm1, ln_m2,
                                                  ALU.mult, ALU.subtract), inc=VE)
            C.wait(VE, VE.v)
            C.op(lambda e: e.activation(ln_std, ln_nvar, AF.Sqrt, scale=-1.0,
                                        bias=eps_b), inc=SE)
            V.wait(SE, SE.v)
            V.op(lambda e: e.reciprocal(ln_is, ln_std))
            V.op(lambda e: e.memset(jcol, 0.0))
            V.op(lambda e: e.tensor_scalar(out_ap, x_ap, ln_nm1, ln_is,
                                           ALU.add, ALU.mult), inc=VE)
            return VE.v

        for dst, src in [(ident[:], ident_h), (tri[:], tri_h), (indik[:], indik_h),
                         (ones1[:], ones1_h), (ct[:], ct_h)]:
            S.dma(dst, src, inc=(LD, 16))
        V.op(lambda e: e.memset(eps_b, EPS))
        V.op(lambda e: e.memset(zcol, 0.0))
        V.op(lambda e: e.memset(jcol, 0.0))
        V.op(lambda e: e.memset(onescol[:], 1.0))
        V.op(lambda e: e.memset(ones_sb[:], 1.0))
        V.op(lambda e: e.memset(exp_b, -6.0), inc=VE)
        C.wait(VE, VE.v)
        C.wait(LD, LD.v)
        ctf = ct[:].rearrange("p c n -> p (c n)")
        C.op(lambda e: e.activation(ctf, ctf, AF.Silu), inc=SE)

        for c in range(DC):
            if c >= 1:
                S.wait(TE, TE.v)
            S.dma(stW[:, 0, :], wmod_h[128 * c:128 * (c + 1), 0:3072], inc=(LD, 16))
            S.dma(stW[:, 1, :], wmod_h[128 * c:128 * (c + 1), 3072:6144], inc=(LD, 16))
            T.wait(LD, LD.v)
            T.wait(SE, SE.v)
            for h in range(2):
                for n in range(6):
                    col = 3072 * h + 512 * n
                    tp = (0, 0) if h == 0 else (0, 32)
                    rows = slice(0, NSEQ) if h == 0 else slice(32, 32 + NSEQ)
                    last = (h == 1 and n == 5)
                    T.op(lambda e, col=col, tp=tp, rows=rows, c=c: e.matmul(
                        P[n][rows, :], ct[:, c, :],
                        stW[:, col // 3072, (col % 3072):(col % 3072) + 512],
                        start=(c == 0), stop=(c == DC - 1), tile_position=tp),
                        inc=(TE if last else None))
        V.wait(TE, TE.v)
        for k in range(6):
            h = k // 3
            n0 = (2 * k) % 6
            rows = slice(0, NSEQ) if h == 0 else slice(32, 32 + NSEQ)
            r0 = 0 if h == 0 else 32
            buf = act_scr[r0:r0 + NSEQ, :]
            tbf = xin[r0:r0 + NSEQ, 0, 0:1024]
            if k >= 1:
                V.wait(ST, ST.v)
            V.op(lambda e, n0=n0, rows=rows, buf=buf: e.tensor_copy(
                buf[:, 0:512], P[n0][rows, :]))
            V.op(lambda e, n0=n0, rows=rows, buf=buf: e.tensor_copy(
                buf[:, 512:1024], P[n0 + 1][rows, :]))
            if k in (1, 4):
                V.op(lambda e, buf=buf: e.tensor_scalar(buf, buf, 1.0, None, ALU.add))
            if k in (2, 5):
                V.op(lambda e, buf=buf, tbf=tbf: e.tensor_scalar(
                    tbf, buf, 0.0, 2.0, ALU.is_ge, ALU.mult))
                V.op(lambda e, tbf=tbf: e.tensor_scalar(tbf, tbf, -1.0, None, ALU.add))
                V.op(lambda e, buf=buf, tbf=tbf: e.tensor_tensor(buf, buf, tbf,
                                                                 ALU.mult))
                V.op(lambda e, buf=buf: e.tensor_scalar(buf, buf, 0.25, None, ALU.max))
                V.op(lambda e, buf=buf, tbf=tbf: e.tensor_tensor(buf, buf, tbf, ALU.mult))
            V.op(lambda e, buf=buf: e.tensor_scalar(buf, buf, 0.0, None, ALU.add),
                 inc=VE)
            G.wait(VE, VE.v)
            G.dma(mod_h[:, 1024 * k:1024 * (k + 1)], buf, inc=(ST, 16))

        if debug_stop == "mod":
            G.wait(ST, ST.v)
            return
        wchunks = []
        for icx in range(cfg.IC):
            wchunks.append((wg_h, 128 * icx, 0, D, wgq_h))
        for icx in range(cfg.IC):
            wchunks.append((wu_h, 128 * icx, 0, D, wuq_h))
        for hc in range(DC):
            for coff, cwid in ((0, 1024), (1024, 1024), (2048, 768)):
                wchunks.append((wd_h, 128 * hc, coff, cwid, wdq_h))
        wstate = {"li": 0, "ti": 0, "te_qbf": [0, 0]}

        def w_load_quant():
            i = wstate["li"]
            if i >= len(wchunks):
                return
            wstate["li"] += 1
            src_h, r0, c0, cw, dst_h = wchunks[i]
            par = i % 2
            S.wait(VE, VE.v)
            S.wait(SE, SE.v)
            S.dma(stW2[:, par, 0:cw], src_h[r0:r0 + 128, c0:c0 + cw],
                  inc=(LD, 16))
            V.wait(LD, LD.v)
            C.wait(LD, LD.v)
            quant(stW2[:, par, 0:cw], qbf2[:, par, 0:cw], cw, QW,
                  te_guard=wstate["te_qbf"][par])

        def w_transpose_store():
            j = wstate["ti"]
            if j >= wstate["li"]:
                return
            wstate["ti"] += 1
            src_h, r0, c0, cw, dst_h = wchunks[j]
            par = j % 2
            nblk = cw // 128
            T.wait(VE, VE.v)
            T.wait(ST, ST.v)
            transpose_blocks(
                lambda b, par=par: qbf2[:, par, 128 * b:128 * (b + 1)], nblk,
                lambda b: tstage2[:, 128 * b:128 * (b + 1)])
            wstate["te_qbf"][par] = TE.v
            G.wait(SE, SE.v)
            if dst_h is wdq_h:
                dst_ap = dst_h[:, (c0 // 128):(c0 // 128) + nblk, r0:r0 + 128]
            else:
                dst_ap = dst_h[:, :, r0:r0 + 128]
            G.dma(dst_ap,
                  tstage2[:, 0:cw].rearrange("p (c w) -> p c w", w=128),
                  inc=(ST, 16))

        def wq_point(s):
            if s != 0:
                return
            w_transpose_store()
            w_load_quant()

        def wq_drain(s):
            if s != 0:
                return
            while wstate["ti"] < len(wchunks):
                w_transpose_store()
                w_load_quant()

        for s in range(NSEQ):
            xrow0 = s * S_

            def expand(k, dstb):
                S.wait(TE, TE.v)
                S.wait(ST, ST.v)
                S.dma(vrow[0:1, :], mod_h[s:s + 1, 1024 * k:1024 * (k + 1)],
                      inc=(LD, 16))
                T.wait(LD, LD.v)
                for half in range(2):
                    T.wait(VE, VE.v)
                    T.op(lambda e, half=half: e.matmul(
                        P[6][:, :], ones1[:], vrow[0:1, 512 * half:512 * half + 512],
                        start=True, stop=True), inc=TE)
                    V.wait(TE, TE.v)
                    V.op(lambda e, half=half: e.tensor_copy(
                        dstb[:, 512 * half:512 * half + 512], P[6][:, :]), inc=VE)

            expand(0, bc0)
            expand(1, bc1)
            expand(2, bc2)
            if s == 0 and debug_taps:
                tap(bc0[:]); tap(bc1[:]); tap(bc2[:])

            ln_prev = [0]
            xin_se = [SE.v, SE.v]
            xin_ve = [VE.v, VE.v]
            for t in range(NT):
                par = t % 2
                S.wait(SE, xin_se[par])
                S.wait(VE, xin_ve[par])
                S.dma(xin[:, par, :], x_h[xrow0 + 128 * t: xrow0 + 128 * (t + 1), :],
                      inc=(LD, 16))
                C.wait(LD, LD.v)
                C.wait(VE, ln_prev[0])
                V.wait(GE, GE.v)
                ln_prev[0] = ln_norm(xin[:, par, :], xnn[:])
                xin_se[par] = SE.v
                xin_ve[par] = VE.v
                G.wait(VE, VE.v)
                G.op(lambda e: e.tensor_tensor(xn_u[:], xnn[:], bc1[:], ALU.mult))
                G.wait(TE, TE.v)
                G.op(lambda e: e.tensor_tensor(xn_t[:], xn_u[:], bc0[:], ALU.add),
                     inc=GE)
                T.wait(GE, GE.v)
                if s == 0 and t == 0 and debug_taps:
                    tap(xn_t[:])
                transpose_blocks(
                    lambda b: xn_t[:, 128 * b:128 * (b + 1)], DC,
                    lambda b, t=t: h0[:, b, 128 * t:128 * (t + 1)])
                wq_point(s)

            def load_w(w_h, dst):
                G.wait(TE, TE.v)
                G.dma(dst[:], w_h.rearrange("(c p) w -> p c w", p=128),
                      inc=(ST, 16))

            def proj_block(wbuf, nq, dst_col_fn):
                T.wait(SE, SE.v)
                for wave in ([0, 1, 2, 3, 4, 5, 6], [7]):
                    for mc in wave:
                        bank = P[mc % 7]
                        if mc == 7:
                            C.wait(TE, TE.v)
                            for mc2 in [0, 1, 2, 3, 4, 5, 6]:
                                C.op(lambda e, mc2=mc2, nq=nq: e.copy(
                                    dst_col_fn(mc2, nq), P[mc2][:, :]), inc=SE)
                            T.wait(SE, SE.v)
                        for kc in range(DC):
                            T.op(lambda e, kc=kc, mc=mc, nq=nq, bank=bank: e.matmul(
                                bank[:, :], wbuf[:, kc, 128 * mc:128 * (mc + 1)],
                                h0[:, kc, 512 * nq:512 * (nq + 1)],
                                start=(kc == 0), stop=(kc == DC - 1)),
                                inc=(TE if kc == DC - 1 else None))
                C.wait(TE, TE.v)
                C.op(lambda e, nq=nq: e.copy(dst_col_fn(7, nq), P[0][:, :]), inc=SE)

            def proj_block_q(wbuf, nq):
                T.wait(SE, SE.v)
                T.wait(VE, VE.v)
                for mc in range(DC):
                    bank = P[mc % 7]
                    if mc == 7:
                        C.wait(TE, TE.v)
                        for mc2 in range(7):
                            C.op(lambda e, mc2=mc2: e.copy(
                                attw[:, mc2, :], P[mc2][:, :]), inc=SE)
                        T.wait(SE, SE.v)
                    for kc in range(DC):
                        T.op(lambda e, kc=kc, mc=mc, nq=nq, bank=bank: e.matmul(
                            bank[:, :], wbuf[:, kc, 128 * mc:128 * (mc + 1)],
                            h0[:, kc, 512 * nq:512 * (nq + 1)],
                            start=(kc == 0), stop=(kc == DC - 1)),
                            inc=(TE if kc == DC - 1 else None))
                C.wait(TE, TE.v)
                C.op(lambda e: e.copy(attw[:, 7, :], P[0][:, :]), inc=SE)
                V.wait(SE, SE.v)
                V.op(lambda e, nq=nq: e.tensor_copy(
                    qf[:, :, 512 * nq:512 * (nq + 1)], attw[:]), inc=VE)

            if s == 0 and debug_taps:
                tap(h0[:, 0, 0:1024])
            load_w(wk_h, wX)
            if s == 0 and debug_taps:
                tap(wX[:, 0, :])
            T.wait(ST, ST.v)
            for nq in range(S_ // 512):
                proj_block(wX, nq, lambda mc, nq: kf[:, mc, 512 * nq:512 * (nq + 1)])
                wq_point(s)
                wq_point(s)

            if s == 0 and debug_taps:
                tap(kf[:, 0, 0:1024])
            load_w(wv_h, wX)
            T.wait(ST, ST.v)
            for t in range(NT):
                for nd in range(2):
                    bank = P[(2 * t + nd) % 7]
                    if t * 2 + nd >= 7:
                        T.wait(SE, SE.v)
                    for kc in range(DC):
                        T.op(lambda e, kc=kc, t=t, nd=nd, bank=bank: e.matmul(
                            bank[:, :], h0[:, kc, 128 * t:128 * (t + 1)],
                            wX[:, kc, 512 * nd:512 * (nd + 1)],
                            start=(kc == 0), stop=(kc == DC - 1)),
                            inc=(TE if kc == DC - 1 else None))
                    C.wait(TE, TE.v)
                    C.op(lambda e, t=t, nd=nd, bank=bank: e.copy(
                        vt[:, t, 512 * nd:512 * (nd + 1)], bank[:, :]), inc=SE)
                if t % 2 == 1:
                    wq_point(s)
            T.wait(SE, SE.v)

            if s == 0 and debug_taps:
                tap(vt[:, 0, :])
            load_w(wq_h, wX)
            T.wait(ST, ST.v)
            for nq in range(S_ // 512):
                proj_block_q(wX, nq)
                wq_point(s)
                wq_point(s)

            if s == 0 and debug_taps:
                tap(qf[:, 0, 0:1024])
            load_w(wo_h, wX)
            T.wait(ST, ST.v)

            att_se = [SE.v, SE.v]
            att_pv_te = [TE.v, TE.v]
            for qc in range(NQC):
                njc = 4 * qc + 4
                for p in range(NPAIR):
                    exp_se = {}
                    sc_te = {}

                    def scores(j):
                        par = j % 2
                        vs = max(0, 128 * j - QCW * qc)
                        diag = (128 * j >= QCW * qc)
                        sA = P[0 + par]
                        sB = P[2 + par]
                        T.wait(SE, att_se[par])
                        q_ap = qf[0:64, p, QCW * qc + vs: QCW * (qc + 1)]
                        qb_ap = qf[64:128, p, QCW * qc + vs: QCW * (qc + 1)]
                        T.op(lambda e, sA=sA, j=j, vs=vs, p=p, q_ap=q_ap:
                             e.matmul(sA[:, vs:QCW], kf[0:64, p, 128 * j:128 * (j + 1)],
                                      q_ap, start=True, stop=not diag,
                                      tile_position=(0, 0)))
                        if diag:
                            T.op(lambda e, sA=sA, vs=vs: e.matmul(
                                sA[:, vs:vs + 128], ident[:], tri[:],
                                start=False, stop=True))
                        T.op(lambda e, sB=sB, j=j, vs=vs, p=p, qb_ap=qb_ap:
                             e.matmul(sB[:, vs:QCW], kf[64:128, p, 128 * j:128 * (j + 1)],
                                      qb_ap, start=True, stop=not diag,
                                      tile_position=(64, 0)),
                             inc=(None if diag else TE))
                        if diag:
                            T.op(lambda e, sB=sB, vs=vs: e.matmul(
                                sB[:, vs:vs + 128], ident[:], tri[:],
                                start=False, stop=True), inc=TE)
                        sc_te[j] = TE.v

                    def expj(j):
                        par = j % 2
                        vs = max(0, 128 * j - QCW * qc)
                        sA = P[0 + par]
                        sB = P[2 + par]
                        C.wait(TE, max(sc_te[j], att_pv_te[par]))
                        C.op(lambda e, sA=sA, vs=vs, par=par: e.activation(
                            pa[:, par, vs:QCW], sA[:, vs:QCW], AF.Exp, scale=0.125,
                            bias=exp_b), inc=SE)
                        C.op(lambda e, sB=sB, vs=vs, par=par: e.activation(
                            pb[:, par, vs:QCW], sB[:, vs:QCW], AF.Exp, scale=0.125,
                            bias=exp_b), inc=SE)
                        exp_se[j] = SE.v
                        att_se[par] = SE.v

                    def pv(j):
                        par = j % 2
                        vs = max(0, 128 * j - QCW * qc)
                        T.wait(SE, exp_se[j])
                        T.op(lambda e, j=j, vs=vs, p=p, par=par: e.matmul(
                            P[4][0:64, vs:QCW], vt[:, j, 128 * p:128 * p + 64],
                            pa[:, par, vs:QCW], start=(j == 0), stop=(j == njc - 1)))
                        T.op(lambda e, j=j, vs=vs, p=p, par=par: e.matmul(
                            P[5][64:128, vs:QCW], vt[:, j, 128 * p + 64:128 * (p + 1)],
                            pb[:, par, vs:QCW], start=(j == 0), stop=(j == njc - 1),
                            tile_position=(0, 64)))
                        T.op(lambda e, j=j, vs=vs, par=par: e.matmul(
                            P[6][0:1, vs:QCW], onescol[:], pa[:, par, vs:QCW],
                            start=(j == 0), stop=(j == njc - 1)))
                        T.op(lambda e, j=j, vs=vs, par=par: e.matmul(
                            P[6][32:33, vs:QCW], onescol[:], pb[:, par, vs:QCW],
                            start=(j == 0), stop=(j == njc - 1),
                            tile_position=(0, 32)), inc=TE)
                        att_pv_te[par] = TE.v

                    for j in range(njc):
                        scores(j)
                        expj(j)
                        if j > 0:
                            pv(j - 1)
                    pv(njc - 1)
                    C.wait(TE, TE.v)
                    C.op(lambda e: e.copy(densb[0:1, 0:QCW], P[6][0:1, :]))
                    C.op(lambda e: e.copy(densb[32:33, 0:QCW], P[6][32:33, :]),
                         inc=SE)
                    V.wait(SE, SE.v)
                    V.op(lambda e: e.reciprocal(densb[0:1, 0:QCW], densb[0:1, 0:QCW]))
                    V.op(lambda e: e.reciprocal(densb[32:33, 0:QCW],
                                                densb[32:33, 0:QCW]), inc=VE)
                    T.wait(VE, VE.v)
                    T.op(lambda e: e.matmul(P[6][0:64, :], ones1[0:1, 0:64],
                                            densb[0:1, 0:QCW],
                                            start=True, stop=True))
                    T.op(lambda e: e.matmul(P[6][64:128, :], ones_sb[32:33, :],
                                            densb[32:33, 0:QCW],
                                            start=True, stop=True,
                                            tile_position=(32, 64)), inc=TE)
                    V.wait(TE, TE.v)
                    V.op(lambda e: e.tensor_copy(recb[:], P[6][:, :]))
                    V.op(lambda e, p=p: e.tensor_tensor(
                        attw[0:64, p, :], P[4][0:64, :], recb[0:64, :], ALU.mult))
                    V.op(lambda e, p=p: e.tensor_tensor(
                        attw[64:128, p, :], P[5][64:128, :], recb[64:128, :],
                        ALU.mult), inc=VE)
                    T.wait(VE, VE.v)
                    wq_point(s)

                if qc == NQC - 1:
                    wq_drain(s)
                op_ve = {4: VE.v, 5: VE.v}
                ox_ve = {0: VE.v, 1: VE.v}
                for qt in range(4):
                    tglob = 4 * qc + qt
                    par = tglob % 2
                    S.wait(VE, ox_ve[par])
                    S.wait(SE, SE.v)
                    S.dma(xin[:, par, :],
                          x_h[xrow0 + 128 * tglob: xrow0 + 128 * (tglob + 1), :],
                          inc=(LD, 16))
                    for nd in range(2):
                        bank = P[4 + nd]
                        T.wait(VE, op_ve[4 + nd])
                        for c in range(DC):
                            T.op(lambda e, c=c, qt=qt, nd=nd, bank=bank: e.matmul(
                                bank[:, :], attw[:, c, 128 * qt:128 * (qt + 1)],
                                wX[:, c, 512 * nd:512 * (nd + 1)],
                                start=(c == 0), stop=(c == DC - 1)),
                                inc=(TE if c == DC - 1 else None))
                        V.wait(TE, TE.v)
                        V.wait(LD, LD.v)
                        V.op(lambda e, nd=nd, bank=bank: e.tensor_tensor(
                            ot1[:, :], bank[:, :], bc2[:, 512 * nd:512 * (nd + 1)],
                            ALU.mult))
                        V.op(lambda e, nd=nd, tglob=tglob, par=par: e.tensor_tensor(
                            x1[:, tglob, 512 * nd:512 * (nd + 1)], ot1[:, :],
                            xin[:, par, 512 * nd:512 * (nd + 1)], ALU.add), inc=VE)
                        op_ve[4 + nd] = VE.v
                        if nd == 1:
                            ox_ve[par] = VE.v
                T.wait(VE, VE.v)

            if s == 0 and debug_taps:
                tap(attw[:, 0, :], 512)
                tap(pa[:], 512)
                tap(recb[:], 512)
                tap(x1[:, 0, :])
                tap(x1[:, NT - 1, :])
            expand(3, bc0)
            expand(4, bc1)
            expand(5, bc2)

            h1_prev_ln = [0]

            def h1_tile(qc2, qt2, dstbuf):
                tglob = 4 * qc2 + qt2
                C.wait(VE, h1_prev_ln[0])
                V.wait(GE, GE.v)
                h1_prev_ln[0] = ln_norm(x1[:, tglob, :], xn2[:])
                G.wait(VE, VE.v)
                G.op(lambda e: e.tensor_tensor(h1g[:], xn2[:], bc1[:], ALU.mult))
                G.wait(SE, SE.v)
                G.op(lambda e: e.tensor_tensor(h1f[:], h1g[:], bc0[:], ALU.add),
                     inc=GE)
                quant(h1f[:], h1qt[:], D, QF)
                T.wait(VE, VE.v)
                transpose_blocks(
                    lambda b: h1qt[:, 128 * b:128 * (b + 1)], DC,
                    lambda b, qt2=qt2: dstbuf[:, b, 128 * qt2:128 * (qt2 + 1)])

            HOIST = False
            CONSV = False
            h1bufs = [h1qf, h1qf2]
            for qt in range(4):
                h1_tile(0, qt, h1bufs[0])
            h1_se = [SE.v]

            gucnt = [0]
            ve_guard = [VE.v] * 7
            se_guard = [SE.v] * 7
            h2f_ve = [VE.v]
            blk_te = [TE.v, TE.v]
            dp_te = [TE.v, TE.v]
            tp_prev = [None]

            def flush_tp():
                if tp_prev[0] is None:
                    return
                pnoff, pnw, pqt, ppar2, q_ve = tp_prev[0]
                tp_prev[0] = None
                T.wait(VE, q_ve)
                transpose_blocks(
                    lambda b, ppar2=ppar2: h2qt[:, ppar2, 128 * b:128 * (b + 1)],
                    pnw // 128,
                    lambda b, pnoff=pnoff, pqt=pqt: h2qi[:, (pnoff // 128) + b,
                                                         128 * pqt:128 * (pqt + 1)])

            for qc in range(NQC):
                hsrc = h1bufs[qc % 2]
                if not HOIST and qc > 0:
                    for qt in range(4):
                        h1_tile(qc, qt, hsrc)
                    h1_se[0] = SE.v
                for nbi, (noff, nw) in enumerate(cfg.NBL):
                    par = nbi % 2
                    S.wait(TE, blk_te[par])
                    S.dma(wgb[:, par, :, 0:nw], wgq_h[:, :, noff:noff + nw],
                          inc=(LD, 16))
                    S.dma(wub[:, par, :, 0:nw], wuq_h[:, :, noff:noff + nw],
                          inc=(LD, 16))
                    for qt in range(4):
                        k = gucnt[0] % 2
                        gucnt[0] += 1
                        gP = P[k]
                        uP = P[2 + k]
                        T.wait(LD, LD.v)
                        T.wait(VE, max(ve_guard[k], ve_guard[2 + k]))
                        T.wait(SE, max(se_guard[k], h1_se[0]))
                        for kc in range(DC):
                            T.op(lambda e, kc=kc, qt=qt, par=par, nw=nw, gP=gP,
                                 hsrc=hsrc: e.matmul(
                                gP[:, 0:nw], hsrc[:, kc, 128 * qt:128 * (qt + 1)],
                                wgb[:, par, kc, 0:nw],
                                start=(kc == 0), stop=(kc == DC - 1)))
                        for kc in range(DC):
                            T.op(lambda e, kc=kc, qt=qt, par=par, nw=nw, uP=uP,
                                 hsrc=hsrc: e.matmul(
                                uP[:, 0:nw], hsrc[:, kc, 128 * qt:128 * (qt + 1)],
                                wub[:, par, kc, 0:nw],
                                start=(kc == 0), stop=(kc == DC - 1)),
                                inc=(TE if kc == DC - 1 else None))
                        C.wait(TE, TE.v)
                        C.wait(VE, h2f_ve[0])
                        C.op(lambda e, gP=gP, nw=nw: e.activation(
                            silu_s[:, 0:nw], gP[:, 0:nw], AF.Silu), inc=SE)
                        se_guard[k] = SE.v
                        flush_tp()
                        V.wait(SE, se_guard[k])
                        V.op(lambda e, uP=uP, nw=nw: e.tensor_tensor(
                            h2f[:, 0:nw], silu_s[:, 0:nw], uP[:, 0:nw], ALU.mult),
                            inc=VE)
                        ve_guard[2 + k] = VE.v
                        h2f_ve[0] = VE.v
                        par2 = gucnt[0] % 2
                        quant(h2f[:, 0:nw], h2qt[:, par2, 0:nw], nw, QF)
                        tp_prev[0] = (noff, nw, qt, par2, VE.v)
                    blk_te[par] = TE.v
                    if HOIST and qc < NQC - 1 and nbi < 4:
                        h1_tile(qc + 1, nbi, h1bufs[(qc + 1) % 2])
                flush_tp()

                for nd in range(2):
                    for icx in range(IC):
                        par = icx % 2
                        S.wait(TE, dp_te[par])
                        S.dma(wdb[:, par, :], wdq_h[:, icx, 512 * nd:512 * (nd + 1)],
                              inc=(LD, 16))
                        T.wait(LD, LD.v)
                        if icx == 0:
                            T.wait(VE, VE.v)
                            T.wait(SE, SE.v)
                        for qt in range(4):
                            T.op(lambda e, icx=icx, qt=qt, par=par: e.matmul(
                                P[qt][:, :], h2qi[:, icx, 128 * qt:128 * (qt + 1)],
                                wdb[:, par, :],
                                start=(icx == 0), stop=(icx == IC - 1)),
                                inc=(TE if qt == 3 else None))
                        dp_te[par] = TE.v
                    V.wait(TE, TE.v)
                    for qt in range(4):
                        tglob = 4 * qc + qt
                        V.wait(ST, ST.v)
                        V.op(lambda e, qt=qt, nd=nd: e.tensor_tensor(
                            o1f[:, :], P[qt][:, :], bc2[:, 512 * nd:512 * (nd + 1)],
                            ALU.mult))
                        V.op(lambda e, qt=qt, nd=nd, tglob=tglob: e.tensor_tensor(
                            x3s[:, qt, 512 * nd:512 * (nd + 1)], o1f[:, :],
                            x1[:, tglob, 512 * nd:512 * (nd + 1)], ALU.add),
                            inc=VE)
                        ve_guard[qt] = VE.v
                        if nd == 1:
                            G.wait(VE, VE.v)
                            G.dma(y_h[xrow0 + 128 * tglob: xrow0 + 128 * (tglob + 1), :],
                                  x3s[:, qt, :], inc=(ST, 16))

        G.wait(ST, ST.v)

    with nc.Block() as blk:
        @blk.sync
        def _(eng):
            pr.begin_pass("sync", eng, allsems)
            script(pr)

        @blk.gpsimd
        def _(eng):
            pr.begin_pass("gpsimd", eng, allsems)
            script(pr)

        @blk.tensor
        def _(eng):
            pr.begin_pass("tensor", eng, allsems)
            script(pr)

        @blk.vector
        def _(eng):
            pr.begin_pass("vector", eng, allsems)
            script(pr)

        @blk.scalar
        def _(eng):
            pr.begin_pass("scalar", eng, allsems)
            script(pr)

    for c in semctx:
        c.__exit__(None, None, None)
    return nc


def _consts():
    ident = np.eye(128, dtype=ml_dtypes.bfloat16)
    tri = np.zeros((128, 128), np.float32)
    r = np.arange(128)
    tri[r[:, None] > r[None, :]] = NEG_BIG
    tri = tri.astype(ml_dtypes.bfloat16)
    indik = np.zeros((2, 128), np.float32)
    indik[0, 0:64] = 1.0
    indik[1, 64:128] = 1.0
    ones1 = np.ones((1, 128), np.float32)
    return ident, tri, indik, ones1


def make_in_maps(cfg, x, wq, wk, wv, wo, wg, wu, wd, c, wmod, n_cores=8):
    ident, tri, indik, ones1 = _consts()
    shared = dict(
        wq=np.ascontiguousarray(wq, dtype=np.float32),
        wk=np.ascontiguousarray(wk, dtype=np.float32),
        wv=np.ascontiguousarray(wv, dtype=np.float32),
        wo=np.ascontiguousarray(wo, dtype=np.float32),
        wg=np.ascontiguousarray(wg, dtype=np.float32),
        wu=np.ascontiguousarray(wu, dtype=np.float32),
        wd=np.ascontiguousarray(wd, dtype=np.float32),
        wmod=np.ascontiguousarray(wmod, dtype=np.float32),
        ident=ident, tri=tri, indik=indik, ones1=ones1,
    )
    c = np.asarray(c, np.float32)
    x = np.asarray(x, np.float32)
    in_maps = []
    for i in range(n_cores):
        ct = np.ascontiguousarray(
            c[cfg.NSEQ * i:cfg.NSEQ * (i + 1)].T.reshape(DC, 128, cfg.NSEQ)
            .transpose(1, 0, 2))
        xi = np.ascontiguousarray(
            x[cfg.NSEQ * i:cfg.NSEQ * (i + 1)].reshape(cfg.NSEQ * cfg.S, D))
        in_maps.append(dict(shared, x=xi, ct=ct))
    return in_maps


def kernel(L0_attn_x, L0_attn_W_Q, L0_attn_W_K, L0_attn_W_V, L0_attn_W_o,
           L0_ffn_w_gate, L0_ffn_w_up, L0_ffn_w_down, L0_c, L0_wmod):
    cfg = Cfg(S=2048, I=2816)
    nc = build_program(cfg)
    in_maps = make_in_maps(cfg, L0_attn_x, L0_attn_W_Q, L0_attn_W_K, L0_attn_W_V,
                           L0_attn_W_o, L0_ffn_w_gate, L0_ffn_w_up, L0_ffn_w_down,
                           L0_c, L0_wmod)
    res = run_bass_kernel_spmd(nc, in_maps, core_ids=list(range(8)))
    out = np.empty((16, 2048, 1024), np.float32)
    for i in range(8):
        out[2 * i:2 * i + 2] = res.results[i]["y"].reshape(2, cfg.S, D)
    return out


# version 17
# speedup vs baseline: 1.4439x; 1.0030x over previous
import numpy as np
import ml_dtypes

import concourse.bass as bass
import concourse.mybir as mybir
from concourse.bass_utils import run_bass_kernel_spmd

F32 = mybir.dt.float32
BF16 = mybir.dt.bfloat16
I32 = mybir.dt.int32
AF = mybir.ActivationFunctionType
ALU = mybir.AluOpType
AX = mybir.AxisListType

AND_M = int(np.uint32(0xFFC00000).view(np.int32))
OR_M = 0x00200000
EXP_M = 0x7F800000
NEG_BIG = -1.0e9

D = 1024
DC = 8
NPAIR = 8
QCW = 512
EPS = 1e-5


class Cfg:
    def __init__(self, S=2048, I=2816):
        self.S = S
        self.NSEQ = 2
        self.I = I
        self.IC = I // 128
        self.NQC = S // QCW
        self.NT = S // 128
        self.NBL = []
        off = 0
        while off < I:
            w = min(512, I - off)
            self.NBL.append((off, w))
            off += w


class Sem:
    def __init__(self, handle):
        self.h = handle
        self.v = 0


class Eng:
    def __init__(self, prog, name):
        self.prog = prog
        self.name = name

    @property
    def real(self):
        return self.prog.active_eng if self.prog.active == self.name else None

    def _fin(self, ins, inc):
        if inc is not None:
            sem, n = (inc, 1) if isinstance(inc, Sem) else inc
            sem.v += n
            if ins is not None:
                ins.then_inc(sem.h, n)
        return ins

    def wait(self, sem, val=None):
        if val is None:
            val = sem.v
        if val <= 0:
            return
        e = self.real
        if e is not None:
            e.wait_ge(sem.h, val)

    def op(self, fn, inc=None):
        e = self.real
        ins = fn(e) if e is not None else None
        return self._fin(ins, inc)

    def dma(self, out, in_, inc=None):
        return self.op(lambda e: e.dma_start(out=out, in_=in_), inc)


class Prog:
    def __init__(self, nc):
        self.nc = nc
        self.active = None
        self.active_eng = None
        self.S = Eng(self, "sync")
        self.G = Eng(self, "gpsimd")
        self.T = Eng(self, "tensor")
        self.V = Eng(self, "vector")
        self.C = Eng(self, "scalar")

    def begin_pass(self, name, eng, sems):
        self.active = name
        self.active_eng = eng
        for s in sems:
            s.v = 0


def build_program(cfg: Cfg, debug_stop=None, debug_taps=False):
    nc = bass.Bass("TRN2", target_bir_lowering=False, detect_race_conditions=False)
    S_, I_, IC, NQC, NT, NSEQ = cfg.S, cfg.I, cfg.IC, cfg.NQC, cfg.NT, cfg.NSEQ

    dt = nc.dram_tensor
    x_h = dt("x", [NSEQ * S_, D], F32, kind="ExternalInput").ap()
    wq_h = dt("wq", [D, D], F32, kind="ExternalInput").ap()
    wk_h = dt("wk", [D, D], F32, kind="ExternalInput").ap()
    wv_h = dt("wv", [D, D], F32, kind="ExternalInput").ap()
    wo_h = dt("wo", [D, D], F32, kind="ExternalInput").ap()
    wg_h = dt("wg", [I_, D], F32, kind="ExternalInput").ap()
    wu_h = dt("wu", [I_, D], F32, kind="ExternalInput").ap()
    wd_h = dt("wd", [D, I_], F32, kind="ExternalInput").ap()
    ct_h = dt("ct", [128, DC, NSEQ], F32, kind="ExternalInput").ap()
    wmod_h = dt("wmod", [D, 6144], F32, kind="ExternalInput").ap()
    ident_h = dt("ident", [128, 128], BF16, kind="ExternalInput").ap()
    tri_h = dt("tri", [128, 128], BF16, kind="ExternalInput").ap()
    indik_h = dt("indik", [2, 128], F32, kind="ExternalInput").ap()
    ones1_h = dt("ones1", [1, 128], F32, kind="ExternalInput").ap()
    y_h = dt("y", [NSEQ * S_, D], F32, kind="ExternalOutput").ap()
    dbg_h = dt("dbg", [128, 16, 1024], F32, kind="ExternalOutput").ap() if debug_taps else None
    wgq_h = dt("wgq_s", [128, DC, I_], BF16, kind="Internal").ap()
    wuq_h = dt("wuq_s", [128, DC, I_], BF16, kind="Internal").ap()
    wdq_h = dt("wdq_s", [128, IC, D], BF16, kind="Internal").ap()
    mod_h = dt("mod_s", [NSEQ, 6144], F32, kind="Internal").ap()

    SBUF_LIMIT = nc.SBUF_PARTITION_SIZE_BYTES
    cur = [17 * 1024]

    def at(name, shape, dtp, align=32):
        nbytes = int(np.prod(shape[1:])) * mybir.dt.size(dtp)
        cur[0] = (cur[0] + align - 1) // align * align
        t = nc.alloc_sbuf_tensor_at(name, shape, dtp, offset=cur[0])
        cur[0] += nbytes
        return t

    ident = at("ident", [128, 128], BF16)
    tri = at("tri", [128, 128], BF16)
    indik = at("indik", [2, 128], F32)
    ones1 = at("ones1", [1, 128], F32)
    ct = at("ct", [128, DC, NSEQ], F32)
    vrow = at("vrow", [NSEQ, 1024], F32)
    bc0 = at("bc0", [128, D], F32)
    bc1 = at("bc1", [128, D], F32)
    bc2 = at("bc2", [128, D], F32)
    ln_s = at("ln_s", [128, 10], F32)
    onescol = at("onescol", [128, 1], BF16)
    ones_sb = at("ones_sb", [128, 64], F32)
    act_scr = at("act_scr", [128, D], F32)
    xin = at("xin", [128, 2, D], F32)
    xn_t = at("xn_t", [128, D], BF16)
    x1_off = (cur[0] + 31) // 32 * 32
    x1 = at("x1", [128, NT, D], BF16)
    shared_end = cur[0]

    ov0 = (shared_end + 31) // 32 * 32

    cur[0] = ov0
    stW = at("stW", [128, 2, 3072], F32)
    assert cur[0] <= SBUF_LIMIT, f"phase-W overlay over budget: {cur[0]}"

    cur[0] = ov0
    h0 = at("h0", [128, DC, S_], BF16)
    qf = nc.alloc_sbuf_tensor_at("qf", [128, DC, S_], BF16, offset=ov0)
    wX = at("wX", [128, DC, D], BF16)
    kf = at("kf", [128, DC, S_], BF16)
    vt = at("vt", [128, NT, 16 * 64], BF16)
    attw_off = (cur[0] + 31) // 32 * 32
    attw = at("attw", [128, DC, QCW], BF16)
    pa = at("pa", [128, 2, QCW], BF16)
    pb = at("pb", [128, 2, QCW], BF16)
    densb = at("densb", [128, QCW], F32)
    recb = at("recb", [128, QCW], F32)
    ot1 = recb
    xn_u = nc.alloc_sbuf_tensor_at("xn_u", [128, D], BF16, offset=attw_off)
    xnn = nc.alloc_sbuf_tensor_at("xnn", [128, D], F32, offset=attw_off + 2048)
    att_end = cur[0]
    assert cur[0] <= SBUF_LIMIT, f"phase-ATT overlay over budget: {cur[0]}"

    cur[0] = ov0
    qU = at("qU", [128, 1024], BF16)
    qX = at("qX", [128, 1024], BF16)
    qM1 = at("qM1", [128, 1024], BF16)
    qSG = at("qSG", [128, 1024], BF16)
    qsm = at("qsm", [128, 3, 64], F32)
    qsb = at("qsb", [128, 64], BF16)
    xn2 = at("xn2", [128, D], F32)
    h1g = at("h1g", [128, D], F32)
    h1f = at("h1f", [128, D], F32)
    h1qt = at("h1qt", [128, D], BF16)
    h1qf = at("h1qf", [128, DC, QCW], BF16)
    wgb = at("wgb", [128, 2, DC, 512], BF16)
    wub = at("wub", [128, 2, DC, 512], BF16)
    wdb = at("wdb", [128, 2, 512], BF16)
    silu_s = at("silu_s", [128, 512], F32)
    h2f = at("h2f", [128, 512], F32)
    h2qt = at("h2qt", [128, 2, 512], BF16)
    h2qi = at("h2qi", [128, IC, QCW], BF16)
    o1f = at("o1f", [128, 512], F32)
    x3s = at("x3s", [128, 4, D], F32)
    ffn_end = cur[0]
    assert cur[0] <= SBUF_LIMIT, f"phase-FFN overlay over budget: {cur[0]}"

    cur[0] = max(att_end, ffn_end)
    wreg0 = (cur[0] + 31) // 32 * 32
    stW2 = at("stW2", [128, 2, 1024], F32)
    h1qf2 = nc.alloc_sbuf_tensor_at("h1qf2", [128, DC, QCW], BF16, offset=wreg0)
    qbf2 = at("qbf2", [128, 2, 1024], BF16)
    tstage2 = at("tstage2", [128, 1024], BF16)
    qsm_w = at("qsm_w", [128, 3, 64], F32)
    qsb_w = at("qsb_w", [128, 64], BF16)
    assert cur[0] <= SBUF_LIMIT, f"W-quant region over budget: {cur[0]}"
    qU_w = nc.alloc_sbuf_tensor_at("qU_w", [128, 1024], BF16, offset=x1_off + 24576)
    qX_w = nc.alloc_sbuf_tensor_at("qX_w", [128, 1024], BF16, offset=x1_off + 26624)
    qM1_w = nc.alloc_sbuf_tensor_at("qM1_w", [128, 1024], BF16, offset=x1_off + 28672)
    qSG_w = nc.alloc_sbuf_tensor_at("qSG_w", [128, 1024], BF16, offset=x1_off + 30720)

    P = [nc.alloc_psum_tensor(f"P{k}", [128, 512], F32) for k in range(7)]
    T1 = nc.alloc_psum_tensor("T1", [128, 512], BF16)
    Ttile = [T1[:, 0:128], T1[:, 128:256]]

    semctx = [nc.semaphore(nm) for nm in ["LD", "ST", "TE", "VE", "SE", "GE"]]
    handles = [c.__enter__() for c in semctx]
    LD, ST, TE, VE, SE, GE = [Sem(h) for h in handles]
    allsems = [LD, ST, TE, VE, SE, GE]
    pr = Prog(nc)

    (ln_sx, ln_m2, ln_nm1, ln_nvar, ln_is, ln_std, eps_b, exp_b, zcol,
     jcol) = (ln_s[:, k:k + 1] for k in range(10))

    def script(pr: Prog):
        S, G, T, V, C = pr.S, pr.G, pr.T, pr.V, pr.C
        dbg_slot = [0]

        def tap(ap, cols=1024):
            if not debug_taps:
                return
            k = dbg_slot[0]
            dbg_slot[0] += 1
            G.wait(VE, VE.v)
            G.wait(SE, SE.v)
            G.wait(TE, TE.v)
            G.wait(GE, GE.v)
            G.dma(dbg_h[:, k, 0:cols], ap, inc=(ST, 16))

        def v3(ap2, w):
            return ap2.rearrange("p (nb b) -> p nb b", b=16)

        def quant(src, dst, w, Q, te_guard=None):
            (U, A_, B_, SG, SM, SB) = Q
            nb = w // 16
            C.wait(GE, GE.v)
            C.wait(VE, VE.v)
            C.op(lambda e: e.activation(SG[:, 0:w], src, AF.Sign), inc=SE)
            se_sgn = SE.v
            m_, s_, r_ = (SM[:, k, 0:nb] for k in range(3))
            sb_ = SB[:, 0:nb]
            V.wait(GE, GE.v)
            hv = nb // 2
            halves = [(0, hv), (hv, nb)] if (2 <= hv < 128) else [(0, nb)]
            srcv = v3(src, w)

            def forall(fn):
                for (a, b) in halves:
                    V.op(lambda e, a=a, b=b: fn(e, slice(a, b)))

            forall(lambda e, sl: e.tensor_reduce(
                m_[:, sl], srcv[:, sl, :], AX.X, ALU.max,
                apply_absolute_value=True))
            forall(lambda e, sl: e.tensor_scalar(s_[:, sl], m_[:, sl], 1.0 / 6.0,
                                                 1e-12, ALU.mult, ALU.max))
            forall(lambda e, sl: e.reciprocal(r_[:, sl], s_[:, sl]))
            forall(lambda e, sl: e.tensor_copy(sb_[:, sl], s_[:, sl]))
            rb = r_.unsqueeze(2).broadcast_to([128, nb, 16])
            sbb = sb_.unsqueeze(2).broadcast_to([128, nb, 16])
            u = U[:, 0:w]
            u3 = v3(u, w)
            V.op(lambda e: e.tensor_tensor(u3, srcv, rb, ALU.mult))
            V.wait(SE, se_sgn)
            V.op(lambda e: e.tensor_tensor(u, u, SG[:, 0:w], ALU.mult))
            Aw, Bw = A_[:, 0:w], B_[:, 0:w]
            V.op(lambda e: e.tensor_scalar(Aw, u, 0.25, None, ALU.is_gt))
            V.op(lambda e: e.scalar_tensor_tensor(Aw, u, 0.75, Aw, ALU.is_gt, ALU.add))
            V.op(lambda e: e.scalar_tensor_tensor(Aw, u, 1.25, Aw, ALU.is_gt, ALU.add))
            V.op(lambda e: e.scalar_tensor_tensor(Aw, u, 1.75, Aw, ALU.is_gt, ALU.add))
            V.op(lambda e: e.tensor_scalar(Bw, u, 5.0, 2.0, ALU.is_gt, ALU.mult))
            V.op(lambda e: e.scalar_tensor_tensor(Bw, u, 2.5, Bw, ALU.is_gt, ALU.add))
            V.op(lambda e: e.scalar_tensor_tensor(Bw, u, 3.5, Bw, ALU.is_gt, ALU.add))
            V.op(lambda e: e.scalar_tensor_tensor(Aw, Aw, 0.5, Bw, ALU.mult, ALU.add))
            V.op(lambda e: e.tensor_tensor(Aw, Aw, SG[:, 0:w], ALU.mult))
            if te_guard is not None:
                V.wait(TE, te_guard)
            V.op(lambda e: e.tensor_tensor(v3(dst, w), v3(Aw, w), sbb, ALU.mult),
                 inc=VE)

        QW = (qU_w, qX_w, qM1_w, qSG_w, qsm_w, qsb_w)
        QF = (qU, qX, qM1, qSG, qsm, qsb)

        def transpose_blocks(src_fn, nblk, dst_fn):
            for b in range(nblk):
                par = b % 2
                T.wait(SE, SE.v)
                T.op(lambda e, b=b, par=par: e.transpose(Ttile[par], src_fn(b), ident[:]),
                     inc=TE)
                C.wait(TE, TE.v)
                C.op(lambda e, b=b, par=par: e.copy(dst_fn(b), Ttile[par]), inc=SE)

        def ln_norm(x_ap, out_ap):
            C.op(lambda e: e.activation(act_scr[:], x_ap, AF.Copy, accum_out=ln_sx))
            C.op(lambda e: e.activation(act_scr[:], x_ap, AF.Square,
                                        scale=1.0 / 32.0, bias=zcol,
                                        accum_out=ln_m2))
            C.op(lambda e: e.activation(ln_nm1, ln_sx, AF.Identity,
                                        scale=-1.0 / D, bias=zcol), inc=SE)
            V.wait(SE, SE.v)
            V.op(lambda e: e.scalar_tensor_tensor(ln_nvar, ln_nm1, ln_nm1, ln_m2,
                                                  ALU.mult, ALU.subtract), inc=VE)
            C.wait(VE, VE.v)
            C.op(lambda e: e.activation(ln_std, ln_nvar, AF.Sqrt, scale=-1.0,
                                        bias=eps_b), inc=SE)
            V.wait(SE, SE.v)
            V.op(lambda e: e.reciprocal(ln_is, ln_std))
            V.op(lambda e: e.memset(jcol, 0.0))
            V.op(lambda e: e.tensor_scalar(out_ap, x_ap, ln_nm1, ln_is,
                                           ALU.add, ALU.mult), inc=VE)
            return VE.v

        for dst, src in [(ident[:], ident_h), (tri[:], tri_h), (indik[:], indik_h),
                         (ones1[:], ones1_h), (ct[:], ct_h)]:
            S.dma(dst, src, inc=(LD, 16))
        V.op(lambda e: e.memset(eps_b, EPS))
        V.op(lambda e: e.memset(zcol, 0.0))
        V.op(lambda e: e.memset(jcol, 0.0))
        V.op(lambda e: e.memset(onescol[:], 1.0))
        V.op(lambda e: e.memset(ones_sb[:], 1.0))
        V.op(lambda e: e.memset(exp_b, -6.0), inc=VE)
        C.wait(VE, VE.v)
        C.wait(LD, LD.v)
        ctf = ct[:].rearrange("p c n -> p (c n)")
        C.op(lambda e: e.activation(ctf, ctf, AF.Silu), inc=SE)

        for c in range(DC):
            if c >= 1:
                S.wait(TE, TE.v)
            S.dma(stW[:, 0, :], wmod_h[128 * c:128 * (c + 1), 0:3072], inc=(LD, 16))
            S.dma(stW[:, 1, :], wmod_h[128 * c:128 * (c + 1), 3072:6144], inc=(LD, 16))
            T.wait(LD, LD.v)
            T.wait(SE, SE.v)
            for h in range(2):
                for n in range(6):
                    col = 3072 * h + 512 * n
                    tp = (0, 0) if h == 0 else (0, 32)
                    rows = slice(0, NSEQ) if h == 0 else slice(32, 32 + NSEQ)
                    last = (h == 1 and n == 5)
                    T.op(lambda e, col=col, tp=tp, rows=rows, c=c: e.matmul(
                        P[n][rows, :], ct[:, c, :],
                        stW[:, col // 3072, (col % 3072):(col % 3072) + 512],
                        start=(c == 0), stop=(c == DC - 1), tile_position=tp),
                        inc=(TE if last else None))
        V.wait(TE, TE.v)
        for k in range(6):
            h = k // 3
            n0 = (2 * k) % 6
            rows = slice(0, NSEQ) if h == 0 else slice(32, 32 + NSEQ)
            r0 = 0 if h == 0 else 32
            buf = act_scr[r0:r0 + NSEQ, :]
            tbf = xin[r0:r0 + NSEQ, 0, 0:1024]
            if k >= 1:
                V.wait(ST, ST.v)
            V.op(lambda e, n0=n0, rows=rows, buf=buf: e.tensor_copy(
                buf[:, 0:512], P[n0][rows, :]))
            V.op(lambda e, n0=n0, rows=rows, buf=buf: e.tensor_copy(
                buf[:, 512:1024], P[n0 + 1][rows, :]))
            if k in (1, 4):
                V.op(lambda e, buf=buf: e.tensor_scalar(buf, buf, 1.0, None, ALU.add))
            if k in (2, 5):
                V.op(lambda e, buf=buf, tbf=tbf: e.tensor_scalar(
                    tbf, buf, 0.0, 2.0, ALU.is_ge, ALU.mult))
                V.op(lambda e, tbf=tbf: e.tensor_scalar(tbf, tbf, -1.0, None, ALU.add))
                V.op(lambda e, buf=buf, tbf=tbf: e.tensor_tensor(buf, buf, tbf,
                                                                 ALU.mult))
                V.op(lambda e, buf=buf: e.tensor_scalar(buf, buf, 0.25, None, ALU.max))
                V.op(lambda e, buf=buf, tbf=tbf: e.tensor_tensor(buf, buf, tbf, ALU.mult))
            V.op(lambda e, buf=buf: e.tensor_scalar(buf, buf, 0.0, None, ALU.add),
                 inc=VE)
            G.wait(VE, VE.v)
            G.dma(mod_h[:, 1024 * k:1024 * (k + 1)], buf, inc=(ST, 16))

        if debug_stop == "mod":
            G.wait(ST, ST.v)
            return
        wchunks = []
        for icx in range(cfg.IC):
            wchunks.append((wg_h, 128 * icx, 0, D, wgq_h))
        for icx in range(cfg.IC):
            wchunks.append((wu_h, 128 * icx, 0, D, wuq_h))
        for hc in range(DC):
            for coff, cwid in ((0, 1024), (1024, 1024), (2048, 768)):
                wchunks.append((wd_h, 128 * hc, coff, cwid, wdq_h))
        wstate = {"li": 0, "ti": 0, "te_qbf": [0, 0]}

        def w_load_quant():
            i = wstate["li"]
            if i >= len(wchunks):
                return
            wstate["li"] += 1
            src_h, r0, c0, cw, dst_h = wchunks[i]
            par = i % 2
            S.wait(VE, VE.v)
            S.wait(SE, SE.v)
            S.dma(stW2[:, par, 0:cw], src_h[r0:r0 + 128, c0:c0 + cw],
                  inc=(LD, 16))
            V.wait(LD, LD.v)
            C.wait(LD, LD.v)
            quant(stW2[:, par, 0:cw], qbf2[:, par, 0:cw], cw, QW,
                  te_guard=wstate["te_qbf"][par])

        def w_transpose_store():
            j = wstate["ti"]
            if j >= wstate["li"]:
                return
            wstate["ti"] += 1
            src_h, r0, c0, cw, dst_h = wchunks[j]
            par = j % 2
            nblk = cw // 128
            T.wait(VE, VE.v)
            T.wait(ST, ST.v)
            transpose_blocks(
                lambda b, par=par: qbf2[:, par, 128 * b:128 * (b + 1)], nblk,
                lambda b: tstage2[:, 128 * b:128 * (b + 1)])
            wstate["te_qbf"][par] = TE.v
            G.wait(SE, SE.v)
            if dst_h is wdq_h:
                dst_ap = dst_h[:, (c0 // 128):(c0 // 128) + nblk, r0:r0 + 128]
            else:
                dst_ap = dst_h[:, :, r0:r0 + 128]
            G.dma(dst_ap,
                  tstage2[:, 0:cw].rearrange("p (c w) -> p c w", w=128),
                  inc=(ST, 16))

        def wq_point(s):
            if s != 0:
                return
            w_transpose_store()
            w_load_quant()

        def wq_drain(s):
            if s != 0:
                return
            while wstate["ti"] < len(wchunks):
                w_transpose_store()
                w_load_quant()

        for s in range(NSEQ):
            xrow0 = s * S_

            def expand(k, dstb):
                S.wait(TE, TE.v)
                S.wait(ST, ST.v)
                S.dma(vrow[0:1, :], mod_h[s:s + 1, 1024 * k:1024 * (k + 1)],
                      inc=(LD, 16))
                T.wait(LD, LD.v)
                for half in range(2):
                    T.wait(VE, VE.v)
                    T.op(lambda e, half=half: e.matmul(
                        P[6][:, :], ones1[:], vrow[0:1, 512 * half:512 * half + 512],
                        start=True, stop=True), inc=TE)
                    V.wait(TE, TE.v)
                    V.op(lambda e, half=half: e.tensor_copy(
                        dstb[:, 512 * half:512 * half + 512], P[6][:, :]), inc=VE)

            expand(0, bc0)
            expand(1, bc1)
            expand(2, bc2)
            if s == 0 and debug_taps:
                tap(bc0[:]); tap(bc1[:]); tap(bc2[:])

            ln_prev = [0]
            xin_se = [SE.v, SE.v]
            xin_ve = [VE.v, VE.v]
            for t in range(NT):
                par = t % 2
                S.wait(SE, xin_se[par])
                S.wait(VE, xin_ve[par])
                S.dma(xin[:, par, :], x_h[xrow0 + 128 * t: xrow0 + 128 * (t + 1), :],
                      inc=(LD, 16))
                C.wait(LD, LD.v)
                C.wait(VE, ln_prev[0])
                V.wait(GE, GE.v)
                ln_prev[0] = ln_norm(xin[:, par, :], xnn[:])
                xin_se[par] = SE.v
                xin_ve[par] = VE.v
                G.wait(VE, VE.v)
                G.op(lambda e: e.tensor_tensor(xn_u[:], xnn[:], bc1[:], ALU.mult))
                G.wait(TE, TE.v)
                G.op(lambda e: e.tensor_tensor(xn_t[:], xn_u[:], bc0[:], ALU.add),
                     inc=GE)
                T.wait(GE, GE.v)
                if s == 0 and t == 0 and debug_taps:
                    tap(xn_t[:])
                transpose_blocks(
                    lambda b: xn_t[:, 128 * b:128 * (b + 1)], DC,
                    lambda b, t=t: h0[:, b, 128 * t:128 * (t + 1)])
                wq_point(s)

            def load_w(w_h, dst):
                G.wait(TE, TE.v)
                G.dma(dst[:], w_h.rearrange("(c p) w -> p c w", p=128),
                      inc=(ST, 16))

            def proj_block(wbuf, nq, dst_col_fn):
                T.wait(SE, SE.v)
                for wave in ([0, 1, 2, 3, 4, 5, 6], [7]):
                    for mc in wave:
                        bank = P[mc % 7]
                        if mc == 7:
                            C.wait(TE, TE.v)
                            for mc2 in [0, 1, 2, 3, 4, 5, 6]:
                                C.op(lambda e, mc2=mc2, nq=nq: e.copy(
                                    dst_col_fn(mc2, nq), P[mc2][:, :]), inc=SE)
                            T.wait(SE, SE.v)
                        for kc in range(DC):
                            T.op(lambda e, kc=kc, mc=mc, nq=nq, bank=bank: e.matmul(
                                bank[:, :], wbuf[:, kc, 128 * mc:128 * (mc + 1)],
                                h0[:, kc, 512 * nq:512 * (nq + 1)],
                                start=(kc == 0), stop=(kc == DC - 1)),
                                inc=(TE if kc == DC - 1 else None))
                C.wait(TE, TE.v)
                C.op(lambda e, nq=nq: e.copy(dst_col_fn(7, nq), P[0][:, :]), inc=SE)

            def proj_block_q(wbuf, nq):
                T.wait(SE, SE.v)
                T.wait(VE, VE.v)
                for mc in range(DC):
                    bank = P[mc % 7]
                    if mc == 7:
                        C.wait(TE, TE.v)
                        for mc2 in range(7):
                            C.op(lambda e, mc2=mc2: e.copy(
                                attw[:, mc2, :], P[mc2][:, :]), inc=SE)
                        T.wait(SE, SE.v)
                    for kc in range(DC):
                        T.op(lambda e, kc=kc, mc=mc, nq=nq, bank=bank: e.matmul(
                            bank[:, :], wbuf[:, kc, 128 * mc:128 * (mc + 1)],
                            h0[:, kc, 512 * nq:512 * (nq + 1)],
                            start=(kc == 0), stop=(kc == DC - 1)),
                            inc=(TE if kc == DC - 1 else None))
                C.wait(TE, TE.v)
                C.op(lambda e: e.copy(attw[:, 7, :], P[0][:, :]), inc=SE)
                V.wait(SE, SE.v)
                V.op(lambda e, nq=nq: e.tensor_copy(
                    qf[:, :, 512 * nq:512 * (nq + 1)], attw[:]), inc=VE)

            if s == 0 and debug_taps:
                tap(h0[:, 0, 0:1024])
            load_w(wk_h, wX)
            if s == 0 and debug_taps:
                tap(wX[:, 0, :])
            T.wait(ST, ST.v)
            for nq in range(S_ // 512):
                proj_block(wX, nq, lambda mc, nq: kf[:, mc, 512 * nq:512 * (nq + 1)])
                wq_point(s)
                wq_point(s)

            if s == 0 and debug_taps:
                tap(kf[:, 0, 0:1024])
            load_w(wv_h, wX)
            T.wait(ST, ST.v)
            for t in range(NT):
                for nd in range(2):
                    bank = P[(2 * t + nd) % 7]
                    if t * 2 + nd >= 7:
                        T.wait(SE, SE.v)
                    for kc in range(DC):
                        T.op(lambda e, kc=kc, t=t, nd=nd, bank=bank: e.matmul(
                            bank[:, :], h0[:, kc, 128 * t:128 * (t + 1)],
                            wX[:, kc, 512 * nd:512 * (nd + 1)],
                            start=(kc == 0), stop=(kc == DC - 1)),
                            inc=(TE if kc == DC - 1 else None))
                    C.wait(TE, TE.v)
                    C.op(lambda e, t=t, nd=nd, bank=bank: e.copy(
                        vt[:, t, 512 * nd:512 * (nd + 1)], bank[:, :]), inc=SE)
                if t % 2 == 1:
                    wq_point(s)
            T.wait(SE, SE.v)

            if s == 0 and debug_taps:
                tap(vt[:, 0, :])
            load_w(wq_h, wX)
            T.wait(ST, ST.v)
            for nq in range(S_ // 512):
                proj_block_q(wX, nq)
                wq_point(s)
                wq_point(s)

            if s == 0 and debug_taps:
                tap(qf[:, 0, 0:1024])
            load_w(wo_h, wX)
            T.wait(ST, ST.v)

            att_se = [SE.v, SE.v]
            att_pv_te = [TE.v, TE.v]
            den_ve = [VE.v]

            def denom(pp):
                C.wait(TE, TE.v)
                C.op(lambda e: e.copy(densb[0:1, 0:QCW], P[6][0:1, :]))
                C.op(lambda e: e.copy(densb[32:33, 0:QCW], P[6][32:33, :]),
                     inc=SE)
                V.wait(SE, SE.v)
                V.op(lambda e: e.reciprocal(densb[0:1, 0:QCW], densb[0:1, 0:QCW]))
                V.op(lambda e: e.reciprocal(densb[32:33, 0:QCW],
                                            densb[32:33, 0:QCW]), inc=VE)
                T.wait(VE, VE.v)
                T.op(lambda e: e.matmul(P[6][0:64, :], ones1[0:1, 0:64],
                                        densb[0:1, 0:QCW],
                                        start=True, stop=True))
                T.op(lambda e: e.matmul(P[6][64:128, :], ones_sb[32:33, :],
                                        densb[32:33, 0:QCW],
                                        start=True, stop=True,
                                        tile_position=(32, 64)), inc=TE)
                V.wait(TE, TE.v)
                V.op(lambda e: e.tensor_copy(recb[:], P[6][:, :]))
                V.op(lambda e, pp=pp: e.tensor_tensor(
                    attw[0:64, pp, :], P[4][0:64, :], recb[0:64, :], ALU.mult))
                V.op(lambda e, pp=pp: e.tensor_tensor(
                    attw[64:128, pp, :], P[5][64:128, :], recb[64:128, :],
                    ALU.mult), inc=VE)
                den_ve[0] = VE.v
                wq_point(s)

            for qc in range(NQC):
                njc = 4 * qc + 4
                for p in range(NPAIR):
                    exp_se = {}
                    sc_te = {}

                    def scores(j):
                        par = j % 2
                        vs = max(0, 128 * j - QCW * qc)
                        diag = (128 * j >= QCW * qc)
                        sA = P[0 + par]
                        sB = P[2 + par]
                        T.wait(SE, att_se[par])
                        q_ap = qf[0:64, p, QCW * qc + vs: QCW * (qc + 1)]
                        qb_ap = qf[64:128, p, QCW * qc + vs: QCW * (qc + 1)]
                        T.op(lambda e, sA=sA, j=j, vs=vs, p=p, q_ap=q_ap:
                             e.matmul(sA[:, vs:QCW], kf[0:64, p, 128 * j:128 * (j + 1)],
                                      q_ap, start=True, stop=not diag,
                                      tile_position=(0, 0)))
                        if diag:
                            T.op(lambda e, sA=sA, vs=vs: e.matmul(
                                sA[:, vs:vs + 128], ident[:], tri[:],
                                start=False, stop=True))
                        T.op(lambda e, sB=sB, j=j, vs=vs, p=p, qb_ap=qb_ap:
                             e.matmul(sB[:, vs:QCW], kf[64:128, p, 128 * j:128 * (j + 1)],
                                      qb_ap, start=True, stop=not diag,
                                      tile_position=(64, 0)),
                             inc=(None if diag else TE))
                        if diag:
                            T.op(lambda e, sB=sB, vs=vs: e.matmul(
                                sB[:, vs:vs + 128], ident[:], tri[:],
                                start=False, stop=True), inc=TE)
                        sc_te[j] = TE.v

                    def expj(j):
                        par = j % 2
                        vs = max(0, 128 * j - QCW * qc)
                        sA = P[0 + par]
                        sB = P[2 + par]
                        C.wait(TE, max(sc_te[j], att_pv_te[par]))
                        C.op(lambda e, sA=sA, vs=vs, par=par: e.activation(
                            pa[:, par, vs:QCW], sA[:, vs:QCW], AF.Exp, scale=0.125,
                            bias=exp_b), inc=SE)
                        C.op(lambda e, sB=sB, vs=vs, par=par: e.activation(
                            pb[:, par, vs:QCW], sB[:, vs:QCW], AF.Exp, scale=0.125,
                            bias=exp_b), inc=SE)
                        exp_se[j] = SE.v
                        att_se[par] = SE.v

                    def pv(j):
                        par = j % 2
                        vs = max(0, 128 * j - QCW * qc)
                        if j == 0:
                            T.wait(VE, den_ve[0])
                        T.wait(SE, exp_se[j])
                        T.op(lambda e, j=j, vs=vs, p=p, par=par: e.matmul(
                            P[4][0:64, vs:QCW], vt[:, j, 128 * p:128 * p + 64],
                            pa[:, par, vs:QCW], start=(j == 0), stop=(j == njc - 1)))
                        T.op(lambda e, j=j, vs=vs, p=p, par=par: e.matmul(
                            P[5][64:128, vs:QCW], vt[:, j, 128 * p + 64:128 * (p + 1)],
                            pb[:, par, vs:QCW], start=(j == 0), stop=(j == njc - 1),
                            tile_position=(0, 64)))
                        T.op(lambda e, j=j, vs=vs, par=par: e.matmul(
                            P[6][0:1, vs:QCW], onescol[:], pa[:, par, vs:QCW],
                            start=(j == 0), stop=(j == njc - 1)))
                        T.op(lambda e, j=j, vs=vs, par=par: e.matmul(
                            P[6][32:33, vs:QCW], onescol[:], pb[:, par, vs:QCW],
                            start=(j == 0), stop=(j == njc - 1),
                            tile_position=(0, 32)), inc=TE)
                        att_pv_te[par] = TE.v

                    for j in range(njc):
                        scores(j)
                        expj(j)
                        if j == 0 and p > 0:
                            denom(p - 1)
                        if j > 0:
                            pv(j - 1)
                    pv(njc - 1)
                    if p == NPAIR - 1:
                        denom(p)

                if qc == NQC - 1:
                    wq_drain(s)
                op_ve = {4: VE.v, 5: VE.v}
                ox_ve = {0: VE.v, 1: VE.v}
                for qt in range(4):
                    tglob = 4 * qc + qt
                    par = tglob % 2
                    S.wait(VE, ox_ve[par])
                    S.wait(SE, SE.v)
                    S.dma(xin[:, par, :],
                          x_h[xrow0 + 128 * tglob: xrow0 + 128 * (tglob + 1), :],
                          inc=(LD, 16))
                    for nd in range(2):
                        bank = P[4 + nd]
                        T.wait(VE, op_ve[4 + nd])
                        for c in range(DC):
                            T.op(lambda e, c=c, qt=qt, nd=nd, bank=bank: e.matmul(
                                bank[:, :], attw[:, c, 128 * qt:128 * (qt + 1)],
                                wX[:, c, 512 * nd:512 * (nd + 1)],
                                start=(c == 0), stop=(c == DC - 1)),
                                inc=(TE if c == DC - 1 else None))
                        V.wait(TE, TE.v)
                        V.wait(LD, LD.v)
                        V.op(lambda e, nd=nd, bank=bank: e.tensor_tensor(
                            ot1[:, :], bank[:, :], bc2[:, 512 * nd:512 * (nd + 1)],
                            ALU.mult))
                        V.op(lambda e, nd=nd, tglob=tglob, par=par: e.tensor_tensor(
                            x1[:, tglob, 512 * nd:512 * (nd + 1)], ot1[:, :],
                            xin[:, par, 512 * nd:512 * (nd + 1)], ALU.add), inc=VE)
                        op_ve[4 + nd] = VE.v
                        if nd == 1:
                            ox_ve[par] = VE.v
                T.wait(VE, VE.v)

            if s == 0 and debug_taps:
                tap(attw[:, 0, :], 512)
                tap(pa[:], 512)
                tap(recb[:], 512)
                tap(x1[:, 0, :])
                tap(x1[:, NT - 1, :])
            expand(3, bc0)
            expand(4, bc1)
            expand(5, bc2)

            h1_prev_ln = [0]

            def h1_tile(qc2, qt2, dstbuf):
                tglob = 4 * qc2 + qt2
                C.wait(VE, h1_prev_ln[0])
                V.wait(GE, GE.v)
                h1_prev_ln[0] = ln_norm(x1[:, tglob, :], xn2[:])
                G.wait(VE, VE.v)
                G.op(lambda e: e.tensor_tensor(h1g[:], xn2[:], bc1[:], ALU.mult))
                G.wait(SE, SE.v)
                G.op(lambda e: e.tensor_tensor(h1f[:], h1g[:], bc0[:], ALU.add),
                     inc=GE)
                quant(h1f[:], h1qt[:], D, QF)
                T.wait(VE, VE.v)
                transpose_blocks(
                    lambda b: h1qt[:, 128 * b:128 * (b + 1)], DC,
                    lambda b, qt2=qt2: dstbuf[:, b, 128 * qt2:128 * (qt2 + 1)])

            HOIST = False
            CONSV = False
            h1bufs = [h1qf, h1qf2]
            for qt in range(4):
                h1_tile(0, qt, h1bufs[0])
            h1_se = [SE.v]

            gucnt = [0]
            ve_guard = [VE.v] * 7
            se_guard = [SE.v] * 7
            h2f_ve = [VE.v]
            blk_te = [TE.v, TE.v]
            dp_te = [TE.v, TE.v]
            tp_prev = [None]

            def flush_tp():
                if tp_prev[0] is None:
                    return
                pnoff, pnw, pqt, ppar2, q_ve = tp_prev[0]
                tp_prev[0] = None
                T.wait(VE, q_ve)
                transpose_blocks(
                    lambda b, ppar2=ppar2: h2qt[:, ppar2, 128 * b:128 * (b + 1)],
                    pnw // 128,
                    lambda b, pnoff=pnoff, pqt=pqt: h2qi[:, (pnoff // 128) + b,
                                                         128 * pqt:128 * (pqt + 1)])

            for qc in range(NQC):
                hsrc = h1bufs[qc % 2]
                if not HOIST and qc > 0:
                    for qt in range(4):
                        h1_tile(qc, qt, hsrc)
                    h1_se[0] = SE.v
                for nbi, (noff, nw) in enumerate(cfg.NBL):
                    par = nbi % 2
                    S.wait(TE, blk_te[par])
                    S.dma(wgb[:, par, :, 0:nw], wgq_h[:, :, noff:noff + nw],
                          inc=(LD, 16))
                    S.dma(wub[:, par, :, 0:nw], wuq_h[:, :, noff:noff + nw],
                          inc=(LD, 16))
                    for qt in range(4):
                        k = gucnt[0] % 2
                        gucnt[0] += 1
                        gP = P[k]
                        uP = P[2 + k]
                        T.wait(LD, LD.v)
                        T.wait(VE, max(ve_guard[k], ve_guard[2 + k]))
                        T.wait(SE, max(se_guard[k], h1_se[0]))
                        for kc in range(DC):
                            T.op(lambda e, kc=kc, qt=qt, par=par, nw=nw, gP=gP,
                                 hsrc=hsrc: e.matmul(
                                gP[:, 0:nw], hsrc[:, kc, 128 * qt:128 * (qt + 1)],
                                wgb[:, par, kc, 0:nw],
                                start=(kc == 0), stop=(kc == DC - 1)))
                        for kc in range(DC):
                            T.op(lambda e, kc=kc, qt=qt, par=par, nw=nw, uP=uP,
                                 hsrc=hsrc: e.matmul(
                                uP[:, 0:nw], hsrc[:, kc, 128 * qt:128 * (qt + 1)],
                                wub[:, par, kc, 0:nw],
                                start=(kc == 0), stop=(kc == DC - 1)),
                                inc=(TE if kc == DC - 1 else None))
                        C.wait(TE, TE.v)
                        C.wait(VE, h2f_ve[0])
                        C.op(lambda e, gP=gP, nw=nw: e.activation(
                            silu_s[:, 0:nw], gP[:, 0:nw], AF.Silu), inc=SE)
                        se_guard[k] = SE.v
                        flush_tp()
                        V.wait(SE, se_guard[k])
                        V.op(lambda e, uP=uP, nw=nw: e.tensor_tensor(
                            h2f[:, 0:nw], silu_s[:, 0:nw], uP[:, 0:nw], ALU.mult),
                            inc=VE)
                        ve_guard[2 + k] = VE.v
                        h2f_ve[0] = VE.v
                        par2 = gucnt[0] % 2
                        quant(h2f[:, 0:nw], h2qt[:, par2, 0:nw], nw, QF)
                        tp_prev[0] = (noff, nw, qt, par2, VE.v)
                    blk_te[par] = TE.v
                    if HOIST and qc < NQC - 1 and nbi < 4:
                        h1_tile(qc + 1, nbi, h1bufs[(qc + 1) % 2])
                flush_tp()

                for nd in range(2):
                    for icx in range(IC):
                        par = icx % 2
                        S.wait(TE, dp_te[par])
                        S.dma(wdb[:, par, :], wdq_h[:, icx, 512 * nd:512 * (nd + 1)],
                              inc=(LD, 16))
                        T.wait(LD, LD.v)
                        if icx == 0:
                            T.wait(VE, VE.v)
                            T.wait(SE, SE.v)
                        for qt in range(4):
                            T.op(lambda e, icx=icx, qt=qt, par=par: e.matmul(
                                P[qt][:, :], h2qi[:, icx, 128 * qt:128 * (qt + 1)],
                                wdb[:, par, :],
                                start=(icx == 0), stop=(icx == IC - 1)),
                                inc=(TE if qt == 3 else None))
                        dp_te[par] = TE.v
                    V.wait(TE, TE.v)
                    for qt in range(4):
                        tglob = 4 * qc + qt
                        V.wait(ST, ST.v)
                        V.op(lambda e, qt=qt, nd=nd: e.tensor_tensor(
                            o1f[:, :], P[qt][:, :], bc2[:, 512 * nd:512 * (nd + 1)],
                            ALU.mult))
                        V.op(lambda e, qt=qt, nd=nd, tglob=tglob: e.tensor_tensor(
                            x3s[:, qt, 512 * nd:512 * (nd + 1)], o1f[:, :],
                            x1[:, tglob, 512 * nd:512 * (nd + 1)], ALU.add),
                            inc=VE)
                        ve_guard[qt] = VE.v
                        if nd == 1:
                            G.wait(VE, VE.v)
                            G.dma(y_h[xrow0 + 128 * tglob: xrow0 + 128 * (tglob + 1), :],
                                  x3s[:, qt, :], inc=(ST, 16))

        G.wait(ST, ST.v)

    with nc.Block() as blk:
        @blk.sync
        def _(eng):
            pr.begin_pass("sync", eng, allsems)
            script(pr)

        @blk.gpsimd
        def _(eng):
            pr.begin_pass("gpsimd", eng, allsems)
            script(pr)

        @blk.tensor
        def _(eng):
            pr.begin_pass("tensor", eng, allsems)
            script(pr)

        @blk.vector
        def _(eng):
            pr.begin_pass("vector", eng, allsems)
            script(pr)

        @blk.scalar
        def _(eng):
            pr.begin_pass("scalar", eng, allsems)
            script(pr)

    for c in semctx:
        c.__exit__(None, None, None)
    return nc


def _consts():
    ident = np.eye(128, dtype=ml_dtypes.bfloat16)
    tri = np.zeros((128, 128), np.float32)
    r = np.arange(128)
    tri[r[:, None] > r[None, :]] = NEG_BIG
    tri = tri.astype(ml_dtypes.bfloat16)
    indik = np.zeros((2, 128), np.float32)
    indik[0, 0:64] = 1.0
    indik[1, 64:128] = 1.0
    ones1 = np.ones((1, 128), np.float32)
    return ident, tri, indik, ones1


def make_in_maps(cfg, x, wq, wk, wv, wo, wg, wu, wd, c, wmod, n_cores=8):
    ident, tri, indik, ones1 = _consts()
    shared = dict(
        wq=np.ascontiguousarray(wq, dtype=np.float32),
        wk=np.ascontiguousarray(wk, dtype=np.float32),
        wv=np.ascontiguousarray(wv, dtype=np.float32),
        wo=np.ascontiguousarray(wo, dtype=np.float32),
        wg=np.ascontiguousarray(wg, dtype=np.float32),
        wu=np.ascontiguousarray(wu, dtype=np.float32),
        wd=np.ascontiguousarray(wd, dtype=np.float32),
        wmod=np.ascontiguousarray(wmod, dtype=np.float32),
        ident=ident, tri=tri, indik=indik, ones1=ones1,
    )
    c = np.asarray(c, np.float32)
    x = np.asarray(x, np.float32)
    in_maps = []
    for i in range(n_cores):
        ct = np.ascontiguousarray(
            c[cfg.NSEQ * i:cfg.NSEQ * (i + 1)].T.reshape(DC, 128, cfg.NSEQ)
            .transpose(1, 0, 2))
        xi = np.ascontiguousarray(
            x[cfg.NSEQ * i:cfg.NSEQ * (i + 1)].reshape(cfg.NSEQ * cfg.S, D))
        in_maps.append(dict(shared, x=xi, ct=ct))
    return in_maps


def kernel(L0_attn_x, L0_attn_W_Q, L0_attn_W_K, L0_attn_W_V, L0_attn_W_o,
           L0_ffn_w_gate, L0_ffn_w_up, L0_ffn_w_down, L0_c, L0_wmod):
    cfg = Cfg(S=2048, I=2816)
    nc = build_program(cfg)
    in_maps = make_in_maps(cfg, L0_attn_x, L0_attn_W_Q, L0_attn_W_K, L0_attn_W_V,
                           L0_attn_W_o, L0_ffn_w_gate, L0_ffn_w_up, L0_ffn_w_down,
                           L0_c, L0_wmod)
    res = run_bass_kernel_spmd(nc, in_maps, core_ids=list(range(8)))
    out = np.empty((16, 2048, 1024), np.float32)
    for i in range(8):
        out[2 * i:2 * i + 2] = res.results[i]["y"].reshape(2, cfg.S, D)
    return out


# version 18 (final)
# speedup vs baseline: 1.4839x; 1.0278x over previous
import numpy as np
import ml_dtypes

import concourse.bass as bass
import concourse.mybir as mybir
from concourse.bass_utils import run_bass_kernel_spmd

F32 = mybir.dt.float32
BF16 = mybir.dt.bfloat16
I32 = mybir.dt.int32
AF = mybir.ActivationFunctionType
ALU = mybir.AluOpType
AX = mybir.AxisListType

AND_M = int(np.uint32(0xFFC00000).view(np.int32))
OR_M = 0x00200000
EXP_M = 0x7F800000
NEG_BIG = -1.0e9

D = 1024
DC = 8
NPAIR = 8
QCW = 512
EPS = 1e-5


class Cfg:
    def __init__(self, S=2048, I=2816):
        self.S = S
        self.NSEQ = 2
        self.I = I
        self.IC = I // 128
        self.NQC = S // QCW
        self.NT = S // 128
        self.NBL = []
        off = 0
        while off < I:
            w = min(512, I - off)
            self.NBL.append((off, w))
            off += w


class Sem:
    def __init__(self, handle):
        self.h = handle
        self.v = 0


class Eng:
    def __init__(self, prog, name):
        self.prog = prog
        self.name = name

    @property
    def real(self):
        return self.prog.active_eng if self.prog.active == self.name else None

    def _fin(self, ins, inc):
        if inc is not None:
            sem, n = (inc, 1) if isinstance(inc, Sem) else inc
            sem.v += n
            if ins is not None:
                ins.then_inc(sem.h, n)
        return ins

    def wait(self, sem, val=None):
        if val is None:
            val = sem.v
        if val <= 0:
            return
        e = self.real
        if e is not None:
            e.wait_ge(sem.h, val)

    def op(self, fn, inc=None):
        e = self.real
        ins = fn(e) if e is not None else None
        return self._fin(ins, inc)

    def dma(self, out, in_, inc=None):
        return self.op(lambda e: e.dma_start(out=out, in_=in_), inc)


class Prog:
    def __init__(self, nc):
        self.nc = nc
        self.active = None
        self.active_eng = None
        self.S = Eng(self, "sync")
        self.G = Eng(self, "gpsimd")
        self.T = Eng(self, "tensor")
        self.V = Eng(self, "vector")
        self.C = Eng(self, "scalar")

    def begin_pass(self, name, eng, sems):
        self.active = name
        self.active_eng = eng
        for s in sems:
            s.v = 0


def build_program(cfg: Cfg, debug_stop=None, debug_taps=False):
    nc = bass.Bass("TRN2", target_bir_lowering=False, detect_race_conditions=False)
    S_, I_, IC, NQC, NT, NSEQ = cfg.S, cfg.I, cfg.IC, cfg.NQC, cfg.NT, cfg.NSEQ

    dt = nc.dram_tensor
    x_h = dt("x", [NSEQ * S_, D], F32, kind="ExternalInput").ap()
    wq_h = dt("wq", [D, D], F32, kind="ExternalInput").ap()
    wk_h = dt("wk", [D, D], F32, kind="ExternalInput").ap()
    wv_h = dt("wv", [D, D], F32, kind="ExternalInput").ap()
    wo_h = dt("wo", [D, D], F32, kind="ExternalInput").ap()
    wg_h = dt("wg", [I_, D], F32, kind="ExternalInput").ap()
    wu_h = dt("wu", [I_, D], F32, kind="ExternalInput").ap()
    wd_h = dt("wd", [D, I_], F32, kind="ExternalInput").ap()
    ct_h = dt("ct", [128, DC, NSEQ], F32, kind="ExternalInput").ap()
    wmod_h = dt("wmod", [D, 6144], F32, kind="ExternalInput").ap()
    ident_h = dt("ident", [128, 128], BF16, kind="ExternalInput").ap()
    tri_h = dt("tri", [128, 128], BF16, kind="ExternalInput").ap()
    indik_h = dt("indik", [2, 128], F32, kind="ExternalInput").ap()
    ones1_h = dt("ones1", [1, 128], F32, kind="ExternalInput").ap()
    y_h = dt("y", [NSEQ * S_, D], F32, kind="ExternalOutput").ap()
    dbg_h = dt("dbg", [128, 16, 1024], F32, kind="ExternalOutput").ap() if debug_taps else None
    wgq_h = dt("wgq_s", [128, DC, I_], BF16, kind="Internal").ap()
    wuq_h = dt("wuq_s", [128, DC, I_], BF16, kind="Internal").ap()
    wdq_h = dt("wdq_s", [128, IC, D], BF16, kind="Internal").ap()
    mod_h = dt("mod_s", [NSEQ, 6144], F32, kind="Internal").ap()

    SBUF_LIMIT = nc.SBUF_PARTITION_SIZE_BYTES
    cur = [17 * 1024]

    def at(name, shape, dtp, align=32):
        nbytes = int(np.prod(shape[1:])) * mybir.dt.size(dtp)
        cur[0] = (cur[0] + align - 1) // align * align
        t = nc.alloc_sbuf_tensor_at(name, shape, dtp, offset=cur[0])
        cur[0] += nbytes
        return t

    ident = at("ident", [128, 128], BF16)
    tri = at("tri", [128, 128], BF16)
    indik = at("indik", [2, 128], F32)
    ones1 = at("ones1", [1, 128], F32)
    ct = at("ct", [128, DC, NSEQ], F32)
    vrow = at("vrow", [NSEQ, 1024], F32)
    bc0 = at("bc0", [128, D], F32)
    bc1 = at("bc1", [128, D], F32)
    bc2 = at("bc2", [128, D], F32)
    ln_s = at("ln_s", [128, 10], F32)
    onescol = at("onescol", [128, 1], BF16)
    ones_sb = at("ones_sb", [128, 64], F32)
    act_scr = at("act_scr", [128, D], F32)
    xin = at("xin", [128, 2, D], F32)
    xn_t = at("xn_t", [128, D], BF16)
    x1_off = (cur[0] + 31) // 32 * 32
    x1 = at("x1", [128, NT, D], BF16)
    shared_end = cur[0]

    ov0 = (shared_end + 31) // 32 * 32

    cur[0] = ov0
    stW = at("stW", [128, 2, 3072], F32)
    assert cur[0] <= SBUF_LIMIT, f"phase-W overlay over budget: {cur[0]}"

    cur[0] = ov0
    h0 = at("h0", [128, DC, S_], BF16)
    qf = nc.alloc_sbuf_tensor_at("qf", [128, DC, S_], BF16, offset=ov0)
    wX = at("wX", [128, DC, D], BF16)
    kf = at("kf", [128, DC, S_], BF16)
    vt = at("vt", [128, NT, 16 * 64], BF16)
    attw_off = (cur[0] + 31) // 32 * 32
    attw = at("attw", [128, DC, QCW], BF16)
    pa = at("pa", [128, 2, QCW], BF16)
    pb = at("pb", [128, 2, QCW], BF16)
    densb = at("densb", [128, QCW], F32)
    recb = at("recb", [128, QCW], F32)
    ot1 = recb
    xn_u = nc.alloc_sbuf_tensor_at("xn_u", [128, D], BF16, offset=attw_off)
    xnn = nc.alloc_sbuf_tensor_at("xnn", [128, D], F32, offset=attw_off + 2048)
    att_end = cur[0]
    assert cur[0] <= SBUF_LIMIT, f"phase-ATT overlay over budget: {cur[0]}"

    cur[0] = ov0
    qU = at("qU", [128, 1024], BF16)
    qX = at("qX", [128, 1024], BF16)
    qM1 = at("qM1", [128, 1024], BF16)
    qSG = at("qSG", [128, 1024], BF16)
    qsm = at("qsm", [128, 3, 64], F32)
    qsb = at("qsb", [128, 64], BF16)
    xn2 = at("xn2", [128, D], F32)
    h1g = at("h1g", [128, D], F32)
    h1f = at("h1f", [128, D], F32)
    h1qt = at("h1qt", [128, D], BF16)
    h1qf = at("h1qf", [128, DC, QCW], BF16)
    wgb = at("wgb", [128, 2, DC, 512], BF16)
    wub = at("wub", [128, 2, DC, 512], BF16)
    wdb = at("wdb", [128, 2, 512], BF16)
    silu_s = at("silu_s", [128, 512], F32)
    h2f = at("h2f", [128, 512], F32)
    h2qt = at("h2qt", [128, 2, 512], BF16)
    h2qi = at("h2qi", [128, IC, QCW], BF16)
    o1f = at("o1f", [128, 512], F32)
    x3s = at("x3s", [128, 4, D], F32)
    ffn_end = cur[0]
    assert cur[0] <= SBUF_LIMIT, f"phase-FFN overlay over budget: {cur[0]}"

    cur[0] = max(att_end, ffn_end)
    wreg0 = (cur[0] + 31) // 32 * 32
    stW2 = at("stW2", [128, 2, 1024], F32)
    h1qf2 = nc.alloc_sbuf_tensor_at("h1qf2", [128, DC, QCW], BF16, offset=wreg0)
    qbf2 = at("qbf2", [128, 2, 1024], BF16)
    tstage2 = at("tstage2", [128, 1024], BF16)
    qsm_w = at("qsm_w", [128, 3, 64], F32)
    qsb_w = at("qsb_w", [128, 64], BF16)
    assert cur[0] <= SBUF_LIMIT, f"W-quant region over budget: {cur[0]}"
    qU_w = nc.alloc_sbuf_tensor_at("qU_w", [128, 1024], BF16, offset=x1_off + 24576)
    qX_w = nc.alloc_sbuf_tensor_at("qX_w", [128, 1024], BF16, offset=x1_off + 26624)
    qM1_w = nc.alloc_sbuf_tensor_at("qM1_w", [128, 1024], BF16, offset=x1_off + 28672)
    qSG_w = nc.alloc_sbuf_tensor_at("qSG_w", [128, 1024], BF16, offset=x1_off + 30720)

    P = [nc.alloc_psum_tensor(f"P{k}", [128, 512], F32) for k in range(7)]
    T1 = nc.alloc_psum_tensor("T1", [128, 512], BF16)
    Ttile = [T1[:, 0:128], T1[:, 128:256]]

    semctx = [nc.semaphore(nm) for nm in ["LD", "ST", "TE", "VE", "SE", "GE"]]
    handles = [c.__enter__() for c in semctx]
    LD, ST, TE, VE, SE, GE = [Sem(h) for h in handles]
    allsems = [LD, ST, TE, VE, SE, GE]
    pr = Prog(nc)

    (ln_sx, ln_m2, ln_nm1, ln_nvar, ln_is, ln_std, eps_b, exp_b, zcol,
     jcol) = (ln_s[:, k:k + 1] for k in range(10))

    def script(pr: Prog):
        S, G, T, V, C = pr.S, pr.G, pr.T, pr.V, pr.C
        dbg_slot = [0]

        def tap(ap, cols=1024):
            if not debug_taps:
                return
            k = dbg_slot[0]
            dbg_slot[0] += 1
            G.wait(VE, VE.v)
            G.wait(SE, SE.v)
            G.wait(TE, TE.v)
            G.wait(GE, GE.v)
            G.dma(dbg_h[:, k, 0:cols], ap, inc=(ST, 16))

        def v3(ap2, w):
            return ap2.rearrange("p (nb b) -> p nb b", b=16)

        def quant(src, dst, w, Q, te_guard=None):
            (U, A_, B_, SG, SM, SB) = Q
            nb = w // 16
            C.wait(GE, GE.v)
            C.wait(VE, VE.v)
            C.op(lambda e: e.activation(SG[:, 0:w], src, AF.Sign), inc=SE)
            se_sgn = SE.v
            m_, s_, r_ = (SM[:, k, 0:nb] for k in range(3))
            sb_ = SB[:, 0:nb]
            V.wait(GE, GE.v)
            hv = nb // 2
            halves = [(0, hv), (hv, nb)] if (2 <= hv < 128) else [(0, nb)]
            srcv = v3(src, w)

            def forall(fn):
                for (a, b) in halves:
                    V.op(lambda e, a=a, b=b: fn(e, slice(a, b)))

            forall(lambda e, sl: e.tensor_reduce(
                m_[:, sl], srcv[:, sl, :], AX.X, ALU.max,
                apply_absolute_value=True))
            forall(lambda e, sl: e.tensor_scalar(s_[:, sl], m_[:, sl], 1.0 / 6.0,
                                                 1e-12, ALU.mult, ALU.max))
            forall(lambda e, sl: e.reciprocal(r_[:, sl], s_[:, sl]))
            forall(lambda e, sl: e.tensor_copy(sb_[:, sl], s_[:, sl]))
            rb = r_.unsqueeze(2).broadcast_to([128, nb, 16])
            sbb = sb_.unsqueeze(2).broadcast_to([128, nb, 16])
            u = U[:, 0:w]
            u3 = v3(u, w)
            V.op(lambda e: e.tensor_tensor(u3, srcv, rb, ALU.mult))
            V.wait(SE, se_sgn)
            V.op(lambda e: e.tensor_tensor(u, u, SG[:, 0:w], ALU.mult))
            Aw, Bw = A_[:, 0:w], B_[:, 0:w]
            V.op(lambda e: e.tensor_scalar(Aw, u, 0.25, None, ALU.is_gt))
            V.op(lambda e: e.scalar_tensor_tensor(Aw, u, 0.75, Aw, ALU.is_gt, ALU.add))
            V.op(lambda e: e.scalar_tensor_tensor(Aw, u, 1.25, Aw, ALU.is_gt, ALU.add))
            V.op(lambda e: e.scalar_tensor_tensor(Aw, u, 1.75, Aw, ALU.is_gt, ALU.add))
            V.op(lambda e: e.tensor_scalar(Bw, u, 5.0, 2.0, ALU.is_gt, ALU.mult))
            V.op(lambda e: e.scalar_tensor_tensor(Bw, u, 2.5, Bw, ALU.is_gt, ALU.add))
            V.op(lambda e: e.scalar_tensor_tensor(Bw, u, 3.5, Bw, ALU.is_gt, ALU.add))
            V.op(lambda e: e.scalar_tensor_tensor(Aw, Aw, 0.5, Bw, ALU.mult, ALU.add))
            V.op(lambda e: e.tensor_tensor(Aw, Aw, SG[:, 0:w], ALU.mult))
            if te_guard is not None:
                V.wait(TE, te_guard)
            V.op(lambda e: e.tensor_tensor(v3(dst, w), v3(Aw, w), sbb, ALU.mult),
                 inc=VE)

        QW = (qU_w, qX_w, qM1_w, qSG_w, qsm_w, qsb_w)
        QF = (qU, qX, qM1, qSG, qsm, qsb)

        def transpose_blocks(src_fn, nblk, dst_fn):
            for b in range(nblk):
                par = b % 2
                T.wait(SE, SE.v)
                T.op(lambda e, b=b, par=par: e.transpose(Ttile[par], src_fn(b), ident[:]),
                     inc=TE)
                C.wait(TE, TE.v)
                C.op(lambda e, b=b, par=par: e.copy(dst_fn(b), Ttile[par]), inc=SE)

        def ln_norm(x_ap, out_ap):
            C.op(lambda e: e.activation(act_scr[:], x_ap, AF.Copy, accum_out=ln_sx))
            C.op(lambda e: e.activation(act_scr[:], x_ap, AF.Square,
                                        scale=1.0 / 32.0, bias=zcol,
                                        accum_out=ln_m2))
            C.op(lambda e: e.activation(ln_nm1, ln_sx, AF.Identity,
                                        scale=-1.0 / D, bias=zcol), inc=SE)
            V.wait(SE, SE.v)
            V.op(lambda e: e.scalar_tensor_tensor(ln_nvar, ln_nm1, ln_nm1, ln_m2,
                                                  ALU.mult, ALU.subtract), inc=VE)
            C.wait(VE, VE.v)
            C.op(lambda e: e.activation(ln_std, ln_nvar, AF.Sqrt, scale=-1.0,
                                        bias=eps_b), inc=SE)
            V.wait(SE, SE.v)
            V.op(lambda e: e.reciprocal(ln_is, ln_std))
            V.op(lambda e: e.memset(jcol, 0.0))
            V.op(lambda e: e.tensor_scalar(out_ap, x_ap, ln_nm1, ln_is,
                                           ALU.add, ALU.mult), inc=VE)
            return VE.v

        for dst, src in [(ident[:], ident_h), (tri[:], tri_h), (indik[:], indik_h),
                         (ones1[:], ones1_h), (ct[:], ct_h)]:
            S.dma(dst, src, inc=(LD, 16))
        V.op(lambda e: e.memset(eps_b, EPS))
        V.op(lambda e: e.memset(zcol, 0.0))
        V.op(lambda e: e.memset(jcol, 0.0))
        V.op(lambda e: e.memset(onescol[:], 1.0))
        V.op(lambda e: e.memset(ones_sb[:], 1.0))
        V.op(lambda e: e.memset(exp_b, -6.0), inc=VE)
        C.wait(VE, VE.v)
        C.wait(LD, LD.v)
        ctf = ct[:].rearrange("p c n -> p (c n)")
        C.op(lambda e: e.activation(ctf, ctf, AF.Silu), inc=SE)

        for c in range(DC):
            if c >= 1:
                S.wait(TE, TE.v)
            S.dma(stW[:, 0, :], wmod_h[128 * c:128 * (c + 1), 0:3072], inc=(LD, 16))
            S.dma(stW[:, 1, :], wmod_h[128 * c:128 * (c + 1), 3072:6144], inc=(LD, 16))
            T.wait(LD, LD.v)
            T.wait(SE, SE.v)
            for h in range(2):
                for n in range(6):
                    col = 3072 * h + 512 * n
                    tp = (0, 0) if h == 0 else (0, 32)
                    rows = slice(0, NSEQ) if h == 0 else slice(32, 32 + NSEQ)
                    last = (h == 1 and n == 5)
                    T.op(lambda e, col=col, tp=tp, rows=rows, c=c: e.matmul(
                        P[n][rows, :], ct[:, c, :],
                        stW[:, col // 3072, (col % 3072):(col % 3072) + 512],
                        start=(c == 0), stop=(c == DC - 1), tile_position=tp),
                        inc=(TE if last else None))
        V.wait(TE, TE.v)
        for k in range(6):
            h = k // 3
            n0 = (2 * k) % 6
            rows = slice(0, NSEQ) if h == 0 else slice(32, 32 + NSEQ)
            r0 = 0 if h == 0 else 32
            buf = act_scr[r0:r0 + NSEQ, :]
            tbf = xin[r0:r0 + NSEQ, 0, 0:1024]
            if k >= 1:
                V.wait(ST, ST.v)
            V.op(lambda e, n0=n0, rows=rows, buf=buf: e.tensor_copy(
                buf[:, 0:512], P[n0][rows, :]))
            V.op(lambda e, n0=n0, rows=rows, buf=buf: e.tensor_copy(
                buf[:, 512:1024], P[n0 + 1][rows, :]))
            if k in (1, 4):
                V.op(lambda e, buf=buf: e.tensor_scalar(buf, buf, 1.0, None, ALU.add))
            if k in (2, 5):
                V.op(lambda e, buf=buf, tbf=tbf: e.tensor_scalar(
                    tbf, buf, 0.0, 2.0, ALU.is_ge, ALU.mult))
                V.op(lambda e, tbf=tbf: e.tensor_scalar(tbf, tbf, -1.0, None, ALU.add))
                V.op(lambda e, buf=buf, tbf=tbf: e.tensor_tensor(buf, buf, tbf,
                                                                 ALU.mult))
                V.op(lambda e, buf=buf: e.tensor_scalar(buf, buf, 0.25, None, ALU.max))
                V.op(lambda e, buf=buf, tbf=tbf: e.tensor_tensor(buf, buf, tbf, ALU.mult))
            V.op(lambda e, buf=buf: e.tensor_scalar(buf, buf, 0.0, None, ALU.add),
                 inc=VE)
            G.wait(VE, VE.v)
            G.dma(mod_h[:, 1024 * k:1024 * (k + 1)], buf, inc=(ST, 16))

        if debug_stop == "mod":
            G.wait(ST, ST.v)
            return
        wchunks = []
        for icx in range(cfg.IC):
            wchunks.append((wg_h, 128 * icx, 0, D, wgq_h))
        for icx in range(cfg.IC):
            wchunks.append((wu_h, 128 * icx, 0, D, wuq_h))
        for hc in range(DC):
            for coff, cwid in ((0, 1024), (1024, 1024), (2048, 768)):
                wchunks.append((wd_h, 128 * hc, coff, cwid, wdq_h))
        wstate = {"li": 0, "ti": 0, "te_qbf": [0, 0]}

        def w_load_quant():
            i = wstate["li"]
            if i >= len(wchunks):
                return
            wstate["li"] += 1
            src_h, r0, c0, cw, dst_h = wchunks[i]
            par = i % 2
            S.wait(VE, VE.v)
            S.wait(SE, SE.v)
            S.dma(stW2[:, par, 0:cw], src_h[r0:r0 + 128, c0:c0 + cw],
                  inc=(LD, 16))
            V.wait(LD, LD.v)
            C.wait(LD, LD.v)
            quant(stW2[:, par, 0:cw], qbf2[:, par, 0:cw], cw, QW,
                  te_guard=wstate["te_qbf"][par])

        def w_transpose_store():
            j = wstate["ti"]
            if j >= wstate["li"]:
                return
            wstate["ti"] += 1
            src_h, r0, c0, cw, dst_h = wchunks[j]
            par = j % 2
            nblk = cw // 128
            T.wait(VE, VE.v)
            T.wait(ST, ST.v)
            transpose_blocks(
                lambda b, par=par: qbf2[:, par, 128 * b:128 * (b + 1)], nblk,
                lambda b: tstage2[:, 128 * b:128 * (b + 1)])
            wstate["te_qbf"][par] = TE.v
            G.wait(SE, SE.v)
            if dst_h is wdq_h:
                dst_ap = dst_h[:, (c0 // 128):(c0 // 128) + nblk, r0:r0 + 128]
            else:
                dst_ap = dst_h[:, :, r0:r0 + 128]
            G.dma(dst_ap,
                  tstage2[:, 0:cw].rearrange("p (c w) -> p c w", w=128),
                  inc=(ST, 16))

        def wq_point(s):
            if s != 0:
                return
            w_transpose_store()
            w_load_quant()

        def wq_drain(s):
            if s != 0:
                return
            while wstate["ti"] < len(wchunks):
                w_transpose_store()
                w_load_quant()

        for s in range(NSEQ):
            xrow0 = s * S_

            def expand(k, dstb):
                S.wait(TE, TE.v)
                S.wait(ST, ST.v)
                S.dma(vrow[0:1, :], mod_h[s:s + 1, 1024 * k:1024 * (k + 1)],
                      inc=(LD, 16))
                T.wait(LD, LD.v)
                for half in range(2):
                    T.wait(VE, VE.v)
                    T.op(lambda e, half=half: e.matmul(
                        P[6][:, :], ones1[:], vrow[0:1, 512 * half:512 * half + 512],
                        start=True, stop=True), inc=TE)
                    V.wait(TE, TE.v)
                    V.op(lambda e, half=half: e.tensor_copy(
                        dstb[:, 512 * half:512 * half + 512], P[6][:, :]), inc=VE)

            expand(0, bc0)
            expand(1, bc1)
            expand(2, bc2)
            if s == 0 and debug_taps:
                tap(bc0[:]); tap(bc1[:]); tap(bc2[:])

            ln_prev = [0]
            xin_se = [SE.v, SE.v]
            xin_ve = [VE.v, VE.v]
            for t in range(NT):
                par = t % 2
                S.wait(SE, xin_se[par])
                S.wait(VE, xin_ve[par])
                S.dma(xin[:, par, :], x_h[xrow0 + 128 * t: xrow0 + 128 * (t + 1), :],
                      inc=(LD, 16))
                C.wait(LD, LD.v)
                C.wait(VE, ln_prev[0])
                V.wait(GE, GE.v)
                ln_prev[0] = ln_norm(xin[:, par, :], xnn[:])
                xin_se[par] = SE.v
                xin_ve[par] = VE.v
                G.wait(VE, VE.v)
                G.op(lambda e: e.tensor_tensor(xn_u[:], xnn[:], bc1[:], ALU.mult))
                G.wait(TE, TE.v)
                G.op(lambda e: e.tensor_tensor(xn_t[:], xn_u[:], bc0[:], ALU.add),
                     inc=GE)
                T.wait(GE, GE.v)
                if s == 0 and t == 0 and debug_taps:
                    tap(xn_t[:])
                transpose_blocks(
                    lambda b: xn_t[:, 128 * b:128 * (b + 1)], DC,
                    lambda b, t=t: h0[:, b, 128 * t:128 * (t + 1)])
                wq_point(s)

            def load_w(w_h, dst):
                G.wait(TE, TE.v)
                G.dma(dst[:], w_h.rearrange("(c p) w -> p c w", p=128),
                      inc=(ST, 16))

            def proj_block(wbuf, nq, dst_col_fn):
                T.wait(SE, SE.v)
                for wave in ([0, 1, 2, 3, 4, 5, 6], [7]):
                    for mc in wave:
                        bank = P[mc % 7]
                        if mc == 7:
                            C.wait(TE, TE.v)
                            for mc2 in [0, 1, 2, 3, 4, 5, 6]:
                                C.op(lambda e, mc2=mc2, nq=nq: e.copy(
                                    dst_col_fn(mc2, nq), P[mc2][:, :]), inc=SE)
                            T.wait(SE, SE.v)
                        for kc in range(DC):
                            T.op(lambda e, kc=kc, mc=mc, nq=nq, bank=bank: e.matmul(
                                bank[:, :], wbuf[:, kc, 128 * mc:128 * (mc + 1)],
                                h0[:, kc, 512 * nq:512 * (nq + 1)],
                                start=(kc == 0), stop=(kc == DC - 1)),
                                inc=(TE if kc == DC - 1 else None))
                C.wait(TE, TE.v)
                C.op(lambda e, nq=nq: e.copy(dst_col_fn(7, nq), P[0][:, :]), inc=SE)

            def proj_block_q(wbuf, nq):
                T.wait(SE, SE.v)
                T.wait(VE, VE.v)
                for mc in range(DC):
                    bank = P[mc % 7]
                    if mc == 7:
                        C.wait(TE, TE.v)
                        for mc2 in range(7):
                            C.op(lambda e, mc2=mc2: e.copy(
                                attw[:, mc2, :], P[mc2][:, :]), inc=SE)
                        T.wait(SE, SE.v)
                    for kc in range(DC):
                        T.op(lambda e, kc=kc, mc=mc, nq=nq, bank=bank: e.matmul(
                            bank[:, :], wbuf[:, kc, 128 * mc:128 * (mc + 1)],
                            h0[:, kc, 512 * nq:512 * (nq + 1)],
                            start=(kc == 0), stop=(kc == DC - 1)),
                            inc=(TE if kc == DC - 1 else None))
                C.wait(TE, TE.v)
                C.op(lambda e: e.copy(attw[:, 7, :], P[0][:, :]), inc=SE)
                V.wait(SE, SE.v)
                V.op(lambda e, nq=nq: e.tensor_copy(
                    qf[:, :, 512 * nq:512 * (nq + 1)], attw[:]), inc=VE)

            if s == 0 and debug_taps:
                tap(h0[:, 0, 0:1024])
            load_w(wk_h, wX)
            if s == 0 and debug_taps:
                tap(wX[:, 0, :])
            T.wait(ST, ST.v)
            for nq in range(S_ // 512):
                proj_block(wX, nq, lambda mc, nq: kf[:, mc, 512 * nq:512 * (nq + 1)])
                wq_point(s)
                wq_point(s)

            if s == 0 and debug_taps:
                tap(kf[:, 0, 0:1024])
            load_w(wv_h, wX)
            T.wait(ST, ST.v)
            for t in range(NT):
                for nd in range(2):
                    bank = P[(2 * t + nd) % 7]
                    if t * 2 + nd >= 7:
                        T.wait(SE, SE.v)
                    for kc in range(DC):
                        T.op(lambda e, kc=kc, t=t, nd=nd, bank=bank: e.matmul(
                            bank[:, :], h0[:, kc, 128 * t:128 * (t + 1)],
                            wX[:, kc, 512 * nd:512 * (nd + 1)],
                            start=(kc == 0), stop=(kc == DC - 1)),
                            inc=(TE if kc == DC - 1 else None))
                    C.wait(TE, TE.v)
                    C.op(lambda e, t=t, nd=nd, bank=bank: e.copy(
                        vt[:, t, 512 * nd:512 * (nd + 1)], bank[:, :]), inc=SE)
                if t % 2 == 1:
                    wq_point(s)
            T.wait(SE, SE.v)

            if s == 0 and debug_taps:
                tap(vt[:, 0, :])
            load_w(wq_h, wX)
            T.wait(ST, ST.v)
            for nq in range(S_ // 512):
                proj_block_q(wX, nq)
                wq_point(s)
                wq_point(s)

            if s == 0 and debug_taps:
                tap(qf[:, 0, 0:1024])
            load_w(wo_h, wX)
            T.wait(ST, ST.v)

            att_se = [SE.v, SE.v]
            att_pv_te = [TE.v, TE.v]
            den_ve = [VE.v]

            def denom(pp):
                C.wait(TE, TE.v)
                C.op(lambda e: e.copy(densb[0:1, 0:QCW], P[6][0:1, :]))
                C.op(lambda e: e.copy(densb[32:33, 0:QCW], P[6][32:33, :]),
                     inc=SE)
                V.wait(SE, SE.v)
                V.op(lambda e: e.reciprocal(densb[0:1, 0:QCW], densb[0:1, 0:QCW]))
                V.op(lambda e: e.reciprocal(densb[32:33, 0:QCW],
                                            densb[32:33, 0:QCW]), inc=VE)
                T.wait(VE, VE.v)
                T.op(lambda e: e.matmul(P[6][0:64, :], ones1[0:1, 0:64],
                                        densb[0:1, 0:QCW],
                                        start=True, stop=True))
                T.op(lambda e: e.matmul(P[6][64:128, :], ones_sb[32:33, :],
                                        densb[32:33, 0:QCW],
                                        start=True, stop=True,
                                        tile_position=(32, 64)), inc=TE)
                V.wait(TE, TE.v)
                V.op(lambda e: e.tensor_copy(recb[:], P[6][:, :]))
                V.op(lambda e, pp=pp: e.tensor_tensor(
                    attw[0:64, pp, :], P[4][0:64, :], recb[0:64, :], ALU.mult))
                V.op(lambda e, pp=pp: e.tensor_tensor(
                    attw[64:128, pp, :], P[5][64:128, :], recb[64:128, :],
                    ALU.mult), inc=VE)
                den_ve[0] = VE.v
                wq_point(s)

            for qc in range(NQC):
                njc = 4 * qc + 4
                for p in range(NPAIR):
                    exp_se = {}
                    sc_te = {}

                    def scores(j):
                        par = j % 2
                        vs = max(0, 128 * j - QCW * qc)
                        diag = (128 * j >= QCW * qc)
                        sA = P[0 + par]
                        sB = P[2 + par]
                        T.wait(SE, att_se[par])
                        q_ap = qf[0:64, p, QCW * qc + vs: QCW * (qc + 1)]
                        qb_ap = qf[64:128, p, QCW * qc + vs: QCW * (qc + 1)]
                        T.op(lambda e, sA=sA, j=j, vs=vs, p=p, q_ap=q_ap:
                             e.matmul(sA[:, vs:QCW], kf[0:64, p, 128 * j:128 * (j + 1)],
                                      q_ap, start=True, stop=not diag,
                                      tile_position=(0, 0)))
                        if diag:
                            T.op(lambda e, sA=sA, vs=vs: e.matmul(
                                sA[:, vs:vs + 128], ident[:], tri[:],
                                start=False, stop=True))
                        T.op(lambda e, sB=sB, j=j, vs=vs, p=p, qb_ap=qb_ap:
                             e.matmul(sB[:, vs:QCW], kf[64:128, p, 128 * j:128 * (j + 1)],
                                      qb_ap, start=True, stop=not diag,
                                      tile_position=(64, 0)),
                             inc=(None if diag else TE))
                        if diag:
                            T.op(lambda e, sB=sB, vs=vs: e.matmul(
                                sB[:, vs:vs + 128], ident[:], tri[:],
                                start=False, stop=True), inc=TE)
                        sc_te[j] = TE.v

                    def expj(j):
                        par = j % 2
                        vs = max(0, 128 * j - QCW * qc)
                        sA = P[0 + par]
                        sB = P[2 + par]
                        C.wait(TE, max(sc_te[j], att_pv_te[par]))
                        C.op(lambda e, sA=sA, vs=vs, par=par: e.activation(
                            pa[:, par, vs:QCW], sA[:, vs:QCW], AF.Exp, scale=0.125,
                            bias=exp_b), inc=SE)
                        C.op(lambda e, sB=sB, vs=vs, par=par: e.activation(
                            pb[:, par, vs:QCW], sB[:, vs:QCW], AF.Exp, scale=0.125,
                            bias=exp_b), inc=SE)
                        exp_se[j] = SE.v
                        att_se[par] = SE.v

                    def pv(j):
                        par = j % 2
                        vs = max(0, 128 * j - QCW * qc)
                        if j == 0:
                            T.wait(VE, den_ve[0])
                        T.wait(SE, exp_se[j])
                        T.op(lambda e, j=j, vs=vs, p=p, par=par: e.matmul(
                            P[4][0:64, vs:QCW], vt[:, j, 128 * p:128 * p + 64],
                            pa[:, par, vs:QCW], start=(j == 0), stop=(j == njc - 1)))
                        T.op(lambda e, j=j, vs=vs, p=p, par=par: e.matmul(
                            P[5][64:128, vs:QCW], vt[:, j, 128 * p + 64:128 * (p + 1)],
                            pb[:, par, vs:QCW], start=(j == 0), stop=(j == njc - 1),
                            tile_position=(0, 64)))
                        T.op(lambda e, j=j, vs=vs, par=par: e.matmul(
                            P[6][0:1, vs:QCW], onescol[:], pa[:, par, vs:QCW],
                            start=(j == 0), stop=(j == njc - 1)))
                        T.op(lambda e, j=j, vs=vs, par=par: e.matmul(
                            P[6][32:33, vs:QCW], onescol[:], pb[:, par, vs:QCW],
                            start=(j == 0), stop=(j == njc - 1),
                            tile_position=(0, 32)), inc=TE)
                        att_pv_te[par] = TE.v

                    for j in range(njc):
                        scores(j)
                        expj(j)
                        if j == 0 and p > 0:
                            denom(p - 1)
                        if j > 0:
                            pv(j - 1)
                    pv(njc - 1)
                    if p == NPAIR - 1:
                        denom(p)

                if qc == NQC - 1:
                    wq_drain(s)
                op_ve = {4: VE.v, 5: VE.v}
                ox_ve = {0: VE.v, 1: VE.v}
                for qt in range(4):
                    tglob = 4 * qc + qt
                    par = tglob % 2
                    S.wait(VE, ox_ve[par])
                    S.wait(SE, SE.v)
                    S.dma(xin[:, par, :],
                          x_h[xrow0 + 128 * tglob: xrow0 + 128 * (tglob + 1), :],
                          inc=(LD, 16))
                    for nd in range(2):
                        bank = P[4 + nd]
                        T.wait(VE, op_ve[4 + nd])
                        for c in range(DC):
                            T.op(lambda e, c=c, qt=qt, nd=nd, bank=bank: e.matmul(
                                bank[:, :], attw[:, c, 128 * qt:128 * (qt + 1)],
                                wX[:, c, 512 * nd:512 * (nd + 1)],
                                start=(c == 0), stop=(c == DC - 1)),
                                inc=(TE if c == DC - 1 else None))
                        V.wait(TE, TE.v)
                        V.wait(LD, LD.v)
                        V.op(lambda e, nd=nd, bank=bank: e.tensor_tensor(
                            ot1[:, :], bank[:, :], bc2[:, 512 * nd:512 * (nd + 1)],
                            ALU.mult))
                        V.op(lambda e, nd=nd, tglob=tglob, par=par: e.tensor_tensor(
                            x1[:, tglob, 512 * nd:512 * (nd + 1)], ot1[:, :],
                            xin[:, par, 512 * nd:512 * (nd + 1)], ALU.add), inc=VE)
                        op_ve[4 + nd] = VE.v
                        if nd == 1:
                            ox_ve[par] = VE.v
                T.wait(VE, VE.v)

            if s == 0 and debug_taps:
                tap(attw[:, 0, :], 512)
                tap(pa[:], 512)
                tap(recb[:], 512)
                tap(x1[:, 0, :])
                tap(x1[:, NT - 1, :])
            expand(3, bc0)
            expand(4, bc1)
            expand(5, bc2)

            h1_prev_ln = [0]

            h1_q_ve = {}

            def h1_compute(qc2, qt2):
                tglob = 4 * qc2 + qt2
                C.wait(VE, h1_prev_ln[0])
                V.wait(GE, GE.v)
                h1_prev_ln[0] = ln_norm(x1[:, tglob, :], xn2[:])
                G.wait(VE, VE.v)
                G.op(lambda e: e.tensor_tensor(h1g[:], xn2[:], bc1[:], ALU.mult))
                G.wait(SE, SE.v)
                G.op(lambda e: e.tensor_tensor(h1f[:], h1g[:], bc0[:], ALU.add),
                     inc=GE)
                quant(h1f[:], h1qt[:], D, QF)
                h1_q_ve[qt2] = VE.v

            def h1_flush(qt2, dstbuf):
                T.wait(VE, h1_q_ve[qt2])
                transpose_blocks(
                    lambda b: h1qt[:, 128 * b:128 * (b + 1)], DC,
                    lambda b, qt2=qt2: dstbuf[:, b, 128 * qt2:128 * (qt2 + 1)])

            def h1_tile(qc2, qt2, dstbuf):
                h1_compute(qc2, qt2)
                h1_flush(qt2, dstbuf)

            HOIST = False
            CONSV = False
            h1bufs = [h1qf, h1qf2]
            for qt in range(4):
                h1_tile(0, qt, h1bufs[0])
            h1_se = [SE.v]

            gucnt = [0]
            ve_guard = [VE.v] * 7
            se_guard = [SE.v] * 7
            h2f_ve = [VE.v]
            blk_te = [TE.v, TE.v]
            dp_te = [TE.v, TE.v]
            tp_prev = [None]

            def flush_tp():
                if tp_prev[0] is None:
                    return
                pnoff, pnw, pqt, ppar2, q_ve = tp_prev[0]
                tp_prev[0] = None
                T.wait(VE, q_ve)
                transpose_blocks(
                    lambda b, ppar2=ppar2: h2qt[:, ppar2, 128 * b:128 * (b + 1)],
                    pnw // 128,
                    lambda b, pnoff=pnoff, pqt=pqt: h2qi[:, (pnoff // 128) + b,
                                                         128 * pqt:128 * (pqt + 1)])

            for qc in range(NQC):
                hsrc = h1bufs[qc % 2]
                for nbi, (noff, nw) in enumerate(cfg.NBL):
                    par = nbi % 2
                    S.wait(TE, blk_te[par])
                    S.dma(wgb[:, par, :, 0:nw], wgq_h[:, :, noff:noff + nw],
                          inc=(LD, 16))
                    S.dma(wub[:, par, :, 0:nw], wuq_h[:, :, noff:noff + nw],
                          inc=(LD, 16))
                    for qt in range(4):
                        k = gucnt[0] % 2
                        gucnt[0] += 1
                        gP = P[k]
                        uP = P[2 + k]
                        T.wait(LD, LD.v)
                        T.wait(VE, max(ve_guard[k], ve_guard[2 + k]))
                        T.wait(SE, max(se_guard[k], h1_se[0]))
                        for kc in range(DC):
                            T.op(lambda e, kc=kc, qt=qt, par=par, nw=nw, gP=gP,
                                 hsrc=hsrc: e.matmul(
                                gP[:, 0:nw], hsrc[:, kc, 128 * qt:128 * (qt + 1)],
                                wgb[:, par, kc, 0:nw],
                                start=(kc == 0), stop=(kc == DC - 1)))
                        for kc in range(DC):
                            T.op(lambda e, kc=kc, qt=qt, par=par, nw=nw, uP=uP,
                                 hsrc=hsrc: e.matmul(
                                uP[:, 0:nw], hsrc[:, kc, 128 * qt:128 * (qt + 1)],
                                wub[:, par, kc, 0:nw],
                                start=(kc == 0), stop=(kc == DC - 1)),
                                inc=(TE if kc == DC - 1 else None))
                        C.wait(TE, TE.v)
                        C.wait(VE, h2f_ve[0])
                        C.op(lambda e, gP=gP, nw=nw: e.activation(
                            silu_s[:, 0:nw], gP[:, 0:nw], AF.Silu), inc=SE)
                        se_guard[k] = SE.v
                        flush_tp()
                        V.wait(SE, se_guard[k])
                        V.op(lambda e, uP=uP, nw=nw: e.tensor_tensor(
                            h2f[:, 0:nw], silu_s[:, 0:nw], uP[:, 0:nw], ALU.mult),
                            inc=VE)
                        ve_guard[2 + k] = VE.v
                        h2f_ve[0] = VE.v
                        par2 = gucnt[0] % 2
                        quant(h2f[:, 0:nw], h2qt[:, par2, 0:nw], nw, QF)
                        tp_prev[0] = (noff, nw, qt, par2, VE.v)
                    blk_te[par] = TE.v
                    if qc < NQC - 1:
                        nbuf = h1bufs[(qc + 1) % 2]
                        if 1 <= nbi <= 4:
                            h1_flush(nbi - 1, nbuf)
                            if nbi == 4:
                                h1_se[0] = SE.v
                        if nbi < 4:
                            h1_compute(qc + 1, nbi)
                flush_tp()

                for nd in range(2):
                    for icx in range(IC):
                        par = icx % 2
                        S.wait(TE, dp_te[par])
                        S.dma(wdb[:, par, :], wdq_h[:, icx, 512 * nd:512 * (nd + 1)],
                              inc=(LD, 16))
                        T.wait(LD, LD.v)
                        if icx == 0:
                            T.wait(VE, VE.v)
                            T.wait(SE, SE.v)
                        for qt in range(4):
                            T.op(lambda e, icx=icx, qt=qt, par=par: e.matmul(
                                P[qt][:, :], h2qi[:, icx, 128 * qt:128 * (qt + 1)],
                                wdb[:, par, :],
                                start=(icx == 0), stop=(icx == IC - 1)),
                                inc=(TE if qt == 3 else None))
                        dp_te[par] = TE.v
                    V.wait(TE, TE.v)
                    for qt in range(4):
                        tglob = 4 * qc + qt
                        V.wait(ST, ST.v)
                        V.op(lambda e, qt=qt, nd=nd: e.tensor_tensor(
                            o1f[:, :], P[qt][:, :], bc2[:, 512 * nd:512 * (nd + 1)],
                            ALU.mult))
                        V.op(lambda e, qt=qt, nd=nd, tglob=tglob: e.tensor_tensor(
                            x3s[:, qt, 512 * nd:512 * (nd + 1)], o1f[:, :],
                            x1[:, tglob, 512 * nd:512 * (nd + 1)], ALU.add),
                            inc=VE)
                        ve_guard[qt] = VE.v
                        if nd == 1:
                            G.wait(VE, VE.v)
                            G.dma(y_h[xrow0 + 128 * tglob: xrow0 + 128 * (tglob + 1), :],
                                  x3s[:, qt, :], inc=(ST, 16))

        G.wait(ST, ST.v)

    with nc.Block() as blk:
        @blk.sync
        def _(eng):
            pr.begin_pass("sync", eng, allsems)
            script(pr)

        @blk.gpsimd
        def _(eng):
            pr.begin_pass("gpsimd", eng, allsems)
            script(pr)

        @blk.tensor
        def _(eng):
            pr.begin_pass("tensor", eng, allsems)
            script(pr)

        @blk.vector
        def _(eng):
            pr.begin_pass("vector", eng, allsems)
            script(pr)

        @blk.scalar
        def _(eng):
            pr.begin_pass("scalar", eng, allsems)
            script(pr)

    for c in semctx:
        c.__exit__(None, None, None)
    return nc


def _consts():
    ident = np.eye(128, dtype=ml_dtypes.bfloat16)
    tri = np.zeros((128, 128), np.float32)
    r = np.arange(128)
    tri[r[:, None] > r[None, :]] = NEG_BIG
    tri = tri.astype(ml_dtypes.bfloat16)
    indik = np.zeros((2, 128), np.float32)
    indik[0, 0:64] = 1.0
    indik[1, 64:128] = 1.0
    ones1 = np.ones((1, 128), np.float32)
    return ident, tri, indik, ones1


def make_in_maps(cfg, x, wq, wk, wv, wo, wg, wu, wd, c, wmod, n_cores=8):
    ident, tri, indik, ones1 = _consts()
    shared = dict(
        wq=np.ascontiguousarray(wq, dtype=np.float32),
        wk=np.ascontiguousarray(wk, dtype=np.float32),
        wv=np.ascontiguousarray(wv, dtype=np.float32),
        wo=np.ascontiguousarray(wo, dtype=np.float32),
        wg=np.ascontiguousarray(wg, dtype=np.float32),
        wu=np.ascontiguousarray(wu, dtype=np.float32),
        wd=np.ascontiguousarray(wd, dtype=np.float32),
        wmod=np.ascontiguousarray(wmod, dtype=np.float32),
        ident=ident, tri=tri, indik=indik, ones1=ones1,
    )
    c = np.asarray(c, np.float32)
    x = np.asarray(x, np.float32)
    in_maps = []
    for i in range(n_cores):
        ct = np.ascontiguousarray(
            c[cfg.NSEQ * i:cfg.NSEQ * (i + 1)].T.reshape(DC, 128, cfg.NSEQ)
            .transpose(1, 0, 2))
        xi = np.ascontiguousarray(
            x[cfg.NSEQ * i:cfg.NSEQ * (i + 1)].reshape(cfg.NSEQ * cfg.S, D))
        in_maps.append(dict(shared, x=xi, ct=ct))
    return in_maps


def kernel(L0_attn_x, L0_attn_W_Q, L0_attn_W_K, L0_attn_W_V, L0_attn_W_o,
           L0_ffn_w_gate, L0_ffn_w_up, L0_ffn_w_down, L0_c, L0_wmod):
    cfg = Cfg(S=2048, I=2816)
    nc = build_program(cfg)
    in_maps = make_in_maps(cfg, L0_attn_x, L0_attn_W_Q, L0_attn_W_K, L0_attn_W_V,
                           L0_attn_W_o, L0_ffn_w_gate, L0_ffn_w_up, L0_ffn_w_down,
                           L0_c, L0_wmod)
    res = run_bass_kernel_spmd(nc, in_maps, core_ids=list(range(8)))
    out = np.empty((16, 2048, 1024), np.float32)
    for i in range(8):
        out[2 * i:2 * i + 2] = res.results[i]["y"].reshape(2, cfg.S, D)
    return out
